# version 09
# baseline (speedup 1.0000x reference)
import math
import contextlib
from collections import defaultdict

import numpy as np
import ml_dtypes

import concourse.bass as bass
import concourse.mybir as mybir
from concourse.bass_utils import run_bass_kernel_spmd
from concourse.alu_op_type import AluOpType as ALU

F32 = mybir.dt.float32
BF16 = mybir.dt.bfloat16
U8 = mybir.dt.uint8
AF = mybir.ActivationFunctionType
AX = mybir.AxisListType

NCORE = 8
SEQ = 8192
DM = 2048
NTILE = SEQ // 128
NSLOT = 8
FT = DM // 128
HEADS0, GRP0, HD = 16, 4, 128
HEADS1 = 16
MOE_E, MOE_D, MOE_K = 16, 1408, 6
FFN_H = 5632
ISQ = 1.0 / math.sqrt(HD)
CAPS = [s // 2 * 2 + 1 + (s % 2) for s in range(NSLOT)]
BF16NP = ml_dtypes.bfloat16


def gtile(c, s):
    m = s // 2
    return 16 * m + (c if s % 2 == 0 else 15 - c)


def own_token_index(c):
    idx = []
    for s in range(NSLOT):
        g = gtile(c, s)
        idx.extend(range(g * 128, (g + 1) * 128))
    return np.array(idx, dtype=np.int64)


ENGINES = ("te", "ve", "se", "pe", "sq")


class Sch:
    def __init__(self):
        self.q = {e: [] for e in ENGINES}
        self.cnt = defaultdict(int)
        self.cnt["te"] = self.cnt["ve"] = self.cnt["se"] = self.cnt["pe"] = 0

    def op(self, eng, emit, wait=(), inc=None, inc_n=1):
        item = {"emit": emit, "wait": tuple(wait), "inc": inc, "inc_n": inc_n}
        self.q[eng].append(item)
        if inc is not None:
            self.cnt[inc] += inc_n
        return self.cnt[inc] if inc is not None else None

    def val(self, sem):
        return self.cnt[sem]

    def dma(self, queue, out, in_, sem, wait=(), **kw):
        def emit(eng, sems):
            return eng.dma_start(out=out, in_=in_, **kw)
        return self.op(queue, emit, wait=wait, inc=sem, inc_n=16)

    def drain(self, eng):
        self.op(eng, lambda e, s: e.drain())

    def build(self, nc, es):
        sems = {name: es.enter_context(nc.semaphore(f"sem_{name}"))
                for name in self.cnt.keys()}
        block = es.enter_context(nc.Block())
        engobjs = {"te": nc.tensor, "ve": nc.vector, "se": nc.scalar,
                   "pe": nc.gpsimd, "sq": nc.sync}

        def section(eng):
            def body(engine):
                for item in self.q[eng]:
                    for (s, v) in item["wait"]:
                        engine.wait_ge(sems[s], v)
                    instr = item["emit"](engine, sems)
                    if item["inc"] is not None:
                        assert instr is not None, f"inc on op with no instruction ({eng})"
                        instr.then_inc(sems[item["inc"]], item["inc_n"])
            return body

        for eng in ENGINES:
            if self.q[eng]:
                getattr(block, {"te": "tensor", "ve": "vector", "se": "scalar",
                                "pe": "gpsimd", "sq": "sync"}[eng])(section(eng))



def e_matmul(out, lhsT, rhs, start, stop):
    return lambda e, s: e.matmul(out, lhsT, rhs, start=start, stop=stop)


def e_transpose(out, in_, ident):
    return lambda e, s: e.transpose(out, in_, ident)


def e_act(out, in_, func, bias=None, scale=None, accum_out=None):
    def f(e, s):
        kw = {}
        if bias is not None:
            kw["bias"] = bias
        if scale is not None:
            kw["scale"] = scale
        if accum_out is not None:
            kw["accum_out"] = accum_out
        return e.activation(out=out, in_=in_, func=func, **kw)
    return f


def e_ts(out, in0, s1, s2=None, op0=ALU.mult, op1=None):
    def f(e, s):
        kw = dict(out=out, in0=in0, scalar1=s1, scalar2=s2, op0=op0)
        if op1 is not None:
            kw["op1"] = op1
        return e.tensor_scalar(**kw)
    return f


def e_tt(out, in0, in1, op):
    return lambda e, s: e.tensor_tensor(out=out, in0=in0, in1=in1, op=op)


def e_stt(out, in0, scalar, in1, op0, op1):
    return lambda e, s: e.scalar_tensor_tensor(out=out, in0=in0, scalar=scalar,
                                               in1=in1, op0=op0, op1=op1)


def e_red(out, in_, axis, op="sum", neg=False, absval=False):
    def f(e, s):
        if op == "sum":
            return e.reduce_sum(out=out, in_=in_, axis=axis, negate=neg,
                                apply_absolute_value=absval)
        if op == "max":
            return e.reduce_max(out=out, in_=in_, axis=axis, negate=neg,
                                apply_absolute_value=absval)
        return e.tensor_reduce(out=out, in_=in_, axis=axis, op=op)
    return f


def e_copy(out, in_):
    def f(e, s):
        if hasattr(e, "tensor_copy"):
            return e.tensor_copy(out=out, in_=in_)
        return e.copy(out=out, in_=in_)
    return f


def e_memset(ap, v):
    return lambda e, s: e.memset(ap, v)


def e_recip(out, in_):
    return lambda e, s: e.reciprocal(out=out, in_=in_)


def e_iota(out, pattern, base=0, cm=0):
    return lambda e, s: e.iota(out, pattern=pattern, base=base, channel_multiplier=cm,
                               allow_small_or_imprecise_dtypes=True)



def build_A1():
    nc = bass.Bass("TRN2", target_bir_lowering=False)
    xT = nc.dram_tensor("xT", [DM, 1024], F32, kind="ExternalInput").ap()
    wq = nc.dram_tensor("wq", [HEADS0, DM, HD], F32, kind="ExternalInput").ap()
    bqT = nc.dram_tensor("bqT", [128, HEADS0], F32, kind="ExternalInput").ap()
    wk = nc.dram_tensor("wk", [GRP0, DM, HD], F32, kind="ExternalInput").ap()
    bkT = nc.dram_tensor("bkT", [128, GRP0], F32, kind="ExternalInput").ap()
    wv = nc.dram_tensor("wv", [GRP0, DM, HD], F32, kind="ExternalInput").ap()
    bvT = nc.dram_tensor("bvT", [128, GRP0], F32, kind="ExternalInput").ap()
    wsl = [nc.dram_tensor(f"w{j}s", [128, 11264], F32, kind="ExternalInput").ap()
           for j in (1, 2, 3)]
    qT_o = nc.dram_tensor("qT_o", [HEADS0, 128, 1024], BF16, kind="ExternalOutput").ap()
    kT_o = nc.dram_tensor("kT_o", [GRP0, 128, 1024], BF16, kind="ExternalOutput").ap()
    v_o = nc.dram_tensor("v_o", [GRP0, 1024, HD], BF16, kind="ExternalOutput").ap()
    wsums_o = nc.dram_tensor("wsums_o", [1, 4], F32, kind="ExternalOutput").ap()

    es = contextlib.ExitStack()
    sb = lambda name, shape, dt=BF16: es.enter_context(nc.sbuf_tensor(name, shape, dt))
    ps = lambda name, shape, dt=F32: es.enter_context(nc.psum_tensor(name, shape, dt))

    xfm = sb("xfm", [128, FT, 1024])
    wst = sb("wst", [128, 2, FT, HD])
    bq_sb = sb("bq_sb", [128, HEADS0], F32)
    bqs_sb = sb("bqs_sb", [128, HEADS0], F32)
    bk_sb = sb("bk_sb", [128, GRP0], F32)
    bv_sb = sb("bv_sb", [128, GRP0], F32)
    ostage = sb("ostage", [128, 4, 1024])
    vstage = sb("vstage", [128, 2, 8, HD])
    ident = sb("ident", [128, 128])
    iot = sb("iot", [128, 128], F32)
    wbuf = sb("wbuf", [128, 2, 2816], F32)
    red1 = sb("red1", [128, 22], F32)
    redc = sb("redc", [128, 12], F32)
    red2 = sb("red2", [128, 1], F32)
    ones_col = sb("ones_col", [128, 1], F32)
    wsums_sb = sb("wsums_sb", [1, 4], F32)
    pa = ps("pa", [128, 512])
    pb = ps("pb", [128, 512])
    ptr = es.enter_context(nc.psum_tensor("ptr", [128, 512], BF16))
    ptr2 = es.enter_context(nc.psum_tensor("ptr2", [128, 512], BF16))
    pss = ps("pss", [1, 4])

    sch = Sch()
    sch.dma("pe", xfm[:], xT.rearrange("(t p) n -> p t n", p=128), sem="dx")
    ld_x = ("dx", sch.val("dx"))
    sch.dma("sq", bq_sb[:], bqT[:], sem="db")
    sch.dma("sq", bk_sb[:], bkT[:], sem="db")
    sch.dma("sq", bv_sb[:], bvT[:], sem="db")
    ld_b = ("db", sch.val("db"))
    sch.op("pe", e_iota(iot[:], [[-1, 128]], 0, 1), inc="pe")
    sch.op("ve", e_ts(ident[:], iot[:], 0.0, None, ALU.is_equal),
           wait=[("pe", sch.val("pe"))], inc="ve")
    id_ready = sch.val("ve")
    sch.op("ve", e_ts(bqs_sb[:], bq_sb[:], ISQ), wait=[ld_b], inc="ve")
    bq_ready = sch.val("ve")

    units = ([("q", h) for h in range(HEADS0)] + [("k", g) for g in range(GRP0)]
             + [("v", g) for g in range(GRP0)])
    pp = [pa, pb]
    wld = {}
    st_of_stage = {}
    tr_of_vstage = {}
    ev_mm = {}
    se_evacs = 0
    ve_evacs = 0

    def load_unit(u, wait):
        kind, idx = units[u]
        srct = {"q": wq, "k": wk, "v": wv}[kind]
        sem = f"dw{u % 2}"
        sch.dma("pe", wst[:, u % 2, :, :],
                srct.rearrange("h (t p) d -> p h t d", p=128)[:, idx, :, :],
                sem=sem, wait=wait)
        wld[u] = (sem, sch.val(sem))

    load_unit(0, ())
    load_unit(1, ())
    for u, (kind, idx) in enumerate(units):
        par = u % 2
        for half in range(2):
            gidx = u * 2 + half
            bank = gidx % 2
            w = [wld[u]]
            if u == 0 and half == 0:
                w.append(ld_x)
            if gidx >= 2:
                w.append(("se", gidx - 1))
            for ft in range(FT):
                sch.op("te", e_matmul(pp[bank][:], wst[:, par, ft, :],
                                      xfm[:, ft, half * 512:(half + 1) * 512],
                                      start=(ft == 0), stop=(ft == FT - 1)),
                       wait=w if ft == 0 else (),
                       inc="te" if ft == FT - 1 else None)
            mmv = sch.val("te")
            if half == 1:
                ev_mm[u] = mmv
                if u + 2 < len(units):
                    load_unit(u + 2, [("te", mmv)])
            slot = u % 4
            wev = [("te", mmv), ld_b]
            if kind == "q":
                wev.append(("ve", bq_ready))
            if half == 0 and st_of_stage:
                wev.append(("so", sch.val("so")))
            bias_t = {"q": bqs_sb, "k": bk_sb, "v": bv_sb}[kind]
            bias = bias_t[:, idx:idx + 1]
            scale = ISQ if kind == "q" else None
            sch.op("se", e_act(ostage[:, slot, half * 512:(half + 1) * 512],
                               pp[bank][:], AF.Identity, bias=bias, scale=scale),
                   inc="se", wait=wev)
            se_evacs = sch.val("se")
        if kind == "q":
            sch.dma("sq", qT_o[idx, :, :], ostage[:, u % 4, :], sem="so",
                    wait=[("se", se_evacs)])
            st_of_stage[u % 4] = sch.val("so")
        elif kind == "k":
            sch.dma("sq", kT_o[idx, :, :], ostage[:, u % 4, :], sem="so",
                    wait=[("se", se_evacs)])
            st_of_stage[u % 4] = sch.val("so")
        else:
            for j4 in range(2):
                ptrb = [ptr, ptr2][j4]
                wtr = [("se", se_evacs), ("ve", id_ready)]
                prev_ev = tr_of_vstage.get(j4)
                if prev_ev is not None:
                    wtr.append(("ve", prev_ev))
                for jj in range(4):
                    j = j4 * 4 + jj
                    sch.op("te", e_transpose(ptrb[:, jj * 128:(jj + 1) * 128],
                                             ostage[:, u % 4, j * 128:(j + 1) * 128],
                                             ident[:]),
                           wait=wtr if jj == 0 else (),
                           inc="te" if jj == 3 else None)
                trv = sch.val("te")
                vpar = idx % 2
                wev2 = [("te", trv)]
                if st_of_stage:
                    wev2.append(("so", sch.val("so")))
                sch.op("ve", e_copy(vstage[:, vpar, j4 * 4:(j4 + 1) * 4, :], ptrb[:]),
                       wait=wev2, inc="ve")
                tr_of_vstage[j4] = sch.val("ve")
            ve_evacs = sch.val("ve")
            sch.dma("sq", v_o[idx, :, :].rearrange("(j p) d -> p j d", p=128),
                    vstage[:, idx % 2, :, :], sem="so", wait=[("ve", ve_evacs)])
            st_of_stage[f"v{idx % 2}"] = sch.val("so")

    sch.op("ve", e_memset(ones_col[:], 1.0), inc="ve")
    prev_red = None
    for j in range(3):
        for ch in range(4):
            k = j * 4 + ch
            par = k % 2
            w = []
            if k >= 2:
                w.append(("ve", prev_red2[k - 2]))
            sch.dma("sq", wbuf[:, par, :], wsl[j][:, ch * 2816:(ch + 1) * 2816],
                    sem=f"dc{par}", wait=w)
            ldv = (f"dc{par}", sch.val(f"dc{par}"))
            if k == 0:
                prev_red2 = {}
            sch.op("ve", e_red(red1[:], wbuf[:, par, :].rearrange("p (a b) -> p a b", b=128),
                               AX.X, "sum", absval=True), wait=[ldv], inc="ve")
            sch.drain("ve")
            sch.op("ve", e_red(redc[:, k:k + 1], red1[:], AX.X, "sum"), inc="ve")
            sch.drain("ve")
            prev_red2[k] = sch.val("ve")
    for j in range(3):
        sch.op("ve", e_red(red2[:], redc[:, j * 4:(j + 1) * 4], AX.X, "sum"),
               wait=[("te", sch.val("te"))] if j > 0 else (), inc="ve")
        sch.drain("ve")
        rv = sch.val("ve")
        sch.op("te", e_matmul(pss[0:1, j:j + 1], ones_col[:], red2[:],
                              start=True, stop=True),
               wait=[("ve", rv)], inc="te")
    sumv = sch.val("te")
    sch.op("ve", e_memset(wsums_sb[:], 0.0), inc="ve")
    sch.drain("ve")
    sch.op("ve", e_copy(wsums_sb[0:1, 0:3], pss[0:1, 0:3]), wait=[("te", sumv)], inc="ve")
    sch.dma("sq", wsums_o[:], wsums_sb[:], sem="so", wait=[("ve", sch.val("ve"))])
    st = sch.val("so")
    sch.op("sq", lambda e, s: e.nop(), wait=[("so", st)])

    sch.build(nc, es)
    with es:
        pass
    return nc



def emit_attention(sch, cfg):
    (mode, n_outer, heads_of, kf, vf, kg_sb, vg_sb, q_sb, p_st, tmask,
     tm_ready, s_sb, negm, l_st, l_tot, rl, pt_sb, ident, id_ready, SP,
     PTR, PV, x_tm, yq, ld_q, ld_x, ytmp) = cfg
    sp_reader = {0: None, 1: None}
    pv_reader = {0: None, 1: None}
    ptr_evac = {0: None, 1: None}
    pst_tr = {}
    lst_free = {0: None, 1: None}
    s_sb_free = None
    chunk_ctr = [0]
    unit_ctr = [0]
    kv_reads = [0]

    def make_pass2(u):
        cap, lpar, pvb, s, h, exp_se = (u["cap"], u["lpar"], u["pvb"], u["s"],
                                        u["h"], u["exp_se"])
        kpar = u["kpar"]
        sch.op("ve", e_red(l_tot[:, lpar:lpar + 1],
                           l_st[:, lpar * 8:lpar * 8 + cap], AX.X, "sum"),
               wait=[("se", exp_se[cap - 1])], inc="ve")
        sch.drain("ve")
        sch.op("ve", e_recip(rl[:, lpar:lpar + 1], l_tot[:, lpar:lpar + 1]),
               inc="ve")
        sch.drain("ve")
        rl_ve = sch.val("ve")
        tr_te = {}
        ev_ve = {}
        def emit_tr(ci):
            ptrb = ci % 2
            wtr = [("ve", id_ready), ("se", exp_se[ci])]
            if ptr_evac[ptrb] is not None:
                wtr.append(ptr_evac[ptrb])
            for j in range(8):
                sch.op("te", e_transpose(
                    PTR[ptrb][:, j * 128:(j + 1) * 128],
                    p_st[:, ci * 1024 + j * 128: ci * 1024 + (j + 1) * 128],
                    ident[:]),
                    wait=wtr if j == 0 else (), inc="te" if j == 7 else None)
            tr_te[ci] = sch.val("te")
            pst_tr[ci] = ("te", tr_te[ci])
            ev_eng = "ve" if ci % 2 == 0 else "se"
            sch.op(ev_eng, e_copy(pt_sb[:, ptrb, :], PTR[ptrb][:]),
                   wait=[("te", tr_te[ci])], inc=ev_eng)
            ev_ve[ci] = (ev_eng, sch.val(ev_eng))
        def emit_pv(ci):
            ptrb = ci % 2
            wpv = [ev_ve[ci]]
            if ci == 0 and pv_reader[pvb] is not None:
                wpv.append(pv_reader[pvb])
            for j in range(8):
                mm = e_matmul(PV[pvb], pt_sb[:, ptrb, j * 128:(j + 1) * 128],
                              vg_sb[:, kpar, ci * 8 + j, :],
                              start=(ci == 0 and j == 0),
                              stop=(ci == cap - 1 and j == 7))
                sch.op("te", mm, wait=wpv if j == 0 else (),
                       inc="te" if j == 7 else None)
            ptr_evac[ptrb] = ("te", sch.val("te"))
        emit_tr(0)
        u["tr_done"] = 0
        yield
        for ci in range(1, cap):
            emit_tr(ci)
            u["tr_done"] = ci
            emit_pv(ci - 1)
            yield
        emit_pv(cap - 1)
        u["tr_done"] = 100
        pv_done = sch.val("te")
        kv_reads[0] = pv_done
        if mode == "l0":
            sch.op("ve", e_stt(x_tm[:, s, h * 128:(h + 1) * 128], PV[pvb],
                               rl[:, lpar:lpar + 1],
                               x_tm[:, s, h * 128:(h + 1) * 128],
                               ALU.mult, ALU.add),
                   wait=[("te", pv_done), ("ve", rl_ve), ld_x], inc="ve")
            pv_reader[pvb] = ("ve", sch.val("ve"))
            lst_free[lpar] = ("ve", sch.val("ve"))
        else:
            sch.op("ve", e_ts(ytmp[:, lpar, :], PV[pvb], rl[:, lpar:lpar + 1]),
                   wait=[("te", pv_done), ("ve", rl_ve)], inc="ve")
            pv_reader[pvb] = ("ve", sch.val("ve"))
            ts_ve = sch.val("ve")
            ptrb = cap % 2
            wtr2 = [("ve", ts_ve), ("ve", id_ready)]
            if ptr_evac[ptrb] is not None:
                wtr2.append(ptr_evac[ptrb])
            sch.op("te", e_transpose(PTR[ptrb][:, 0:128], ytmp[:, lpar, :],
                                     ident[:]), wait=wtr2, inc="te")
            ytr = sch.val("te")
            sch.op("ve", e_copy(yq[:, h, s * 128:(s + 1) * 128],
                                PTR[ptrb][:, 0:128]),
                   wait=[("te", ytr)], inc="ve")
            ptr_evac[ptrb] = ("ve", sch.val("ve"))
            lst_free[lpar] = ("ve", sch.val("ve"))

    pending = None

    def advance_pass2(upto):
        if pending is None:
            return
        u, gen = pending
        while True:
            if upto is not None and u.get("tr_done", -1) >= min(upto, u["cap"] - 1):
                return
            try:
                next(gen)
            except StopIteration:
                return

    def flush_pass2():
        nonlocal pending
        if pending is not None:
            u, gen = pending
            for _ in gen:
                pass
            pending = None

    kv_te_of_par = {0: 0, 1: 0}
    kv_ld_of_par = {}
    def stage_kv(outer):
        par = outer % 2
        wkv = [("te", kv_te_of_par[par])] if outer >= 2 else []
        sch.dma("sq", kg_sb[:, par, :], kf[outer, :, :], sem=f"dkv{par}", wait=wkv)
        sch.dma("sq", vg_sb[:, par, :, :],
                vf[outer, :, :].rearrange("(j p) d -> p j d", p=128),
                sem=f"dkv{par}", wait=wkv)
        kv_ld_of_par[outer] = (f"dkv{par}", sch.val(f"dkv{par}"))
    stage_kv(0)
    if n_outer > 1:
        stage_kv(1)
    for outer in range(n_outer):
        kpar = outer % 2
        kv_ld = kv_ld_of_par[outer]
        for h in heads_of(outer):
            for s in range(NSLOT):
                cap = CAPS[s]
                pvb = unit_ctr[0] % 2
                lpar = unit_ctr[0] % 2
                ustate = {"cap": cap, "lpar": lpar, "pvb": pvb, "s": s, "h": h,
                          "kpar": kpar}
                exp_se = {}
                ustate["exp_se"] = exp_se
                rm_ve = None
                for ci in range(cap):
                    bank = chunk_ctr[0] % 2
                    masked = (ci == cap - 1)
                    w0 = [kv_ld, ld_q] if ci == 0 else []
                    if sp_reader[bank] is not None:
                        w0.append(sp_reader[bank])
                    for half in range(2):
                        sch.op("te", e_matmul(
                            SP[bank][:, half * 512:(half + 1) * 512],
                            q_sb[:, h, s * 128:(s + 1) * 128],
                            kg_sb[:, kpar, ci * 1024 + half * 512: ci * 1024 + (half + 1) * 512],
                            start=True, stop=True),
                            wait=w0 if half == 0 else (), inc="te")
                    mmv = sch.val("te")
                    advance_pass2(ci)
                    exp_in = SP[bank][:]
                    stt_ve = None
                    if masked:
                        wstt = [("te", mmv), ("ve", tm_ready)]
                        if s_sb_free is not None:
                            wstt.append(s_sb_free)
                        sch.op("ve", e_stt(s_sb[:], tmask[:, s, :], -1e30, SP[bank][:],
                                           ALU.mult, ALU.add), wait=wstt, inc="ve")
                        sch.drain("ve")
                        stt_ve = sch.val("ve")
                        sp_reader[bank] = ("ve", stt_ve)
                        exp_in = s_sb[:]
                    if ci == 0:
                        wrm = [] if masked else [("te", mmv)]
                        if lst_free[lpar] is not None:
                            wrm.append(lst_free[lpar])
                        sch.op("ve", e_red(negm[:, lpar:lpar + 1], exp_in, AX.X,
                                           "max", neg=True), wait=wrm, inc="ve")
                        sch.drain("ve")
                        rm_ve = sch.val("ve")
                    wex = [("ve", rm_ve)]
                    if masked:
                        wex.append(("ve", stt_ve))
                    else:
                        wex.append(("te", mmv))
                    if pst_tr.get(ci) is not None:
                        wex.append(pst_tr[ci])
                    if ci == 0 and lst_free[lpar] is not None:
                        wex.append(lst_free[lpar])
                    sch.op("se", e_act(p_st[:, ci * 1024:(ci + 1) * 1024], exp_in,
                                       AF.Exp, bias=negm[:, lpar:lpar + 1],
                                       accum_out=l_st[:, lpar * 8 + ci:lpar * 8 + ci + 1]),
                           wait=wex, inc="se")
                    exp_se[ci] = sch.val("se")
                    if masked:
                        s_sb_free = ("se", exp_se[ci])
                    else:
                        sp_reader[bank] = ("se", exp_se[ci])
                    chunk_ctr[0] += 1
                flush_pass2()
                pending = (ustate, make_pass2(ustate))
                unit_ctr[0] += 1
        kv_te_of_par[kpar] = None
        flush_pass2()
        kv_te_of_par[kpar] = kv_reads[0]
        if outer + 2 < n_outer:
            stage_kv(outer + 2)
    flush_pass2()
    return sch.val("te"), sch.val("ve"), sch.val("se")



def build_A2():
    nc = bass.Bass("TRN2", target_bir_lowering=False)
    x_own = nc.dram_tensor("x_own", [1024, DM], F32, kind="ExternalInput").ap()
    qT = nc.dram_tensor("qT", [HEADS0, 128, 1024], BF16, kind="ExternalInput").ap()
    kf = nc.dram_tensor("kf", [GRP0, 128, SEQ], BF16, kind="ExternalInput").ap()
    vf = nc.dram_tensor("vf", [GRP0, SEQ, HD], BF16, kind="ExternalInput").ap()
    qpos = nc.dram_tensor("qpos", [128, NSLOT], F32, kind="ExternalInput").ap()
    gwT = nc.dram_tensor("gwT", [DM, MOE_E], F32, kind="ExternalInput").ap()
    x1_o = nc.dram_tensor("x1_o", [1024, DM], F32, kind="ExternalOutput").ap()
    x1T_o = nc.dram_tensor("x1T_o", [DM, 1024], BF16, kind="ExternalOutput").ap()
    cw_o = nc.dram_tensor("cw_o", [1024, MOE_E], F32, kind="ExternalOutput").ap()

    es = contextlib.ExitStack()
    sb = lambda name, shape, dt=BF16: es.enter_context(nc.sbuf_tensor(name, shape, dt))
    ps = lambda name, shape, dt=F32: es.enter_context(nc.psum_tensor(name, shape, dt))

    x_tm = sb("x_tm", [128, NSLOT, DM], F32)
    q_sb = sb("q_sb", [128, HEADS0, 1024])
    kg_sb = sb("kg_sb", [128, 2, SEQ])
    vg_sb = sb("vg_sb", [128, 2, 64, HD])
    p_st = sb("p_st", [128, SEQ])
    bvec = sb("bvec", [128, NSLOT], F32)
    tmask = sb("tmask", [128, NSLOT, 1024], U8)
    qpos_sb = sb("qpos_sb", [128, NSLOT], F32)
    s_sb = sb("s_sb", [128, 1024], F32)
    negm = sb("negm", [128, 2], F32)
    l_stv = sb("l_stv", [128, 16], F32)
    l_tot = sb("l_tot", [128, 2], F32)
    rl = sb("rl", [128, 2], F32)
    pt_sb = sb("pt_sb", [128, 2, 1024])
    ident = sb("ident", [128, 128])
    iot = sb("iot", [128, 128], F32)
    b8 = ln8_bufs(sb)
    x1bf = sb("x1bf", [128, DM])
    x1ts = sb("x1ts", [128, FT, 128])
    gw_sb = sb("gw_sb", [128, FT, MOE_E])
    esc_all = sb("esc_all", [128, NSLOT * MOE_E], F32)
    cur_all = sb("cur_all", [128, NSLOT * MOE_E], F32)
    tsel_all = sb("tsel_all", [128, NSLOT * MOE_E], F32)
    gmax = sb("gmax", [128, 1], F32)
    esum8 = sb("esum8", [128, NSLOT], F32)
    ers8 = sb("ers8", [128, NSLOT], F32)
    mk8 = sb("mk8", [128, NSLOT], F32)
    cw_sb = sb("cw_sb", [128, NSLOT, MOE_E], F32)

    SP = [ps("sp0", [128, 1024]), ps("sp1", [128, 1024])]
    PTR = [es.enter_context(nc.psum_tensor("ptr0", [128, 1024], BF16)),
           es.enter_context(nc.psum_tensor("ptr1", [128, 1024], BF16))]
    PVT = [ps("pv0", [128, 512]), ps("pv1", [128, 512])]
    PV = [PVT[0][:, 0:128], PVT[1][:, 0:128]]
    pgate = PVT[0][:, 128:128 + MOE_E]

    sch = Sch()
    sch.dma("pe", x_tm[:], x_own.rearrange("(s p) f -> p s f", p=128), sem="dinp")
    sch.dma("sq", q_sb[:], qT.rearrange("h p n -> p h n"), sem="dinq")
    sch.dma("sq", qpos_sb[:], qpos[:], sem="dinq")
    sch.dma("pe", gw_sb[:], gwT.rearrange("(t p) e -> p t e", p=128), sem="dinp")
    ld_p = ("dinp", sch.val("dinp"))
    ld_q2 = ("dinq", sch.val("dinq"))
    ld_x = ld_gw = ld_p
    ld_q = ld_qp = ld_q2
    sch.op("pe", e_iota(iot[:], [[-1, 128]], 0, 1), inc="pe")
    sch.op("ve", e_ts(ident[:], iot[:], 0.0, None, ALU.is_equal),
           wait=[("pe", sch.val("pe"))], inc="ve")
    id_ready = sch.val("ve")
    jvec = s_sb
    sch.op("pe", e_iota(jvec[:], [[1, 1024]], 0, 0), inc="pe")
    jv_pe = sch.val("pe")
    for s in range(NSLOT):
        sch.op("ve", e_ts(bvec[:, s:s + 1], qpos_sb[:, s:s + 1], 1.0,
                          float(-1024 * (CAPS[s] - 1)), ALU.mult, ALU.add),
               wait=[ld_qp, ("pe", jv_pe)], inc="ve")
    sch.drain("ve")
    for s in range(NSLOT):
        sch.op("ve", e_ts(tmask[:, s, :], jvec[:], bvec[:, s:s + 1], None, ALU.is_gt),
               inc="ve")
    sch.drain("ve")
    jv_ready = sch.val("ve")

    cfg = ("l0", GRP0, lambda g: range(4 * g, 4 * g + 4), kf, vf, kg_sb, vg_sb,
           q_sb, p_st, tmask, jv_ready, s_sb, negm, l_stv, l_tot,
           rl, pt_sb, ident, id_ready, SP, PTR, PV, x_tm, None, ld_q, ld_x, None)
    emit_attention(sch, cfg)
    attn_ve = sch.val("ve")
    attn_se = sch.val("se")
    ptr_evac = {0: ("ve", attn_ve), 1: ("ve", attn_ve)}

    stats_ve = emit_ln_batched(sch, b8, x_tm, x1bf[:], attn_ve, attn_se)
    rstd8, nmr8 = b8["rstd8"], b8["nmr8"]
    st_x1t = {}
    pgate_all = PVT[0][:, 128:256]
    last_x1t_te = sch.val("te")
    for s in range(NSLOT):
        xs = x_tm[:, s, :]
        wap = [("ve", stats_ve)] if s == 0 else [("se", sch.val("se"))]
        sch.op("ve", e_ts(xs, xs, rstd8[:, s:s + 1], nmr8[:, s:s + 1],
                          ALU.mult, ALU.add), wait=wap, inc="ve")
        ln_ve = sch.val("ve")
        wbf = [("ve", ln_ve), ("te", last_x1t_te), ("se", sch.val("se"))]
        sch.op("se", e_act(x1bf[:], xs, AF.Identity), wait=wbf, inc="se")
        bf_se = sch.val("se")
        for f4 in range(4):
            ptrb = f4 % 2
            wtr = [("se", bf_se), ("ve", id_ready)]
            if ptr_evac[ptrb] is not None:
                wtr.append(ptr_evac[ptrb])
            for jj in range(4):
                ft = f4 * 4 + jj
                sch.op("te", e_transpose(PTR[ptrb][:, jj * 128:(jj + 1) * 128],
                                         x1bf[:, ft * 128:(ft + 1) * 128],
                                         ident[:]),
                       wait=wtr if jj == 0 else (), inc="te" if jj == 3 else None)
            trv = sch.val("te")
            last_x1t_te = trv
            wev = [("te", trv)]
            if s > 0 and f4 == 0 and st_x1t:
                wev.append(("so", sch.val("so")))
            sch.op("ve", e_copy(x1ts[:, f4 * 4:(f4 + 1) * 4, :], PTR[ptrb][:, 0:512]),
                   wait=wev, inc="ve")
            ptr_evac[ptrb] = ("ve", sch.val("ve"))
        x1t_ve = sch.val("ve")
        sch.dma("sq", x1T_o.rearrange("(t p) (sl i) -> p t sl i", p=128, i=128)[:, :, s, :],
                x1ts[:], sem="so", wait=[("ve", x1t_ve)])
        st_x1t[s] = sch.val("so")
        wg0 = [("ve", x1t_ve), ld_gw]
        if s == 0:
            wg0 += [("ve", attn_ve), ("se", attn_se)]
        for ft in range(FT):
            sch.op("te", e_matmul(pgate_all[:, s * 16:(s + 1) * 16],
                                  x1ts[:, ft, :], gw_sb[:, ft, :],
                                  start=(ft == 0), stop=(ft == FT - 1)),
                   wait=wg0 if ft == 0 else (), inc="te" if ft == FT - 1 else None)
    gv = sch.val("te")
    esc3 = esc_all.ap().rearrange("p (s e) -> p s e", e=MOE_E)
    cur3 = cur_all.ap().rearrange("p (s e) -> p s e", e=MOE_E)
    ts3 = tsel_all.ap().rearrange("p (s e) -> p s e", e=MOE_E)
    pg3 = pgate_all.rearrange("p (s e) -> p s e", e=MOE_E)
    sch.op("ve", e_red(mk8[:], pg3, AX.X, "max"), wait=[("te", gv)], inc="ve")
    sch.drain("ve")
    mkb0 = mk8.ap().rearrange("p (s o) -> p s o", o=1).broadcast_to((128, NSLOT, MOE_E))
    sch.op("ve", e_tt(esc3, pg3, mkb0, ALU.subtract), inc="ve")
    sch.drain("ve")
    sch.op("se", e_act(esc_all[:], esc_all[:], AF.Exp),
           wait=[("ve", sch.val("ve"))], inc="se")
    sch.op("ve", e_red(esum8[:], esc3, AX.X, "sum"),
           wait=[("se", sch.val("se"))], inc="ve")
    sch.drain("ve")
    sch.op("ve", e_recip(ers8[:], esum8[:]), inc="ve")
    sch.drain("ve")
    ers_b = ers8.ap().rearrange("p (s o) -> p s o", o=1).broadcast_to((128, NSLOT, MOE_E))
    sch.op("ve", e_tt(esc3, esc3, ers_b, ALU.mult), inc="ve")
    sch.drain("ve")
    sch.op("ve", e_copy(cur_all[:], esc_all[:]), inc="ve")
    sch.drain("ve")
    for k6 in range(MOE_K):
        sch.op("ve", e_red(mk8[:], cur3, AX.X, "max"), inc="ve")
        sch.drain("ve")
        mk_b = mk8.ap().rearrange("p (s o) -> p s o", o=1).broadcast_to((128, NSLOT, MOE_E))
        if k6 < MOE_K - 1:
            sch.op("ve", e_tt(ts3, cur3, mk_b, ALU.is_ge), inc="ve")
            sch.drain("ve")
            sch.op("ve", e_stt(cur3, ts3, -2.0, cur3, ALU.mult, ALU.add), inc="ve")
            sch.drain("ve")
    sch.op("ve", e_tt(ts3, esc3, mk_b, ALU.is_ge), inc="ve")
    sch.drain("ve")
    sch.op("ve", e_tt(cw_sb.ap(), esc3, ts3, ALU.mult), inc="ve")
    sch.drain("ve")
    done_ve = sch.val("ve")
    sch.dma("sq", x1_o.rearrange("(s p) f -> p s f", p=128), x_tm[:],
            sem="so", wait=[("ve", done_ve)])
    sch.dma("sq", cw_o.rearrange("(s p) e -> p s e", p=128), cw_sb[:],
            sem="so", wait=[("ve", done_ve)])
    st = sch.val("so")
    sch.op("sq", lambda e, s: e.nop(), wait=[("so", st)])

    sch.build(nc, es)
    with es:
        pass
    return nc



def build_B():
    nc = bass.Bass("TRN2", target_bir_lowering=False)
    x1T = nc.dram_tensor("x1T", [DM, SEQ], BF16, kind="ExternalInput").ap()
    cwe_f = nc.dram_tensor("cwe_f", [SEQ, 2], F32, kind="ExternalInput").ap()
    wg = nc.dram_tensor("wg", [2, DM, MOE_D], F32, kind="ExternalInput").ap()
    wu = nc.dram_tensor("wu", [2, DM, MOE_D], F32, kind="ExternalInput").ap()
    wdn = nc.dram_tensor("wdn", [2, MOE_D, DM], F32, kind="ExternalInput").ap()
    wsums = nc.dram_tensor("wsums", [8, 4], F32, kind="ExternalInput").ap()
    wsl = [nc.dram_tensor(f"w{j}s", [128, 11264], F32, kind="ExternalInput").ap()
           for j in (1, 2, 3)]
    yp_o = nc.dram_tensor("yp_o", [SEQ, DM], BF16, kind="ExternalOutput").ap()
    qw_o = [nc.dram_tensor(f"qw{j}_o", [128, 11264], BF16, kind="ExternalOutput").ap()
            for j in (1, 2, 3)]

    es = contextlib.ExitStack()
    sb = lambda name, shape, dt=BF16: es.enter_context(nc.sbuf_tensor(name, shape, dt))
    ps = lambda name, shape, dt=F32: es.enter_context(nc.psum_tensor(name, shape, dt))

    xb = sb("xb", [128, 2, FT, 1024])
    at = sb("at", [128, 11, 1024])
    wgu = sb("wgu", [128, 3, 2, FT, 128])
    wd_sb = sb("wd_sb", [128, 11, DM])
    yp_sb = sb("yp_sb", [128, 8, DM])
    sg = sb("sg", [128, 1024])
    cwe = sb("cwe", [128, 8, 2], F32)
    ws_sb = sb("ws_sb", [8, 4], F32)
    ones8 = sb("ones8", [8, 1], F32)
    ones1r = sb("ones1r", [1, 128], F32)
    gsc = sb("gsc", [1, 1], F32)
    ginv = sb("ginv", [1, 1], F32)
    invb = sb("invb", [128, 3], F32)
    QCH = 512
    wqb = sb("wqb", [128, 2, QCH], F32)
    ubuf_t = sb("ubuf_t", [128, QCH], F32)
    sgn_t = sb("sgn_t", [128, QCH])
    msk_t = sb("msk_t", [128, QCH])
    qbuf = sb("qbuf", [128, 2, QCH])
    ubuf = ubuf_t[:]
    sgn = sgn_t[:]
    msk = msk_t[:]

    PG = [ps("pg0", [128, 512]), ps("pg1", [128, 512])]
    PU = [ps("pu0", [128, 512]), ps("pu1", [128, 512])]
    PD = [ps("pd0", [128, 512]), ps("pd1", [128, 512])]
    pb1 = ps("pb1", [128, 512])

    sch = Sch()
    MT = 11
    sch.dma("sq", ws_sb[:], wsums[:], sem="dws")
    ld_ws = ("dws", sch.val("dws"))
    sch.op("ve", e_memset(ones8[:], 1.0), inc="ve")
    sch.op("ve", e_memset(ones1r[:], 1.0), inc="ve")
    ones_ve = sch.val("ve")
    NW = {0: DM * FFN_H, 1: DM * FFN_H, 2: FFN_H * DM}
    for j in range(3):
        wbj = [("ve", ones_ve), ld_ws]
        if j > 0:
            wbj.append(("ve", sch.val("ve")))
        sch.op("te", e_matmul(pb1[0:1, 0:1], ones8[:], ws_sb[:, j:j + 1],
                              start=True, stop=True), wait=wbj, inc="te")
        sch.op("ve", e_ts(gsc[:], pb1[0:1, 0:1], 1.0 / NW[j], 1e-8, ALU.mult, ALU.add),
               wait=[("te", sch.val("te"))], inc="ve")
        sch.drain("ve")
        sch.op("ve", e_recip(ginv[:], gsc[:]), inc="ve")
        sch.drain("ve")
        gv = sch.val("ve")
        sch.op("te", e_matmul(pb1[:, 1:2], ones1r[:], ginv[:], start=True, stop=True),
               wait=[("ve", gv)], inc="te")
        sch.op("ve", e_copy(invb[:, j:j + 1], pb1[:, 1:2]),
               wait=[("te", sch.val("te"))], inc="ve")
        sch.drain("ve")
    invb_ve = sch.val("ve")
    gamma_te = sch.val("te")

    QN = 11264 // QCH
    qstate = {"k": 0, "qve": {}, "qso": {}}

    def emit_quant_chunk():
        k = qstate["k"]
        if k >= 3 * QN:
            return
        qstate["k"] += 1
        j, ch = divmod(k, QN)
        par = k % 2
        w = []
        if k >= 2:
            w.append(("ve", qstate["qve"][k - 2]))
        sch.dma("sq", wqb[:, par, :], wsl[j][:, ch * QCH:(ch + 1) * QCH],
                sem=f"dq{par}", wait=w)
        ldv = (f"dq{par}", sch.val(f"dq{par}"))
        wabs = [ldv, ("ve", invb_ve)]
        if k >= 1:
            wabs.append(("ve", qstate["qve"][k - 1]))
        sch.op("se", e_act(ubuf, wqb[:, par, :], AF.Abs,
                           scale=invb[:, j:j + 1]),
               wait=wabs + [("se", sch.val("se"))], inc="se")
        sch.op("se", e_act(sgn, wqb[:, par, :], AF.Sign),
               wait=[("se", sch.val("se"))], inc="se")
        sgn_se = sch.val("se")
        sch.op("ve", e_ts(msk, ubuf, 0.5, None, ALU.is_gt),
               wait=[("se", sgn_se)], inc="ve")
        sch.drain("ve")
        wq2 = []
        if qstate["qso"].get(par) is not None:
            wq2.append(("so", sch.val("so")))
        sch.op("ve", e_tt(qbuf[:, par, :], msk, sgn, ALU.mult),
               wait=wq2, inc="ve")
        qstate["qve"][k] = sch.val("ve")
        sch.dma("sq", qw_o[j][:, ch * QCH:(ch + 1) * QCH], qbuf[:, par, :],
                sem="so", wait=[("ve", qstate["qve"][k])])
        qstate["qso"][par] = sch.val("so")

    gu_mm = {}
    pg_reader = {}
    pd_reader = {}
    wdn_mm = 0
    last_store = 0
    xb_te2 = {}
    prev_cw_ve = 0
    pending_store = [None]

    def flush_store():
        if pending_store[0] is not None:
            pending_store[0]()
            pending_store[0] = None

    for tb in range(8):
        xpar = tb % 2
        wx = [("te", xb_te2[tb - 2])] if tb >= 2 else []
        sch.dma("sq", xb[:, xpar, :, :],
                x1T.rearrange("(t p) n -> p t n", p=128)[:, :, tb * 1024:(tb + 1) * 1024],
                sem=f"dxb{xpar}", wait=wx)
        ld_xb = (f"dxb{xpar}", sch.val(f"dxb{xpar}"))
        wcw = [("ve", prev_cw_ve)] if tb > 0 else []
        sch.dma("sq", cwe[:], cwe_f.rearrange("(b t p) e -> b p t e", p=128, t=8)[tb],
                sem="dcw", wait=wcw)
        ld_cw = ("dcw", sch.val("dcw"))
        flush_store()
        for e in range(2):
            wdn_ld = None
            for mt in range(MT):
                par = mt % 3
                wld = []
                if gu_mm.get(par) is not None:
                    wld.append(("te", gu_mm[par]))
                sch.dma("pe", wgu[:, par, 0, :, :],
                        wg.rearrange("a (t p) m -> a p t m", p=128)[e][:, :, mt * 128:(mt + 1) * 128],
                        sem=f"dg{par}", wait=wld)
                sch.dma("pe", wgu[:, par, 1, :, :],
                        wu.rearrange("a (t p) m -> a p t m", p=128)[e][:, :, mt * 128:(mt + 1) * 128],
                        sem=f"dg{par}", wait=())
                ldg = (f"dg{par}", sch.val(f"dg{par}"))
                wwd = [("te", wdn_mm)] if (tb + e > 0 and mt == 0) else []
                sch.dma("pe", wd_sb[:, mt, :],
                        wdn.rearrange("a (t p) f -> a p t f", p=128)[e][:, mt, :],
                        sem="dwd", wait=wwd)
                if mt == MT - 1:
                    wdn_ld = ("dwd", sch.val("dwd"))
                for half in range(2):
                    pgb = PG[(mt * 2 + half) % 2]
                    pub = PU[(mt * 2 + half) % 2]
                    wmm = [ldg, ld_xb]
                    if pg_reader.get((mt * 2 + half) % 2) is not None:
                        wmm.append(pg_reader[(mt * 2 + half) % 2])
                    for ft in range(FT):
                        sch.op("te", e_matmul(pgb[:], wgu[:, par, 0, ft, :],
                                              xb[:, xpar, ft, half * 512:(half + 1) * 512],
                                              start=(ft == 0), stop=(ft == FT - 1)),
                               wait=wmm if ft == 0 else (),
                               inc="te" if ft == FT - 1 else None)
                    gv = sch.val("te")
                    for ft in range(FT):
                        sch.op("te", e_matmul(pub[:], wgu[:, par, 1, ft, :],
                                              xb[:, xpar, ft, half * 512:(half + 1) * 512],
                                              start=(ft == 0), stop=(ft == FT - 1)),
                               inc="te" if ft == FT - 1 else None)
                    uv = sch.val("te")
                    gu_mm[par] = uv
                    xb_te2[tb] = uv
                    sch.op("se", e_act(sg[:, half * 512:(half + 1) * 512], pgb[:],
                                       AF.Silu), wait=[("te", gv), ("ve", sch.val("ve"))],
                           inc="se")
                    sv = sch.val("se")
                    sch.op("ve", e_tt(at[:, mt, half * 512:(half + 1) * 512],
                                      sg[:, half * 512:(half + 1) * 512], pub[:],
                                      ALU.mult),
                           wait=[("se", sv), ("te", uv)], inc="ve")
                    pg_reader[(mt * 2 + half) % 2] = ("ve", sch.val("ve"))
            at_ve = sch.val("ve")
            first = (e == 0)
            for t8 in range(8):
                for d4 in range(4):
                    pdb = PD[(t8 * 4 + d4) % 2]
                    wmm = [("ve", at_ve), wdn_ld]
                    if pd_reader.get((t8 * 4 + d4) % 2) is not None:
                        wmm.append(pd_reader[(t8 * 4 + d4) % 2])
                    for mt in range(MT):
                        sch.op("te", e_matmul(pdb[:], at[:, mt, t8 * 128:(t8 + 1) * 128],
                                              wd_sb[:, mt, d4 * 512:(d4 + 1) * 512],
                                              start=(mt == 0), stop=(mt == MT - 1)),
                               wait=wmm if mt == 0 else (),
                               inc="te" if mt == MT - 1 else None)
                    wdn_mm = sch.val("te")
                    if d4 == 0:
                        emit_quant_chunk()
                    dst = yp_sb[:, t8, d4 * 512:(d4 + 1) * 512]
                    wev = [("te", wdn_mm), ld_cw]
                    if first and t8 == 0 and d4 == 0 and tb > 0:
                        wev.append(("so", last_store))
                    if first:
                        sch.op("ve", e_ts(dst, pdb[:], cwe[:, t8, 0:1]),
                               wait=wev, inc="ve")
                    else:
                        sch.op("ve", e_stt(dst, pdb[:], cwe[:, t8, 1:2], dst,
                                           ALU.mult, ALU.add),
                               wait=wev, inc="ve")
                    pd_reader[(t8 * 4 + d4) % 2] = ("ve", sch.val("ve"))
        prev_cw_ve = sch.val("ve")
        cwv = prev_cw_ve
        tb_ = tb
        def mk_store(tb_=tb_, cwv=cwv):
            def go():
                sch.dma("sq", yp_o[tb_ * 1024:(tb_ + 1) * 1024, :]
                        .rearrange("(t p) f -> p t f", p=128),
                        yp_sb[:], sem="so", wait=[("ve", cwv)])
            return go
        pending_store[0] = mk_store()
        if tb == 7:
            flush_store()
        last_store = sch.val("so") + 16 if tb < 7 else sch.val("so")

    while qstate["k"] < 3 * QN:
        emit_quant_chunk()
    st = sch.val("so")
    sch.op("sq", lambda e, s: e.nop(), wait=[("so", st)])

    sch.build(nc, es)
    with es:
        pass
    return nc



def emit_ln_rows(sch, bufs, xs, wait_ve, wait_se=None):
    s1, sqs, mean, ex2, m2, var, sd, rstd, nmr_t, nmr, jm = bufs
    w0 = [("ve", wait_ve)]
    w1 = [("ve", wait_ve)] + ([("se", wait_se)] if wait_se else [])
    sch.op("ve", e_red(s1[:], xs, AX.X, "sum"), wait=w0, inc="ve")
    sch.op("se", e_act(jm[:], xs, AF.Square, accum_out=sqs[:]), wait=w1, inc="se")
    sq_se = sch.val("se")
    sch.drain("ve")
    sch.op("ve", e_ts(mean[:], s1[:], 1.0 / DM), inc="ve")
    sch.op("ve", e_ts(ex2[:], sqs[:], 1.0 / DM, 1e-5, ALU.mult, ALU.add),
           wait=[("se", sq_se)], inc="ve")
    sch.drain("ve")
    sch.op("ve", e_tt(m2[:], mean[:], mean[:], ALU.mult), inc="ve")
    sch.drain("ve")
    sch.op("ve", e_tt(var[:], ex2[:], m2[:], ALU.subtract), inc="ve")
    sch.drain("ve")
    sch.op("se", e_act(sd[:], var[:], AF.Sqrt), wait=[("ve", sch.val("ve"))], inc="se")
    sch.op("ve", e_recip(rstd[:], sd[:]), wait=[("se", sch.val("se"))], inc="ve")
    sch.drain("ve")
    sch.op("ve", e_tt(nmr_t[:], mean[:], rstd[:], ALU.mult), inc="ve")
    sch.drain("ve")
    sch.op("ve", e_ts(nmr[:], nmr_t[:], -1.0), inc="ve")
    sch.drain("ve")
    sch.op("ve", e_ts(xs, xs, rstd[:], nmr[:], ALU.mult, ALU.add), inc="ve")
    return sch.val("ve")


def ln8_bufs(sb):
    names = ["s1a", "sqa", "mean8", "ex28", "m28", "var8", "sd8", "rstd8",
             "nmrt8", "nmr8"]
    return {n: sb("l8_" + n, [128, NSLOT], F32) for n in names}


def emit_ln_batched(sch, b8, x_tm, jm_bf, wait_ve, wait_se):
    for s in range(NSLOT):
        sch.op("ve", e_red(b8["s1a"][:, s:s + 1], x_tm[:, s, :], AX.X, "sum"),
               wait=[("ve", wait_ve)] if s == 0 else (), inc="ve")
        sch.op("se", e_act(jm_bf, x_tm[:, s, :], AF.Square,
                           accum_out=b8["sqa"][:, s:s + 1]),
               wait=([("se", wait_se)] if s == 0 else [("se", sch.val("se"))])
               + [("ve", wait_ve)], inc="se")
    sq_se = sch.val("se")
    sch.drain("ve")
    sch.op("ve", e_ts(b8["mean8"][:], b8["s1a"][:], 1.0 / DM), inc="ve")
    sch.op("ve", e_ts(b8["ex28"][:], b8["sqa"][:], 1.0 / DM, 1e-5, ALU.mult, ALU.add),
           wait=[("se", sq_se)], inc="ve")
    sch.drain("ve")
    sch.op("ve", e_tt(b8["m28"][:], b8["mean8"][:], b8["mean8"][:], ALU.mult), inc="ve")
    sch.drain("ve")
    sch.op("ve", e_tt(b8["var8"][:], b8["ex28"][:], b8["m28"][:], ALU.subtract), inc="ve")
    sch.drain("ve")
    sch.op("se", e_act(b8["sd8"][:], b8["var8"][:], AF.Sqrt),
           wait=[("ve", sch.val("ve"))], inc="se")
    sch.op("ve", e_recip(b8["rstd8"][:], b8["sd8"][:]),
           wait=[("se", sch.val("se"))], inc="ve")
    sch.drain("ve")
    sch.op("ve", e_tt(b8["nmrt8"][:], b8["mean8"][:], b8["rstd8"][:], ALU.mult), inc="ve")
    sch.drain("ve")
    sch.op("ve", e_ts(b8["nmr8"][:], b8["nmrt8"][:], -1.0), inc="ve")
    sch.drain("ve")
    return sch.val("ve")


def ln_bufs(sb, jm=None):
    names = ["s1", "sqs", "mean", "ex2", "m2", "var", "sd", "rstd", "nmr_t", "nmr"]
    out = [sb("ln_" + n, [128, 1], F32) for n in names]
    out.append(sb("ln_jm", [128, DM], F32) if jm is None else jm)
    return out


def emit_transpose_out(sch, nc_objs, s, xs_bf_getter, dram_out, PTR, ident,
                       id_ready, x1ts, ptr_evac, store_sem="so", extra_wait=None):
    for f4 in range(4):
        ptrb = f4 % 2
        wtr = [("ve", id_ready)]
        if extra_wait is not None:
            wtr.append(extra_wait)
        if ptr_evac[ptrb] is not None:
            wtr.append(ptr_evac[ptrb])
        for jj in range(4):
            ft = f4 * 4 + jj
            sch.op("te", e_transpose(PTR[ptrb][:, jj * 128:(jj + 1) * 128],
                                     xs_bf_getter(ft), ident[:]),
                   wait=wtr if jj == 0 else (), inc="te" if jj == 3 else None)
        trv = sch.val("te")
        wev = [("te", trv)]
        if sch.val(store_sem) > 0:
            wev.append((store_sem, sch.val(store_sem)))
        sch.op("ve", e_copy(x1ts[:, f4 * 4:(f4 + 1) * 4, :], PTR[ptrb][:, 0:512]),
               wait=wev, inc="ve")
        ptr_evac[ptrb] = ("ve", sch.val("ve"))
    ev = sch.val("ve")
    sch.dma("sq", dram_out.rearrange("(t p) (sl i) -> p t sl i", p=128, i=128)[:, :, s, :],
            x1ts[:], sem=store_sem, wait=[("ve", ev)])
    return ptr_evac


def build_C1a():
    nc = bass.Bass("TRN2", target_bir_lowering=False)
    x1 = nc.dram_tensor("x1", [1024, DM], F32, kind="ExternalInput").ap()
    yps = nc.dram_tensor("yps", [8, 1024, DM], BF16, kind="ExternalInput").ap()
    x2_o = nc.dram_tensor("x2_o", [1024, DM], F32, kind="ExternalOutput").ap()
    x2T_o = nc.dram_tensor("x2T_o", [DM, 1024], BF16, kind="ExternalOutput").ap()

    es = contextlib.ExitStack()
    sb = lambda name, shape, dt=BF16: es.enter_context(nc.sbuf_tensor(name, shape, dt))
    ps = lambda name, shape, dt=F32: es.enter_context(nc.psum_tensor(name, shape, dt))
    x_tm = sb("x_tm", [128, NSLOT, DM], F32)
    ypb = sb("ypb", [128, 2, NSLOT, DM])
    b8c = ln8_bufs(sb)
    xbf = sb("xbf", [128, DM])
    x1ts = sb("x1ts", [128, FT, 128])
    ident = sb("ident", [128, 128])
    iot = sb("iot", [128, 128], F32)
    PTR = [es.enter_context(nc.psum_tensor("ptr0", [128, 512], BF16)),
           es.enter_context(nc.psum_tensor("ptr1", [128, 512], BF16))]

    sch = Sch()
    sch.dma("pe", x_tm[:], x1.rearrange("(s p) f -> p s f", p=128), sem="dx")
    ld_x = ("dx", sch.val("dx"))
    sch.op("pe", e_iota(iot[:], [[-1, 128]], 0, 1), inc="pe")
    sch.op("ve", e_ts(ident[:], iot[:], 0.0, None, ALU.is_equal),
           wait=[("pe", sch.val("pe"))], inc="ve")
    id_ready = sch.val("ve")
    add_ve = {}
    for p in range(8):
        par = p % 2
        w = [("ve", add_ve[p - 2])] if p >= 2 else []
        sch.dma("sq", ypb[:, par, :, :], yps[p].rearrange("(s q) f -> q s f", q=128),
                sem=f"dy{par}", wait=w)
        ldv = (f"dy{par}", sch.val(f"dy{par}"))
        wadd = [ldv, ld_x] if p < 2 else [ldv]
        if p > 0:
            sch.drain("ve")
        sch.op("ve", e_tt(x_tm.ap().rearrange("p s f -> p (s f)"),
                          ypb[:, par, :, :].rearrange("p s f -> p (s f)"),
                          x_tm.ap().rearrange("p s f -> p (s f)"), ALU.add),
               wait=wadd, inc="ve")
        add_ve[p] = sch.val("ve")
    sum_ve = sch.val("ve")
    stats_ve = emit_ln_batched(sch, b8c, x_tm, xbf[:], sum_ve, sch.val("se"))
    ptr_evac = {0: None, 1: None}
    for s in range(NSLOT):
        xs = x_tm[:, s, :]
        wap = [("ve", stats_ve)] if s == 0 else [("se", sch.val("se"))]
        sch.op("ve", e_ts(xs, xs, b8c["rstd8"][:, s:s + 1], b8c["nmr8"][:, s:s + 1],
                          ALU.mult, ALU.add), wait=wap, inc="ve")
        ln_ve = sch.val("ve")
        wbf = [("ve", ln_ve), ("se", sch.val("se"))]
        if s > 0:
            wbf.append(("te", sch.val("te")))
        sch.op("se", e_act(xbf[:], xs, AF.Identity), wait=wbf, inc="se")
        bf_se = sch.val("se")
        ptr_evac = emit_transpose_out(
            sch, None, s, lambda ft: xbf[:, ft * 128:(ft + 1) * 128],
            x2T_o, PTR, ident, id_ready, x1ts, ptr_evac,
            extra_wait=("se", bf_se))
    done = sch.val("ve")
    sch.dma("sq", x2_o.rearrange("(s p) f -> p s f", p=128), x_tm[:], sem="so",
            wait=[("ve", done)])
    st = sch.val("so")
    sch.op("sq", lambda e, s: e.nop(), wait=[("so", st)])
    sch.build(nc, es)
    with es:
        pass
    return nc


def max_wait(a, b):
    if a is None:
        return b
    if b is None:
        return a
    if a[0] == b[0]:
        return (a[0], max(a[1], b[1]))
    return b



def build_C1b():
    nc = bass.Bass("TRN2", target_bir_lowering=False)
    x2T = nc.dram_tensor("x2T", [DM, SEQ], BF16, kind="ExternalInput").ap()
    wqkv = nc.dram_tensor("wqkv", [2, 3, DM, HD], F32, kind="ExternalInput").ap()
    qTh_o = nc.dram_tensor("qTh_o", [2, 128, SEQ], BF16, kind="ExternalOutput").ap()
    kTh_o = nc.dram_tensor("kTh_o", [2, 128, SEQ], BF16, kind="ExternalOutput").ap()
    vh_o = nc.dram_tensor("vh_o", [2, SEQ, HD], BF16, kind="ExternalOutput").ap()

    es = contextlib.ExitStack()
    sb = lambda name, shape, dt=BF16: es.enter_context(nc.sbuf_tensor(name, shape, dt))
    ps = lambda name, shape, dt=F32: es.enter_context(nc.psum_tensor(name, shape, dt))
    xb = sb("xb", [128, 2, FT, 1024])
    wsb = sb("wsb", [128, 2, 3, FT, HD])
    ost = sb("ost", [128, 4, 1024])
    vst = sb("vst", [128, 2, 8, HD])
    ident = sb("ident", [128, 128])
    iot = sb("iot", [128, 128], F32)
    pa = ps("pa", [128, 512])
    pb = ps("pb", [128, 512])
    PTR = [es.enter_context(nc.psum_tensor("ptr0", [128, 512], BF16)),
           es.enter_context(nc.psum_tensor("ptr1", [128, 512], BF16))]

    sch = Sch()
    sch.dma("pe", wsb[:], wqkv.rearrange("a b (t p) d -> p a b t d", p=128), sem="dw")
    ld_w = ("dw", sch.val("dw"))
    sch.op("pe", e_iota(iot[:], [[-1, 128]], 0, 1), inc="pe")
    sch.op("ve", e_ts(ident[:], iot[:], 0.0, None, ALU.is_equal),
           wait=[("pe", sch.val("pe"))], inc="ve")
    id_ready = sch.val("ve")
    xb_te = {}
    pp = [pa, pb]
    gidx = 0
    tr_ve = {0: None, 1: None}
    for tc in range(8):
        par = tc % 2
        w = [("te", xb_te[tc - 2])] if tc >= 2 else []
        sch.dma("sq", xb[:, par, :, :],
                x2T.rearrange("(t p) n -> p t n", p=128)[:, :, tc * 1024:(tc + 1) * 1024],
                sem=f"dx{par}", wait=w)
        ldx = (f"dx{par}", sch.val(f"dx{par}"))
        for hj in range(2):
            for mat in range(3):
                unit = tc * 6 + hj * 3 + mat
                slot = unit % 4
                for half in range(2):
                    bank = gidx % 2
                    wmm = [ldx, ld_w]
                    if gidx >= 2:
                        wmm.append(("se", gidx - 1))
                    for ft in range(FT):
                        sch.op("te", e_matmul(pp[bank][:], wsb[:, hj, mat, ft, :],
                                              xb[:, par, ft, half * 512:(half + 1) * 512],
                                              start=(ft == 0), stop=(ft == FT - 1)),
                               wait=wmm if ft == 0 else (),
                               inc="te" if ft == FT - 1 else None)
                    mmv = sch.val("te")
                    xb_te[tc] = mmv
                    wev = [("te", mmv)]
                    if half == 0 and sch.val("so") > 0:
                        wev.append(("so", sch.val("so")))
                    sch.op("se", e_act(ost[:, slot, half * 512:(half + 1) * 512],
                                       pp[bank][:], AF.Identity,
                                       scale=ISQ if mat == 0 else None),
                           wait=wev, inc="se")
                    gidx += 1
                ev = sch.val("se")
                if mat == 0:
                    sch.dma("sq", qTh_o[hj, :, tc * 1024:(tc + 1) * 1024],
                            ost[:, slot, :], sem="so", wait=[("se", ev)])
                elif mat == 1:
                    sch.dma("sq", kTh_o[hj, :, tc * 1024:(tc + 1) * 1024],
                            ost[:, slot, :], sem="so", wait=[("se", ev)])
                else:
                    for j4 in range(2):
                        ptrb = j4
                        wtr = [("se", ev), ("ve", id_ready)]
                        if tr_ve[ptrb] is not None:
                            wtr.append(tr_ve[ptrb])
                        for jj in range(4):
                            j = j4 * 4 + jj
                            sch.op("te", e_transpose(
                                PTR[ptrb][:, jj * 128:(jj + 1) * 128],
                                ost[:, slot, j * 128:(j + 1) * 128], ident[:]),
                                wait=wtr if jj == 0 else (),
                                inc="te" if jj == 3 else None)
                        trv = sch.val("te")
                        wev2 = [("te", trv)]
                        if sch.val("so") > 0:
                            wev2.append(("so", sch.val("so")))
                        sch.op("ve", e_copy(vst[:, j4, 0:4, :], PTR[ptrb][:]),
                               wait=wev2, inc="ve")
                        tr_ve[ptrb] = ("ve", sch.val("ve"))
                        sch.dma("sq",
                                vh_o[hj, tc * 1024 + j4 * 512: tc * 1024 + (j4 + 1) * 512, :]
                                .rearrange("(j p) d -> p j d", p=128),
                                vst[:, j4, 0:4, :], sem="so",
                                wait=[("ve", sch.val("ve"))])
    st = sch.val("so")
    sch.op("sq", lambda e, s: e.nop(), wait=[("so", st)])
    sch.build(nc, es)
    with es:
        pass
    return nc



def build_C2():
    nc = bass.Bass("TRN2", target_bir_lowering=False)
    x2 = nc.dram_tensor("x2", [1024, DM], F32, kind="ExternalInput").ap()
    qT = nc.dram_tensor("qT", [HEADS1, 128, 1024], BF16, kind="ExternalInput").ap()
    kf = nc.dram_tensor("kf", [HEADS1, 128, SEQ], BF16, kind="ExternalInput").ap()
    vf = nc.dram_tensor("vf", [HEADS1, SEQ, HD], BF16, kind="ExternalInput").ap()
    qpos = nc.dram_tensor("qpos", [128, NSLOT], F32, kind="ExternalInput").ap()
    wpT = nc.dram_tensor("wpT", [DM, DM], F32, kind="ExternalInput").ap()
    x3_o = nc.dram_tensor("x3_o", [1024, DM], F32, kind="ExternalOutput").ap()
    x3T_o = nc.dram_tensor("x3T_o", [DM, 1024], BF16, kind="ExternalOutput").ap()

    es = contextlib.ExitStack()
    sb = lambda name, shape, dt=BF16: es.enter_context(nc.sbuf_tensor(name, shape, dt))
    ps = lambda name, shape, dt=F32: es.enter_context(nc.psum_tensor(name, shape, dt))
    x_tm = sb("x_tm", [128, NSLOT, DM], F32)
    yq = sb("yq", [128, HEADS1, 1024])
    kg_sb = sb("kg_sb", [128, 2, SEQ])
    vg_sb = sb("vg_sb", [128, 2, 64, HD])
    p_st = sb("p_st", [128, SEQ])
    bvec = sb("bvec", [128, NSLOT], F32)
    tmask = sb("tmask", [128, NSLOT, 1024], U8)
    ytmp = sb("ytmp", [128, 2, 128])
    qpos_sb = sb("qpos_sb", [128, NSLOT], F32)
    s_sb = sb("s_sb", [128, 1024], F32)
    negm = sb("negm", [128, 2], F32)
    l_stv = sb("l_stv", [128, 16], F32)
    l_tot = sb("l_tot", [128, 2], F32)
    rl = sb("rl", [128, 2], F32)
    pt_sb = sb("pt_sb", [128, 2, 1024])
    ident = sb("ident", [128, 128])
    iot = sb("iot", [128, 128], F32)
    wp_sb = sb("wp_sb", [128, FT, 256])
    b8c2 = ln8_bufs(sb)
    x1ts = sb("x1ts", [128, FT, 128])
    SP = [ps("sp0", [128, 1024]), ps("sp1", [128, 1024])]
    PTR = [es.enter_context(nc.psum_tensor("ptr0", [128, 1024], BF16)),
           es.enter_context(nc.psum_tensor("ptr1", [128, 1024], BF16))]
    PVT = [ps("pv0", [128, 512]), ps("pv1", [128, 512])]
    PV = [PVT[0][:, 0:128], PVT[1][:, 0:128]]
    PJ = [SP[0][:, 0:512], SP[1][:, 0:512]]
    PTR2_VIEW = [PTR[0][:, 0:512], PTR[1][:, 0:512]]

    sch = Sch()
    sch.dma("pe", x_tm[:], x2.rearrange("(s p) f -> p s f", p=128), sem="dinp")
    sch.dma("sq", yq[:], qT.rearrange("h p n -> p h n"), sem="dinq")
    sch.dma("sq", qpos_sb[:], qpos[:], sem="dinq")
    ld_p = ("dinp", sch.val("dinp"))
    ld_q = ("dinq", sch.val("dinq"))
    sch.op("pe", e_iota(iot[:], [[-1, 128]], 0, 1), inc="pe")
    sch.op("ve", e_ts(ident[:], iot[:], 0.0, None, ALU.is_equal),
           wait=[("pe", sch.val("pe"))], inc="ve")
    id_ready = sch.val("ve")
    jvec = s_sb
    sch.op("pe", e_iota(jvec[:], [[1, 1024]], 0, 0), inc="pe")
    jv_pe = sch.val("pe")
    for s in range(NSLOT):
        sch.op("ve", e_ts(bvec[:, s:s + 1], qpos_sb[:, s:s + 1], 1.0,
                          float(-1024 * (CAPS[s] - 1)), ALU.mult, ALU.add),
               wait=[ld_q, ("pe", jv_pe)], inc="ve")
    sch.drain("ve")
    for s in range(NSLOT):
        sch.op("ve", e_ts(tmask[:, s, :], jvec[:], bvec[:, s:s + 1], None, ALU.is_gt),
               inc="ve")
    sch.drain("ve")
    jv_ready = sch.val("ve")

    cfg = ("l1", HEADS1, lambda h: [h], kf, vf, kg_sb, vg_sb,
           yq, p_st, tmask, jv_ready, s_sb, negm, l_stv, l_tot,
           rl, pt_sb, ident, id_ready, SP, PTR, PV, None, yq, ld_q, ld_p, ytmp)
    emit_attention(sch, cfg)
    attn_se = sch.val("se")

    pj_reader = {0: None, 1: None}
    wp_mm = None
    for d8 in range(8):
        wld = [("te", wp_mm)] if wp_mm is not None else []
        sch.dma("pe", wp_sb[:],
                wpT.rearrange("(t p) f -> p t f", p=128)[:, :, d8 * 256:(d8 + 1) * 256],
                sem="dwp", wait=wld)
        ld_wp = ("dwp", sch.val("dwp"))
        for s in range(NSLOT):
            bank = (d8 * 8 + s) % 2
            wmm = [ld_wp, ("se", attn_se)]
            if pj_reader[bank] is not None:
                wmm.append(pj_reader[bank])
            for ct in range(FT):
                sch.op("te", e_matmul(PJ[bank][:, 0:256], yq[:, ct, s * 128:(s + 1) * 128],
                                      wp_sb[:, ct, :],
                                      start=(ct == 0), stop=(ct == FT - 1)),
                       wait=wmm if ct == 0 else (), inc="te" if ct == FT - 1 else None)
            wp_mm = sch.val("te")
            sch.op("ve", e_tt(x_tm[:, s, d8 * 256:(d8 + 1) * 256], PJ[bank][:, 0:256],
                              x_tm[:, s, d8 * 256:(d8 + 1) * 256], ALU.add),
                   wait=[("te", wp_mm), ld_p], inc="ve")
            pj_reader[bank] = ("ve", sch.val("ve"))
    proj_ve = sch.val("ve")

    xbf = pt_sb.ap().rearrange("p a b -> p (a b)")
    stats_ve = emit_ln_batched(sch, b8c2, x_tm, xbf, proj_ve, sch.val("se"))
    ptr_evac2 = {0: None, 1: None}
    for s in range(NSLOT):
        xs = x_tm[:, s, :]
        wap = [("ve", stats_ve)] if s == 0 else [("se", sch.val("se"))]
        sch.op("ve", e_ts(xs, xs, b8c2["rstd8"][:, s:s + 1], b8c2["nmr8"][:, s:s + 1],
                          ALU.mult, ALU.add), wait=wap, inc="ve")
        ln_ve = sch.val("ve")
        wbf = [("ve", ln_ve), ("se", sch.val("se"))]
        if s > 0:
            wbf.append(("te", sch.val("te")))
        sch.op("se", e_act(xbf, xs, AF.Identity), wait=wbf, inc="se")
        bf_se = sch.val("se")
        ptr_evac2 = emit_transpose_out(
            sch, None, s, lambda ft: xbf[:, ft * 128:(ft + 1) * 128],
            x3T_o, PTR2_VIEW, ident, id_ready, x1ts, ptr_evac2,
            extra_wait=("se", bf_se))
    done = sch.val("ve")
    sch.dma("sq", x3_o.rearrange("(s p) f -> p s f", p=128), x_tm[:], sem="so",
            wait=[("ve", done)])
    st = sch.val("so")
    sch.op("sq", lambda e, s: e.nop(), wait=[("so", st)])
    sch.build(nc, es)
    with es:
        pass
    return nc



def build_D():
    nc = bass.Bass("TRN2", target_bir_lowering=False)
    x3 = nc.dram_tensor("x3", [1024, DM], F32, kind="ExternalInput").ap()
    x3T = nc.dram_tensor("x3T", [DM, 1024], BF16, kind="ExternalInput").ap()
    qw1 = nc.dram_tensor("qw1", [DM, FFN_H], BF16, kind="ExternalInput").ap()
    qw2 = nc.dram_tensor("qw2", [DM, FFN_H], BF16, kind="ExternalInput").ap()
    qw3 = nc.dram_tensor("qw3", [FFN_H, DM], BF16, kind="ExternalInput").ap()
    out_o = nc.dram_tensor("out_o", [1024, DM], F32, kind="ExternalOutput").ap()

    es = contextlib.ExitStack()
    sb = lambda name, shape, dt=BF16: es.enter_context(nc.sbuf_tensor(name, shape, dt))
    ps = lambda name, shape, dt=F32: es.enter_context(nc.psum_tensor(name, shape, dt))
    x_tm = sb("x_tm", [128, NSLOT, DM], F32)
    pool = sb("pool", [128, 4, FT, 512])
    hidT = sb("hidT", [128, 44, 512])
    xh = sb("xh", [128, FT, 512])
    sgst = sb("sgst", [128, DM])
    b8d = ln8_bufs(sb)
    P1 = [ps("p10", [128, 512]), ps("p11", [128, 512])]
    P2 = [ps("p20", [128, 512]), ps("p21", [128, 512])]
    PD = [ps("pd0", [128, 512]), ps("pd1", [128, 512])]

    qw3c = pool.ap().rearrange("p a b c -> p (a b c)")[:, 0:44 * 512]\
        .rearrange("p (m j) -> p m j", j=512)

    sch = Sch()
    sch.dma("pe", x_tm[:], x3.rearrange("(s p) f -> p s f", p=128), sem="dx")
    ld_x = ("dx", sch.val("dx"))
    p1_reader = {0: None, 1: None}
    p2_reader = {0: None, 1: None}
    pd_reader = {0: None, 1: None}
    gu_mm = {0: None, 1: None}
    down_mm = None
    xh_mm = None
    for th in range(2):
        wxh = [("te", xh_mm)] if xh_mm is not None else []
        sch.dma("sq", xh[:], x3T.rearrange("(t p) n -> p t n", p=128)[:, :, th * 512:(th + 1) * 512],
                sem="dxh", wait=wxh)
        ld_xh = ("dxh", sch.val("dxh"))
        for hc in range(11):
            par = hc % 2
            wld = []
            if gu_mm[par] is not None:
                wld.append(("te", gu_mm[par]))
            if hc == 0 and down_mm is not None:
                wld.append(("te", down_mm))
            sch.dma("pe", pool[:, 2 * par + 0, :, :],
                    qw1.rearrange("(t p) m -> p t m", p=128)[:, :, hc * 512:(hc + 1) * 512],
                    sem=f"dg{par}", wait=wld)
            sch.dma("pe", pool[:, 2 * par + 1, :, :],
                    qw2.rearrange("(t p) m -> p t m", p=128)[:, :, hc * 512:(hc + 1) * 512],
                    sem=f"dg{par}", wait=())
            ldg = (f"dg{par}", sch.val(f"dg{par}"))
            for mt in range(4):
                b = (hc * 4 + mt) % 2
                wmm = [ldg, ld_xh]
                if p1_reader[b] is not None:
                    wmm.append(p1_reader[b])
                for ft in range(FT):
                    sch.op("te", e_matmul(P1[b][:], pool[:, 2 * par + 0, ft, mt * 128:(mt + 1) * 128],
                                          xh[:, ft, :],
                                          start=(ft == 0), stop=(ft == FT - 1)),
                           wait=wmm if ft == 0 else (), inc="te" if ft == FT - 1 else None)
                g1 = sch.val("te")
                wmm2 = []
                if p2_reader[b] is not None:
                    wmm2.append(p2_reader[b])
                for ft in range(FT):
                    sch.op("te", e_matmul(P2[b][:], pool[:, 2 * par + 1, ft, mt * 128:(mt + 1) * 128],
                                          xh[:, ft, :],
                                          start=(ft == 0), stop=(ft == FT - 1)),
                           wait=wmm2 if ft == 0 else (), inc="te" if ft == FT - 1 else None)
                g2 = sch.val("te")
                gu_mm[par] = g2
                xh_mm = g2
                sch.op("se", e_act(sgst[:, 0:512], P1[b][:], AF.Sigmoid),
                       wait=[("te", g1), ("ve", sch.val("ve"))], inc="se")
                sv = sch.val("se")
                p1_reader[b] = ("se", sv)
                sch.op("ve", e_tt(hidT[:, hc * 4 + mt, :], sgst[:, 0:512], P2[b][:], ALU.mult),
                       wait=[("se", sv), ("te", g2)], inc="ve")
                p2_reader[b] = ("ve", sch.val("ve"))
        hid_ve = sch.val("ve")
        for dc in range(4):
            wld = [("ve", hid_ve)]
            if down_mm is not None:
                wld.append(("te", down_mm))
            if dc == 0:
                wld.append(("te", gu_mm[0] or 0))
                wld.append(("te", gu_mm[1] or 0))
            q3r = qw3.rearrange("(m p) f -> p m f", p=128)
            sch.dma("pe", qw3c[:, 0:22, :], q3r[:, 0:22, dc * 512:(dc + 1) * 512],
                    sem="dq3a", wait=wld)
            ld3a = ("dq3a", sch.val("dq3a"))
            sch.dma("pe", qw3c[:, 22:44, :], q3r[:, 22:44, dc * 512:(dc + 1) * 512],
                    sem="dq3b", wait=())
            ld3b = ("dq3b", sch.val("dq3b"))
            for t4 in range(4):
                s = th * 4 + t4
                b = (dc * 4 + t4) % 2
                wmm = [ld3a, ("ve", hid_ve)]
                if pd_reader[b] is not None:
                    wmm.append(pd_reader[b])
                for mt in range(44):
                    w_ = wmm if mt == 0 else ([ld3b] if mt == 22 and t4 == 0 else ())
                    sch.op("te", e_matmul(PD[b][:], hidT[:, mt, t4 * 128:(t4 + 1) * 128],
                                          qw3c[:, mt, :],
                                          start=(mt == 0), stop=(mt == 43)),
                           wait=w_, inc="te" if mt == 43 else None)
                down_mm = sch.val("te")
                sch.op("ve", e_tt(x_tm[:, s, dc * 512:(dc + 1) * 512], PD[b][:],
                                  x_tm[:, s, dc * 512:(dc + 1) * 512], ALU.add),
                       wait=[("te", down_mm), ld_x], inc="ve")
                pd_reader[b] = ("ve", sch.val("ve"))
    ffn_ve = sch.val("ve")
    stats_ve = emit_ln_batched(sch, b8d, x_tm, sgst.ap().rearrange("p n -> p n"),
                               ffn_ve, sch.val("se"))
    for s in range(NSLOT):
        sch.op("ve", e_ts(x_tm[:, s, :], x_tm[:, s, :], b8d["rstd8"][:, s:s + 1],
                          b8d["nmr8"][:, s:s + 1], ALU.mult, ALU.add),
               wait=[("ve", stats_ve)] if s == 0 else (), inc="ve")
    done = sch.val("ve")
    sch.dma("sq", out_o.rearrange("(s p) f -> p s f", p=128), x_tm[:], sem="so",
            wait=[("ve", done)])
    st = sch.val("so")
    sch.op("sq", lambda e, s: e.nop(), wait=[("so", st)])
    sch.build(nc, es)
    with es:
        pass
    return nc



def _bf(x):
    return np.ascontiguousarray(x, dtype=BF16NP)


def _f32(x):
    return np.ascontiguousarray(x, dtype=np.float32)


def _qpos(c):
    qp = np.zeros((128, NSLOT), dtype=np.float32)
    for s in range(NSLOT):
        qp[:, s] = gtile(c, s) * 128 + np.arange(128)
    return qp


def kernel(L0_attn_x, L0_attn_Wq, L0_attn_bq, L0_attn_Wk, L0_attn_bk,
           L0_attn_Wv, L0_attn_bv, L0_ffn_gate_w, L0_ffn_w_gate, L0_ffn_w_up,
           L0_ffn_w_down, L1_attn_W_qkv, L1_attn_W_proj, L1_ffn_w1, L1_ffn_w2,
           L1_ffn_w3):
    cores = list(range(NCORE))
    x = _f32(np.asarray(L0_attn_x)[0])
    xT = _f32(np.asarray(x).T)
    own_idx = [own_token_index(c) for c in range(NCORE)]

    w1T = _f32(np.asarray(L1_ffn_w1).T)
    w2T = _f32(np.asarray(L1_ffn_w2).T)
    w3T = _f32(np.asarray(L1_ffn_w3).T)
    w1s = w1T.reshape(NCORE, 128, 11264)
    w2s = w2T.reshape(NCORE, 128, 11264)
    w3s = w3T.reshape(NCORE, 128, 11264)
    wq = _f32(L0_attn_Wq)
    wk = _f32(L0_attn_Wk)
    wv = _f32(L0_attn_Wv)
    bqT = _f32(np.asarray(L0_attn_bq).T.reshape(128, HEADS0))
    bkT = _f32(np.asarray(L0_attn_bk).T.reshape(128, GRP0))
    bvT = _f32(np.asarray(L0_attn_bv).T.reshape(128, GRP0))
    nc = build_A1()
    res = run_bass_kernel_spmd(nc, [{
        "xT": _f32(xT[:, own_idx[c]]),
        "wq": wq, "bqT": bqT, "wk": wk, "bkT": bkT, "wv": wv, "bvT": bvT,
        "w1s": w1s[c], "w2s": w2s[c], "w3s": w3s[c],
    } for c in cores], core_ids=cores)
    a1 = [dict(r) for r in res.results]

    kf0 = np.zeros((GRP0, 128, SEQ), dtype=BF16NP)
    vf0 = np.zeros((GRP0, SEQ, HD), dtype=BF16NP)
    for c in cores:
        kf0[:, :, own_idx[c]] = a1[c]["kT_o"]
        vf0[:, own_idx[c], :] = a1[c]["v_o"]
    gwT = _f32(np.asarray(L0_ffn_gate_w).T)
    nc = build_A2()
    res = run_bass_kernel_spmd(nc, [{
        "x_own": _f32(x[own_idx[c]]),
        "qT": a1[c]["qT_o"], "kf": kf0, "vf": vf0,
        "qpos": _qpos(c), "gwT": gwT,
    } for c in cores], core_ids=cores)
    a2 = [dict(r) for r in res.results]

    x1T_full = np.zeros((DM, SEQ), dtype=BF16NP)
    cwf = np.zeros((SEQ, MOE_E), dtype=np.float32)
    for c in cores:
        x1T_full[:, own_idx[c]] = a2[c]["x1T_o"]
        cwf[own_idx[c]] = a2[c]["cw_o"]
    wsums_all = _f32(np.stack([a1[c]["wsums_o"][0] for c in cores]))
    wgt = np.asarray(L0_ffn_w_gate)
    wut = np.asarray(L0_ffn_w_up)
    wdt = np.asarray(L0_ffn_w_down)
    nc = build_B()
    res = run_bass_kernel_spmd(nc, [{
        "x1T": x1T_full,
        "cwe_f": _f32(cwf[:, [2 * c, 2 * c + 1]]),
        "wg": _f32(wgt[2 * c:2 * c + 2].transpose(0, 2, 1)),
        "wu": _f32(wut[2 * c:2 * c + 2].transpose(0, 2, 1)),
        "wdn": _f32(wdt[2 * c:2 * c + 2].transpose(0, 2, 1)),
        "wsums": wsums_all,
        "w1s": w1s[c], "w2s": w2s[c], "w3s": w3s[c],
    } for c in cores], core_ids=cores)
    b = [dict(r) for r in res.results]

    nc = build_C1a()
    res = run_bass_kernel_spmd(nc, [{
        "x1": a2[c]["x1_o"],
        "yps": np.ascontiguousarray(
            np.stack([b[p]["yp_o"][own_idx[c]] for p in cores])),
    } for c in cores], core_ids=cores)
    c1a = [dict(r) for r in res.results]

    x2T_full = np.zeros((DM, SEQ), dtype=BF16NP)
    for c in cores:
        x2T_full[:, own_idx[c]] = c1a[c]["x2T_o"]
    Wqkv = np.asarray(L1_attn_W_qkv)
    wqkv_sl = []
    for c in cores:
        per = np.zeros((2, 3, DM, HD), dtype=np.float32)
        for j in range(2):
            h = 2 * c + j
            for m in range(3):
                per[j, m] = np.asarray(Wqkv[m * DM + h * HD:m * DM + (h + 1) * HD, :].T)
        wqkv_sl.append(_f32(per))
    nc = build_C1b()
    res = run_bass_kernel_spmd(nc, [{
        "x2T": x2T_full, "wqkv": wqkv_sl[c],
    } for c in cores], core_ids=cores)
    c1b = [dict(r) for r in res.results]

    kf1 = np.concatenate([c1b[c]["kTh_o"] for c in cores], 0)
    vf1 = np.concatenate([c1b[c]["vh_o"] for c in cores], 0)
    qf1 = np.concatenate([c1b[c]["qTh_o"] for c in cores], 0)
    wpT = _f32(np.asarray(L1_attn_W_proj).T)
    nc = build_C2()
    res = run_bass_kernel_spmd(nc, [{
        "x2": c1a[c]["x2_o"],
        "qT": np.ascontiguousarray(qf1[:, :, own_idx[c]]),
        "kf": kf1, "vf": vf1, "qpos": _qpos(c), "wpT": wpT,
    } for c in cores], core_ids=cores)
    c2 = [dict(r) for r in res.results]

    qw1_full = np.concatenate([b[c]["qw1_o"].reshape(256, FFN_H) for c in cores], 0)
    qw2_full = np.concatenate([b[c]["qw2_o"].reshape(256, FFN_H) for c in cores], 0)
    qw3_full = np.concatenate([b[c]["qw3_o"].reshape(704, DM) for c in cores], 0)
    nc = build_D()
    res = run_bass_kernel_spmd(nc, [{
        "x3": c2[c]["x3_o"], "x3T": c2[c]["x3T_o"],
        "qw1": qw1_full, "qw2": qw2_full, "qw3": qw3_full,
    } for c in cores], core_ids=cores)
    d = [dict(r) for r in res.results]

    out = np.zeros((SEQ, DM), dtype=np.float32)
    for c in cores:
        out[own_idx[c]] = d[c]["out_o"]
    return out.reshape(1, SEQ, DM)


# version 10
# speedup vs baseline: 1.0061x; 1.0061x over previous
import math
import contextlib
from collections import defaultdict

import numpy as np
import ml_dtypes

import concourse.bass as bass
import concourse.mybir as mybir
from concourse.bass_utils import run_bass_kernel_spmd
from concourse.alu_op_type import AluOpType as ALU

F32 = mybir.dt.float32
BF16 = mybir.dt.bfloat16
U8 = mybir.dt.uint8
AF = mybir.ActivationFunctionType
AX = mybir.AxisListType

NCORE = 8
SEQ = 8192
DM = 2048
NTILE = SEQ // 128
NSLOT = 8
FT = DM // 128
HEADS0, GRP0, HD = 16, 4, 128
HEADS1 = 16
MOE_E, MOE_D, MOE_K = 16, 1408, 6
FFN_H = 5632
ISQ = 1.0 / math.sqrt(HD)
CAPS = [s // 2 * 2 + 1 + (s % 2) for s in range(NSLOT)]
BF16NP = ml_dtypes.bfloat16


def gtile(c, s):
    m = s // 2
    return 16 * m + (c if s % 2 == 0 else 15 - c)


def own_token_index(c):
    idx = []
    for s in range(NSLOT):
        g = gtile(c, s)
        idx.extend(range(g * 128, (g + 1) * 128))
    return np.array(idx, dtype=np.int64)


ENGINES = ("te", "ve", "se", "pe", "sq")


class Sch:
    def __init__(self):
        self.q = {e: [] for e in ENGINES}
        self.cnt = defaultdict(int)
        self.cnt["te"] = self.cnt["ve"] = self.cnt["se"] = self.cnt["pe"] = 0

    def op(self, eng, emit, wait=(), inc=None, inc_n=1):
        item = {"emit": emit, "wait": tuple(wait), "inc": inc, "inc_n": inc_n}
        self.q[eng].append(item)
        if inc is not None:
            self.cnt[inc] += inc_n
        return self.cnt[inc] if inc is not None else None

    def val(self, sem):
        return self.cnt[sem]

    def dma(self, queue, out, in_, sem, wait=(), **kw):
        def emit(eng, sems):
            return eng.dma_start(out=out, in_=in_, **kw)
        return self.op(queue, emit, wait=wait, inc=sem, inc_n=16)

    def drain(self, eng):
        self.op(eng, lambda e, s: e.drain())

    def build(self, nc, es):
        sems = {name: es.enter_context(nc.semaphore(f"sem_{name}"))
                for name in self.cnt.keys()}
        block = es.enter_context(nc.Block())
        engobjs = {"te": nc.tensor, "ve": nc.vector, "se": nc.scalar,
                   "pe": nc.gpsimd, "sq": nc.sync}

        def section(eng):
            def body(engine):
                for item in self.q[eng]:
                    for (s, v) in item["wait"]:
                        engine.wait_ge(sems[s], v)
                    instr = item["emit"](engine, sems)
                    if item["inc"] is not None:
                        assert instr is not None, f"inc on op with no instruction ({eng})"
                        instr.then_inc(sems[item["inc"]], item["inc_n"])
            return body

        for eng in ENGINES:
            if self.q[eng]:
                getattr(block, {"te": "tensor", "ve": "vector", "se": "scalar",
                                "pe": "gpsimd", "sq": "sync"}[eng])(section(eng))



def e_matmul(out, lhsT, rhs, start, stop):
    return lambda e, s: e.matmul(out, lhsT, rhs, start=start, stop=stop)


def e_transpose(out, in_, ident):
    return lambda e, s: e.transpose(out, in_, ident)


def e_act(out, in_, func, bias=None, scale=None, accum_out=None):
    def f(e, s):
        kw = {}
        if bias is not None:
            kw["bias"] = bias
        if scale is not None:
            kw["scale"] = scale
        if accum_out is not None:
            kw["accum_out"] = accum_out
        return e.activation(out=out, in_=in_, func=func, **kw)
    return f


def e_ts(out, in0, s1, s2=None, op0=ALU.mult, op1=None):
    def f(e, s):
        kw = dict(out=out, in0=in0, scalar1=s1, scalar2=s2, op0=op0)
        if op1 is not None:
            kw["op1"] = op1
        return e.tensor_scalar(**kw)
    return f


def e_tt(out, in0, in1, op):
    return lambda e, s: e.tensor_tensor(out=out, in0=in0, in1=in1, op=op)


def e_stt(out, in0, scalar, in1, op0, op1):
    return lambda e, s: e.scalar_tensor_tensor(out=out, in0=in0, scalar=scalar,
                                               in1=in1, op0=op0, op1=op1)


def e_red(out, in_, axis, op="sum", neg=False, absval=False):
    def f(e, s):
        if op == "sum":
            return e.reduce_sum(out=out, in_=in_, axis=axis, negate=neg,
                                apply_absolute_value=absval)
        if op == "max":
            return e.reduce_max(out=out, in_=in_, axis=axis, negate=neg,
                                apply_absolute_value=absval)
        return e.tensor_reduce(out=out, in_=in_, axis=axis, op=op)
    return f


def e_copy(out, in_):
    def f(e, s):
        if hasattr(e, "tensor_copy"):
            return e.tensor_copy(out=out, in_=in_)
        return e.copy(out=out, in_=in_)
    return f


def e_memset(ap, v):
    return lambda e, s: e.memset(ap, v)


def e_recip(out, in_):
    return lambda e, s: e.reciprocal(out=out, in_=in_)


def e_iota(out, pattern, base=0, cm=0):
    return lambda e, s: e.iota(out, pattern=pattern, base=base, channel_multiplier=cm,
                               allow_small_or_imprecise_dtypes=True)



def build_A1():
    nc = bass.Bass("TRN2", target_bir_lowering=False)
    xT = nc.dram_tensor("xT", [DM, 1024], F32, kind="ExternalInput").ap()
    wq = nc.dram_tensor("wq", [HEADS0, DM, HD], F32, kind="ExternalInput").ap()
    bqT = nc.dram_tensor("bqT", [128, HEADS0], F32, kind="ExternalInput").ap()
    wk = nc.dram_tensor("wk", [GRP0, DM, HD], F32, kind="ExternalInput").ap()
    bkT = nc.dram_tensor("bkT", [128, GRP0], F32, kind="ExternalInput").ap()
    wv = nc.dram_tensor("wv", [GRP0, DM, HD], F32, kind="ExternalInput").ap()
    bvT = nc.dram_tensor("bvT", [128, GRP0], F32, kind="ExternalInput").ap()
    wsl = [nc.dram_tensor(f"w{j}s", [128, 11264], F32, kind="ExternalInput").ap()
           for j in (1, 2, 3)]
    qT_o = nc.dram_tensor("qT_o", [HEADS0, 128, 1024], BF16, kind="ExternalOutput").ap()
    kT_o = nc.dram_tensor("kT_o", [GRP0, 128, 1024], BF16, kind="ExternalOutput").ap()
    v_o = nc.dram_tensor("v_o", [GRP0, 1024, HD], BF16, kind="ExternalOutput").ap()
    wsums_o = nc.dram_tensor("wsums_o", [1, 4], F32, kind="ExternalOutput").ap()

    es = contextlib.ExitStack()
    sb = lambda name, shape, dt=BF16: es.enter_context(nc.sbuf_tensor(name, shape, dt))
    ps = lambda name, shape, dt=F32: es.enter_context(nc.psum_tensor(name, shape, dt))

    xfm = sb("xfm", [128, FT, 1024])
    wst = sb("wst", [128, 2, FT, HD])
    bq_sb = sb("bq_sb", [128, HEADS0], F32)
    bqs_sb = sb("bqs_sb", [128, HEADS0], F32)
    bk_sb = sb("bk_sb", [128, GRP0], F32)
    bv_sb = sb("bv_sb", [128, GRP0], F32)
    ostage = sb("ostage", [128, 4, 1024])
    vstage = sb("vstage", [128, 2, 8, HD])
    ident = sb("ident", [128, 128])
    iot = sb("iot", [128, 128], F32)
    wbuf = sb("wbuf", [128, 2, 2816], F32)
    red1 = sb("red1", [128, 22], F32)
    redc = sb("redc", [128, 12], F32)
    red2 = sb("red2", [128, 1], F32)
    ones_col = sb("ones_col", [128, 1], F32)
    wsums_sb = sb("wsums_sb", [1, 4], F32)
    pa = ps("pa", [128, 512])
    pb = ps("pb", [128, 512])
    ptr = es.enter_context(nc.psum_tensor("ptr", [128, 512], BF16))
    ptr2 = es.enter_context(nc.psum_tensor("ptr2", [128, 512], BF16))
    pss = ps("pss", [1, 4])

    sch = Sch()
    sch.dma("pe", xfm[:], xT.rearrange("(t p) n -> p t n", p=128), sem="dx")
    ld_x = ("dx", sch.val("dx"))
    sch.dma("sq", bq_sb[:], bqT[:], sem="db")
    sch.dma("sq", bk_sb[:], bkT[:], sem="db")
    sch.dma("sq", bv_sb[:], bvT[:], sem="db")
    ld_b = ("db", sch.val("db"))
    sch.op("pe", e_iota(iot[:], [[-1, 128]], 0, 1), inc="pe")
    sch.op("ve", e_ts(ident[:], iot[:], 0.0, None, ALU.is_equal),
           wait=[("pe", sch.val("pe"))], inc="ve")
    id_ready = sch.val("ve")
    sch.op("ve", e_ts(bqs_sb[:], bq_sb[:], ISQ), wait=[ld_b], inc="ve")
    bq_ready = sch.val("ve")

    units = ([("q", h) for h in range(HEADS0)] + [("k", g) for g in range(GRP0)]
             + [("v", g) for g in range(GRP0)])
    pp = [pa, pb]
    wld = {}
    st_of_stage = {}
    tr_of_vstage = {}
    ev_mm = {}
    se_evacs = 0
    ve_evacs = 0

    def load_unit(u, wait):
        kind, idx = units[u]
        srct = {"q": wq, "k": wk, "v": wv}[kind]
        sem = f"dw{u % 2}"
        sch.dma("pe", wst[:, u % 2, :, :],
                srct.rearrange("h (t p) d -> p h t d", p=128)[:, idx, :, :],
                sem=sem, wait=wait)
        wld[u] = (sem, sch.val(sem))

    load_unit(0, ())
    load_unit(1, ())
    for u, (kind, idx) in enumerate(units):
        par = u % 2
        for half in range(2):
            gidx = u * 2 + half
            bank = gidx % 2
            w = [wld[u]]
            if u == 0 and half == 0:
                w.append(ld_x)
            if gidx >= 2:
                w.append(("se", gidx - 1))
            for ft in range(FT):
                sch.op("te", e_matmul(pp[bank][:], wst[:, par, ft, :],
                                      xfm[:, ft, half * 512:(half + 1) * 512],
                                      start=(ft == 0), stop=(ft == FT - 1)),
                       wait=w if ft == 0 else (),
                       inc="te" if ft == FT - 1 else None)
            mmv = sch.val("te")
            if half == 1:
                ev_mm[u] = mmv
                if u + 2 < len(units):
                    load_unit(u + 2, [("te", mmv)])
            slot = u % 4
            wev = [("te", mmv), ld_b]
            if kind == "q":
                wev.append(("ve", bq_ready))
            if half == 0 and st_of_stage:
                wev.append(("so", sch.val("so")))
            bias_t = {"q": bqs_sb, "k": bk_sb, "v": bv_sb}[kind]
            bias = bias_t[:, idx:idx + 1]
            scale = ISQ if kind == "q" else None
            sch.op("se", e_act(ostage[:, slot, half * 512:(half + 1) * 512],
                               pp[bank][:], AF.Identity, bias=bias, scale=scale),
                   inc="se", wait=wev)
            se_evacs = sch.val("se")
        if kind == "q":
            sch.dma("sq", qT_o[idx, :, :], ostage[:, u % 4, :], sem="so",
                    wait=[("se", se_evacs)])
            st_of_stage[u % 4] = sch.val("so")
        elif kind == "k":
            sch.dma("sq", kT_o[idx, :, :], ostage[:, u % 4, :], sem="so",
                    wait=[("se", se_evacs)])
            st_of_stage[u % 4] = sch.val("so")
        else:
            for j4 in range(2):
                ptrb = [ptr, ptr2][j4]
                wtr = [("se", se_evacs), ("ve", id_ready)]
                prev_ev = tr_of_vstage.get(j4)
                if prev_ev is not None:
                    wtr.append(("ve", prev_ev))
                for jj in range(4):
                    j = j4 * 4 + jj
                    sch.op("te", e_transpose(ptrb[:, jj * 128:(jj + 1) * 128],
                                             ostage[:, u % 4, j * 128:(j + 1) * 128],
                                             ident[:]),
                           wait=wtr if jj == 0 else (),
                           inc="te" if jj == 3 else None)
                trv = sch.val("te")
                vpar = idx % 2
                wev2 = [("te", trv)]
                if st_of_stage:
                    wev2.append(("so", sch.val("so")))
                sch.op("ve", e_copy(vstage[:, vpar, j4 * 4:(j4 + 1) * 4, :], ptrb[:]),
                       wait=wev2, inc="ve")
                tr_of_vstage[j4] = sch.val("ve")
            ve_evacs = sch.val("ve")
            sch.dma("sq", v_o[idx, :, :].rearrange("(j p) d -> p j d", p=128),
                    vstage[:, idx % 2, :, :], sem="so", wait=[("ve", ve_evacs)])
            st_of_stage[f"v{idx % 2}"] = sch.val("so")

    sch.op("ve", e_memset(ones_col[:], 1.0), inc="ve")
    prev_red = None
    for j in range(3):
        for ch in range(4):
            k = j * 4 + ch
            par = k % 2
            w = []
            if k >= 2:
                w.append(("ve", prev_red2[k - 2]))
            sch.dma("sq", wbuf[:, par, :], wsl[j][:, ch * 2816:(ch + 1) * 2816],
                    sem=f"dc{par}", wait=w)
            ldv = (f"dc{par}", sch.val(f"dc{par}"))
            if k == 0:
                prev_red2 = {}
            sch.op("ve", e_red(red1[:], wbuf[:, par, :].rearrange("p (a b) -> p a b", b=128),
                               AX.X, "sum", absval=True), wait=[ldv], inc="ve")
            sch.drain("ve")
            sch.op("ve", e_red(redc[:, k:k + 1], red1[:], AX.X, "sum"), inc="ve")
            sch.drain("ve")
            prev_red2[k] = sch.val("ve")
    for j in range(3):
        sch.op("ve", e_red(red2[:], redc[:, j * 4:(j + 1) * 4], AX.X, "sum"),
               wait=[("te", sch.val("te"))] if j > 0 else (), inc="ve")
        sch.drain("ve")
        rv = sch.val("ve")
        sch.op("te", e_matmul(pss[0:1, j:j + 1], ones_col[:], red2[:],
                              start=True, stop=True),
               wait=[("ve", rv)], inc="te")
    sumv = sch.val("te")
    sch.op("ve", e_memset(wsums_sb[:], 0.0), inc="ve")
    sch.drain("ve")
    sch.op("ve", e_copy(wsums_sb[0:1, 0:3], pss[0:1, 0:3]), wait=[("te", sumv)], inc="ve")
    sch.dma("sq", wsums_o[:], wsums_sb[:], sem="so", wait=[("ve", sch.val("ve"))])
    st = sch.val("so")
    sch.op("sq", lambda e, s: e.nop(), wait=[("so", st)])

    sch.build(nc, es)
    with es:
        pass
    return nc



def emit_attention(sch, cfg):
    (mode, n_outer, heads_of, kf, vf, kg_sb, vg_sb, q_sb, p_st, tmask,
     tm_ready, s_sb, negm, l_st, l_tot, rl, pt_sb, ident, id_ready, SP,
     PTR, PV, x_tm, yq, ld_q, ld_x, ytmp) = cfg
    sp_reader = {0: None, 1: None}
    pv_reader = {0: None, 1: None}
    ptr_evac = {0: None, 1: None}
    pst_tr = {}
    lst_free = {0: None, 1: None}
    s_sb_free = None
    chunk_ctr = [0]
    unit_ctr = [0]
    kv_reads = [0]

    def make_pass2(u):
        cap, lpar, pvb, s, h, exp_se = (u["cap"], u["lpar"], u["pvb"], u["s"],
                                        u["h"], u["exp_se"])
        kpar = u["kpar"]
        sch.op("ve", e_red(l_tot[:, lpar:lpar + 1],
                           l_st[:, lpar * 8:lpar * 8 + cap], AX.X, "sum"),
               wait=[("se", exp_se[cap - 1])], inc="ve")
        sch.drain("ve")
        sch.op("ve", e_recip(rl[:, lpar:lpar + 1], l_tot[:, lpar:lpar + 1]),
               inc="ve")
        sch.drain("ve")
        rl_ve = sch.val("ve")
        tr_te = {}
        ev_ve = {}
        def emit_tr(ci):
            ptrb = ci % 2
            wtr = [("ve", id_ready), ("se", exp_se[ci])]
            if ptr_evac[ptrb] is not None:
                wtr.append(ptr_evac[ptrb])
            for j in range(8):
                sch.op("te", e_transpose(
                    PTR[ptrb][:, j * 128:(j + 1) * 128],
                    p_st[:, ci * 1024 + j * 128: ci * 1024 + (j + 1) * 128],
                    ident[:]),
                    wait=wtr if j == 0 else (), inc="te" if j == 7 else None)
            tr_te[ci] = sch.val("te")
            pst_tr[ci] = ("te", tr_te[ci])
            ev_eng = "ve" if ci % 2 == 0 else "se"
            sch.op(ev_eng, e_copy(pt_sb[:, ptrb, :], PTR[ptrb][:]),
                   wait=[("te", tr_te[ci])], inc=ev_eng)
            ev_ve[ci] = (ev_eng, sch.val(ev_eng))
        def emit_pv(ci):
            ptrb = ci % 2
            wpv = [ev_ve[ci]]
            if ci == 0 and pv_reader[pvb] is not None:
                wpv.append(pv_reader[pvb])
            for j in range(8):
                mm = e_matmul(PV[pvb], pt_sb[:, ptrb, j * 128:(j + 1) * 128],
                              vg_sb[:, kpar, ci * 8 + j, :],
                              start=(ci == 0 and j == 0),
                              stop=(ci == cap - 1 and j == 7))
                sch.op("te", mm, wait=wpv if j == 0 else (),
                       inc="te" if j == 7 else None)
            ptr_evac[ptrb] = ("te", sch.val("te"))
        emit_tr(0)
        u["tr_done"] = 0
        yield
        for ci in range(1, cap):
            emit_tr(ci)
            u["tr_done"] = ci
            emit_pv(ci - 1)
            yield
        emit_pv(cap - 1)
        u["tr_done"] = 100
        pv_done = sch.val("te")
        kv_reads[0] = pv_done
        if mode == "l0":
            sch.op("ve", e_stt(x_tm[:, s, h * 128:(h + 1) * 128], PV[pvb],
                               rl[:, lpar:lpar + 1],
                               x_tm[:, s, h * 128:(h + 1) * 128],
                               ALU.mult, ALU.add),
                   wait=[("te", pv_done), ("ve", rl_ve), ld_x], inc="ve")
            pv_reader[pvb] = ("ve", sch.val("ve"))
            lst_free[lpar] = ("ve", sch.val("ve"))
        else:
            sch.op("ve", e_ts(ytmp[:, lpar, :], PV[pvb], rl[:, lpar:lpar + 1]),
                   wait=[("te", pv_done), ("ve", rl_ve)], inc="ve")
            pv_reader[pvb] = ("ve", sch.val("ve"))
            ts_ve = sch.val("ve")
            ptrb = cap % 2
            wtr2 = [("ve", ts_ve), ("ve", id_ready)]
            if ptr_evac[ptrb] is not None:
                wtr2.append(ptr_evac[ptrb])
            sch.op("te", e_transpose(PTR[ptrb][:, 0:128], ytmp[:, lpar, :],
                                     ident[:]), wait=wtr2, inc="te")
            ytr = sch.val("te")
            sch.op("ve", e_copy(yq[:, h, s * 128:(s + 1) * 128],
                                PTR[ptrb][:, 0:128]),
                   wait=[("te", ytr)], inc="ve")
            ptr_evac[ptrb] = ("ve", sch.val("ve"))
            lst_free[lpar] = ("ve", sch.val("ve"))

    pending = None

    def advance_pass2(upto):
        if pending is None:
            return
        u, gen = pending
        while True:
            if upto is not None and u.get("tr_done", -1) >= min(upto, u["cap"] - 1):
                return
            try:
                next(gen)
            except StopIteration:
                return

    def flush_pass2():
        nonlocal pending
        if pending is not None:
            u, gen = pending
            for _ in gen:
                pass
            pending = None

    kv_te_of_par = {0: 0, 1: 0}
    kv_ld_of_par = {}
    def stage_kv(outer):
        par = outer % 2
        wkv = [("te", kv_te_of_par[par])] if outer >= 2 else []
        sch.dma("sq", kg_sb[:, par, :], kf[outer, :, :], sem=f"dkv{par}", wait=wkv)
        sch.dma("sq", vg_sb[:, par, :, :],
                vf[outer, :, :].rearrange("(j p) d -> p j d", p=128),
                sem=f"dkv{par}", wait=wkv)
        kv_ld_of_par[outer] = (f"dkv{par}", sch.val(f"dkv{par}"))
    stage_kv(0)
    if n_outer > 1:
        stage_kv(1)
    for outer in range(n_outer):
        kpar = outer % 2
        kv_ld = kv_ld_of_par[outer]
        for h in heads_of(outer):
            for s in range(NSLOT):
                cap = CAPS[s]
                pvb = unit_ctr[0] % 2
                lpar = unit_ctr[0] % 2
                ustate = {"cap": cap, "lpar": lpar, "pvb": pvb, "s": s, "h": h,
                          "kpar": kpar}
                exp_se = {}
                ustate["exp_se"] = exp_se
                rm_ve = None
                for ci in range(cap):
                    bank = chunk_ctr[0] % 2
                    masked = (ci == cap - 1)
                    w0 = [kv_ld, ld_q] if ci == 0 else []
                    if sp_reader[bank] is not None:
                        w0.append(sp_reader[bank])
                    for half in range(2):
                        sch.op("te", e_matmul(
                            SP[bank][:, half * 512:(half + 1) * 512],
                            q_sb[:, h, s * 128:(s + 1) * 128],
                            kg_sb[:, kpar, ci * 1024 + half * 512: ci * 1024 + (half + 1) * 512],
                            start=True, stop=True),
                            wait=w0 if half == 0 else (), inc="te")
                    mmv = sch.val("te")
                    advance_pass2(ci)
                    exp_in = SP[bank][:]
                    stt_ve = None
                    if masked:
                        wstt = [("te", mmv), ("ve", tm_ready)]
                        if s_sb_free is not None:
                            wstt.append(s_sb_free)
                        sch.op("ve", e_stt(s_sb[:], tmask[:, s, :], -1e30, SP[bank][:],
                                           ALU.mult, ALU.add), wait=wstt, inc="ve")
                        sch.drain("ve")
                        stt_ve = sch.val("ve")
                        sp_reader[bank] = ("ve", stt_ve)
                        exp_in = s_sb[:]
                    if ci == 0:
                        wrm = [] if masked else [("te", mmv)]
                        if lst_free[lpar] is not None:
                            wrm.append(lst_free[lpar])
                        sch.op("ve", e_red(negm[:, lpar:lpar + 1], exp_in, AX.X,
                                           "max", neg=True), wait=wrm, inc="ve")
                        sch.drain("ve")
                        rm_ve = sch.val("ve")
                    wex = [("ve", rm_ve)]
                    if masked:
                        wex.append(("ve", stt_ve))
                    else:
                        wex.append(("te", mmv))
                    if pst_tr.get(ci) is not None:
                        wex.append(pst_tr[ci])
                    if ci == 0 and lst_free[lpar] is not None:
                        wex.append(lst_free[lpar])
                    sch.op("se", e_act(p_st[:, ci * 1024:(ci + 1) * 1024], exp_in,
                                       AF.Exp, bias=negm[:, lpar:lpar + 1],
                                       accum_out=l_st[:, lpar * 8 + ci:lpar * 8 + ci + 1]),
                           wait=wex, inc="se")
                    exp_se[ci] = sch.val("se")
                    if masked:
                        s_sb_free = ("se", exp_se[ci])
                    else:
                        sp_reader[bank] = ("se", exp_se[ci])
                    chunk_ctr[0] += 1
                flush_pass2()
                pending = (ustate, make_pass2(ustate))
                unit_ctr[0] += 1
        kv_te_of_par[kpar] = None
        flush_pass2()
        kv_te_of_par[kpar] = kv_reads[0]
        if outer + 2 < n_outer:
            stage_kv(outer + 2)
    flush_pass2()
    return sch.val("te"), sch.val("ve"), sch.val("se")



def build_A2():
    nc = bass.Bass("TRN2", target_bir_lowering=False)
    x_own = nc.dram_tensor("x_own", [1024, DM], F32, kind="ExternalInput").ap()
    qT = nc.dram_tensor("qT", [HEADS0, 128, 1024], BF16, kind="ExternalInput").ap()
    kf = nc.dram_tensor("kf", [GRP0, 128, SEQ], BF16, kind="ExternalInput").ap()
    vf = nc.dram_tensor("vf", [GRP0, SEQ, HD], BF16, kind="ExternalInput").ap()
    qpos = nc.dram_tensor("qpos", [128, NSLOT], F32, kind="ExternalInput").ap()
    gwT = nc.dram_tensor("gwT", [DM, MOE_E], F32, kind="ExternalInput").ap()
    x1_o = nc.dram_tensor("x1_o", [1024, DM], F32, kind="ExternalOutput").ap()
    x1T_o = nc.dram_tensor("x1T_o", [DM, 1024], BF16, kind="ExternalOutput").ap()
    cw_o = nc.dram_tensor("cw_o", [1024, MOE_E], F32, kind="ExternalOutput").ap()

    es = contextlib.ExitStack()
    sb = lambda name, shape, dt=BF16: es.enter_context(nc.sbuf_tensor(name, shape, dt))
    ps = lambda name, shape, dt=F32: es.enter_context(nc.psum_tensor(name, shape, dt))

    x_tm = sb("x_tm", [128, NSLOT, DM], F32)
    q_sb = sb("q_sb", [128, HEADS0, 1024])
    kg_sb = sb("kg_sb", [128, 2, SEQ])
    vg_sb = sb("vg_sb", [128, 2, 64, HD])
    p_st = sb("p_st", [128, SEQ])
    bvec = sb("bvec", [128, NSLOT], F32)
    tmask = sb("tmask", [128, NSLOT, 1024], U8)
    qpos_sb = sb("qpos_sb", [128, NSLOT], F32)
    s_sb = sb("s_sb", [128, 1024], F32)
    negm = sb("negm", [128, 2], F32)
    l_stv = sb("l_stv", [128, 16], F32)
    l_tot = sb("l_tot", [128, 2], F32)
    rl = sb("rl", [128, 2], F32)
    pt_sb = sb("pt_sb", [128, 2, 1024])
    ident = sb("ident", [128, 128])
    iot = sb("iot", [128, 128], F32)
    b8 = ln8_bufs(sb)
    x1bf = sb("x1bf", [128, DM])
    x1ts = sb("x1ts", [128, FT, 128])
    gw_sb = sb("gw_sb", [128, FT, MOE_E])
    esc_all = sb("esc_all", [128, NSLOT * MOE_E], F32)
    cur_all = sb("cur_all", [128, NSLOT * MOE_E], F32)
    tsel_all = sb("tsel_all", [128, NSLOT * MOE_E], F32)
    gmax = sb("gmax", [128, 1], F32)
    esum8 = sb("esum8", [128, NSLOT], F32)
    ers8 = sb("ers8", [128, NSLOT], F32)
    mk8 = sb("mk8", [128, NSLOT], F32)
    cw_sb = sb("cw_sb", [128, NSLOT, MOE_E], F32)

    SP = [ps("sp0", [128, 1024]), ps("sp1", [128, 1024])]
    PTR = [es.enter_context(nc.psum_tensor("ptr0", [128, 1024], BF16)),
           es.enter_context(nc.psum_tensor("ptr1", [128, 1024], BF16))]
    PVT = [ps("pv0", [128, 512]), ps("pv1", [128, 512])]
    PV = [PVT[0][:, 0:128], PVT[1][:, 0:128]]
    pgate = PVT[0][:, 128:128 + MOE_E]

    sch = Sch()
    sch.dma("pe", x_tm[:], x_own.rearrange("(s p) f -> p s f", p=128), sem="dinp")
    sch.dma("sq", q_sb[:], qT.rearrange("h p n -> p h n"), sem="dinq")
    sch.dma("sq", qpos_sb[:], qpos[:], sem="dinq")
    sch.dma("pe", gw_sb[:], gwT.rearrange("(t p) e -> p t e", p=128), sem="dinp")
    ld_p = ("dinp", sch.val("dinp"))
    ld_q2 = ("dinq", sch.val("dinq"))
    ld_x = ld_gw = ld_p
    ld_q = ld_qp = ld_q2
    sch.op("pe", e_iota(iot[:], [[-1, 128]], 0, 1), inc="pe")
    sch.op("ve", e_ts(ident[:], iot[:], 0.0, None, ALU.is_equal),
           wait=[("pe", sch.val("pe"))], inc="ve")
    id_ready = sch.val("ve")
    jvec = s_sb
    sch.op("pe", e_iota(jvec[:], [[1, 1024]], 0, 0), inc="pe")
    jv_pe = sch.val("pe")
    for s in range(NSLOT):
        sch.op("ve", e_ts(bvec[:, s:s + 1], qpos_sb[:, s:s + 1], 1.0,
                          float(-1024 * (CAPS[s] - 1)), ALU.mult, ALU.add),
               wait=[ld_qp, ("pe", jv_pe)], inc="ve")
    sch.drain("ve")
    for s in range(NSLOT):
        sch.op("ve", e_ts(tmask[:, s, :], jvec[:], bvec[:, s:s + 1], None, ALU.is_gt),
               inc="ve")
    sch.drain("ve")
    jv_ready = sch.val("ve")

    cfg = ("l0", GRP0, lambda g: range(4 * g, 4 * g + 4), kf, vf, kg_sb, vg_sb,
           q_sb, p_st, tmask, jv_ready, s_sb, negm, l_stv, l_tot,
           rl, pt_sb, ident, id_ready, SP, PTR, PV, x_tm, None, ld_q, ld_x, None)
    emit_attention(sch, cfg)
    attn_ve = sch.val("ve")
    attn_se = sch.val("se")
    ptr_evac = {0: ("ve", attn_ve), 1: ("ve", attn_ve)}

    stats_ve = emit_ln_batched(sch, b8, x_tm, x1bf[:], attn_ve, attn_se)
    rstd8, nmr8 = b8["rstd8"], b8["nmr8"]
    st_x1t = {}
    pgate_all = PVT[0][:, 128:256]
    last_x1t_te = sch.val("te")
    for s in range(NSLOT):
        xs = x_tm[:, s, :]
        wap = [("ve", stats_ve)] if s == 0 else [("se", sch.val("se"))]
        sch.op("ve", e_ts(xs, xs, rstd8[:, s:s + 1], nmr8[:, s:s + 1],
                          ALU.mult, ALU.add), wait=wap, inc="ve")
        ln_ve = sch.val("ve")
        wbf = [("ve", ln_ve), ("te", last_x1t_te), ("se", sch.val("se"))]
        sch.op("se", e_act(x1bf[:], xs, AF.Identity), wait=wbf, inc="se")
        bf_se = sch.val("se")
        for f4 in range(4):
            ptrb = f4 % 2
            wtr = [("se", bf_se), ("ve", id_ready)]
            if ptr_evac[ptrb] is not None:
                wtr.append(ptr_evac[ptrb])
            for jj in range(4):
                ft = f4 * 4 + jj
                sch.op("te", e_transpose(PTR[ptrb][:, jj * 128:(jj + 1) * 128],
                                         x1bf[:, ft * 128:(ft + 1) * 128],
                                         ident[:]),
                       wait=wtr if jj == 0 else (), inc="te" if jj == 3 else None)
            trv = sch.val("te")
            last_x1t_te = trv
            wev = [("te", trv)]
            if s > 0 and f4 == 0 and st_x1t:
                wev.append(("so", sch.val("so")))
            sch.op("ve", e_copy(x1ts[:, f4 * 4:(f4 + 1) * 4, :], PTR[ptrb][:, 0:512]),
                   wait=wev, inc="ve")
            ptr_evac[ptrb] = ("ve", sch.val("ve"))
        x1t_ve = sch.val("ve")
        sch.dma("sq", x1T_o.rearrange("(t p) (sl i) -> p t sl i", p=128, i=128)[:, :, s, :],
                x1ts[:], sem="so", wait=[("ve", x1t_ve)])
        st_x1t[s] = sch.val("so")
        wg0 = [("ve", x1t_ve), ld_gw]
        if s == 0:
            wg0 += [("ve", attn_ve), ("se", attn_se)]
        for ft in range(FT):
            sch.op("te", e_matmul(pgate_all[:, s * 16:(s + 1) * 16],
                                  x1ts[:, ft, :], gw_sb[:, ft, :],
                                  start=(ft == 0), stop=(ft == FT - 1)),
                   wait=wg0 if ft == 0 else (), inc="te" if ft == FT - 1 else None)
    gv = sch.val("te")
    esc3 = esc_all.ap().rearrange("p (s e) -> p s e", e=MOE_E)
    cur3 = cur_all.ap().rearrange("p (s e) -> p s e", e=MOE_E)
    ts3 = tsel_all.ap().rearrange("p (s e) -> p s e", e=MOE_E)
    pg3 = pgate_all.rearrange("p (s e) -> p s e", e=MOE_E)
    sch.op("ve", e_red(mk8[:], pg3, AX.X, "max"), wait=[("te", gv)], inc="ve")
    sch.drain("ve")
    mkb0 = mk8.ap().rearrange("p (s o) -> p s o", o=1).broadcast_to((128, NSLOT, MOE_E))
    sch.op("ve", e_tt(esc3, pg3, mkb0, ALU.subtract), inc="ve")
    sch.drain("ve")
    sch.op("se", e_act(esc_all[:], esc_all[:], AF.Exp),
           wait=[("ve", sch.val("ve"))], inc="se")
    sch.op("ve", e_red(esum8[:], esc3, AX.X, "sum"),
           wait=[("se", sch.val("se"))], inc="ve")
    sch.drain("ve")
    sch.op("ve", e_recip(ers8[:], esum8[:]), inc="ve")
    sch.drain("ve")
    ers_b = ers8.ap().rearrange("p (s o) -> p s o", o=1).broadcast_to((128, NSLOT, MOE_E))
    sch.op("ve", e_tt(esc3, esc3, ers_b, ALU.mult), inc="ve")
    sch.drain("ve")
    sch.op("ve", e_copy(cur_all[:], esc_all[:]), inc="ve")
    sch.drain("ve")
    for k6 in range(MOE_K):
        sch.op("ve", e_red(mk8[:], cur3, AX.X, "max"), inc="ve")
        sch.drain("ve")
        mk_b = mk8.ap().rearrange("p (s o) -> p s o", o=1).broadcast_to((128, NSLOT, MOE_E))
        if k6 < MOE_K - 1:
            sch.op("ve", e_tt(ts3, cur3, mk_b, ALU.is_ge), inc="ve")
            sch.drain("ve")
            sch.op("ve", e_stt(cur3, ts3, -2.0, cur3, ALU.mult, ALU.add), inc="ve")
            sch.drain("ve")
    sch.op("ve", e_tt(ts3, esc3, mk_b, ALU.is_ge), inc="ve")
    sch.drain("ve")
    sch.op("ve", e_tt(cw_sb.ap(), esc3, ts3, ALU.mult), inc="ve")
    sch.drain("ve")
    done_ve = sch.val("ve")
    sch.dma("sq", x1_o.rearrange("(s p) f -> p s f", p=128), x_tm[:],
            sem="so", wait=[("ve", done_ve)])
    sch.dma("sq", cw_o.rearrange("(s p) e -> p s e", p=128), cw_sb[:],
            sem="so", wait=[("ve", done_ve)])
    st = sch.val("so")
    sch.op("sq", lambda e, s: e.nop(), wait=[("so", st)])

    sch.build(nc, es)
    with es:
        pass
    return nc



def build_B():
    nc = bass.Bass("TRN2", target_bir_lowering=False)
    x1T = nc.dram_tensor("x1T", [DM, SEQ], BF16, kind="ExternalInput").ap()
    cwe_f = nc.dram_tensor("cwe_f", [SEQ, 2], F32, kind="ExternalInput").ap()
    wg = nc.dram_tensor("wg", [2, DM, MOE_D], F32, kind="ExternalInput").ap()
    wu = nc.dram_tensor("wu", [2, DM, MOE_D], F32, kind="ExternalInput").ap()
    wdn = nc.dram_tensor("wdn", [2, MOE_D, DM], F32, kind="ExternalInput").ap()
    wsums = nc.dram_tensor("wsums", [8, 4], F32, kind="ExternalInput").ap()
    wsl = [nc.dram_tensor(f"w{j}s", [128, 11264], F32, kind="ExternalInput").ap()
           for j in (1, 2, 3)]
    yp_o = nc.dram_tensor("yp_o", [SEQ, DM], BF16, kind="ExternalOutput").ap()
    qw_o = [nc.dram_tensor(f"qw{j}_o", [128, 11264], BF16, kind="ExternalOutput").ap()
            for j in (1, 2, 3)]

    es = contextlib.ExitStack()
    sb = lambda name, shape, dt=BF16: es.enter_context(nc.sbuf_tensor(name, shape, dt))
    ps = lambda name, shape, dt=F32: es.enter_context(nc.psum_tensor(name, shape, dt))

    xb = sb("xb", [128, 2, FT, 1024])
    at = sb("at", [128, 11, 1024])
    wgu = sb("wgu", [128, 3, 2, FT, 128])
    wd_sb = sb("wd_sb", [128, 11, DM])
    yp_sb = sb("yp_sb", [128, 8, DM])
    sg = sb("sg", [128, 1024])
    cwe = sb("cwe", [128, 8, 2], F32)
    ws_sb = sb("ws_sb", [8, 4], F32)
    ones8 = sb("ones8", [8, 1], F32)
    ones1r = sb("ones1r", [1, 128], F32)
    gsc = sb("gsc", [1, 1], F32)
    ginv = sb("ginv", [1, 1], F32)
    invb = sb("invb", [128, 3], F32)
    QCH = 512
    wqb = sb("wqb", [128, 2, QCH], F32)
    ubuf_t = sb("ubuf_t", [128, QCH], F32)
    sgn_t = sb("sgn_t", [128, QCH])
    msk_t = sb("msk_t", [128, QCH])
    qbuf = sb("qbuf", [128, 2, QCH])
    ubuf = ubuf_t[:]
    sgn = sgn_t[:]
    msk = msk_t[:]

    PG = [ps("pg0", [128, 512]), ps("pg1", [128, 512])]
    PU = [ps("pu0", [128, 512]), ps("pu1", [128, 512])]
    PD = [ps("pd0", [128, 512]), ps("pd1", [128, 512])]
    pb1 = ps("pb1", [128, 512])

    sch = Sch()
    MT = 11
    sch.dma("sq", ws_sb[:], wsums[:], sem="dws")
    ld_ws = ("dws", sch.val("dws"))
    sch.op("ve", e_memset(ones8[:], 1.0), inc="ve")
    sch.op("ve", e_memset(ones1r[:], 1.0), inc="ve")
    ones_ve = sch.val("ve")
    NW = {0: DM * FFN_H, 1: DM * FFN_H, 2: FFN_H * DM}
    for j in range(3):
        wbj = [("ve", ones_ve), ld_ws]
        if j > 0:
            wbj.append(("ve", sch.val("ve")))
        sch.op("te", e_matmul(pb1[0:1, 0:1], ones8[:], ws_sb[:, j:j + 1],
                              start=True, stop=True), wait=wbj, inc="te")
        sch.op("ve", e_ts(gsc[:], pb1[0:1, 0:1], 1.0 / NW[j], 1e-8, ALU.mult, ALU.add),
               wait=[("te", sch.val("te"))], inc="ve")
        sch.drain("ve")
        sch.op("ve", e_recip(ginv[:], gsc[:]), inc="ve")
        sch.drain("ve")
        gv = sch.val("ve")
        sch.op("te", e_matmul(pb1[:, 1:2], ones1r[:], ginv[:], start=True, stop=True),
               wait=[("ve", gv)], inc="te")
        sch.op("ve", e_copy(invb[:, j:j + 1], pb1[:, 1:2]),
               wait=[("te", sch.val("te"))], inc="ve")
        sch.drain("ve")
    invb_ve = sch.val("ve")
    gamma_te = sch.val("te")

    QN = 11264 // QCH
    qstate = {"k": 0, "qve": {}, "qso": {}}

    def emit_quant_chunk():
        k = qstate["k"]
        if k >= 3 * QN:
            return
        qstate["k"] += 1
        j, ch = divmod(k, QN)
        par = k % 2
        w = []
        if k >= 2:
            w.append(("ve", qstate["qve"][k - 2]))
        sch.dma("sq", wqb[:, par, :], wsl[j][:, ch * QCH:(ch + 1) * QCH],
                sem=f"dq{par}", wait=w)
        ldv = (f"dq{par}", sch.val(f"dq{par}"))
        wabs = [ldv, ("ve", invb_ve)]
        if k >= 1:
            wabs.append(("ve", qstate["qve"][k - 1]))
        sch.op("se", e_act(ubuf, wqb[:, par, :], AF.Abs,
                           scale=invb[:, j:j + 1]),
               wait=wabs + [("se", sch.val("se"))], inc="se")
        sch.op("se", e_act(sgn, wqb[:, par, :], AF.Sign),
               wait=[("se", sch.val("se"))], inc="se")
        sgn_se = sch.val("se")
        sch.op("ve", e_ts(msk, ubuf, 0.5, None, ALU.is_gt),
               wait=[("se", sgn_se)], inc="ve")
        sch.drain("ve")
        wq2 = []
        if qstate["qso"].get(par) is not None:
            wq2.append(("so", sch.val("so")))
        sch.op("ve", e_tt(qbuf[:, par, :], msk, sgn, ALU.mult),
               wait=wq2, inc="ve")
        qstate["qve"][k] = sch.val("ve")
        sch.dma("sq", qw_o[j][:, ch * QCH:(ch + 1) * QCH], qbuf[:, par, :],
                sem="so", wait=[("ve", qstate["qve"][k])])
        qstate["qso"][par] = sch.val("so")

    gu_mm = {}
    pg_reader = {}
    pd_reader = {}
    wdn_mm = 0
    last_store = 0
    xb_te2 = {}
    prev_cw_ve = 0
    pending_store = [None]

    def flush_store():
        if pending_store[0] is not None:
            pending_store[0]()
            pending_store[0] = None

    for tb in range(8):
        xpar = tb % 2
        wx = [("te", xb_te2[tb - 2])] if tb >= 2 else []
        sch.dma("sq", xb[:, xpar, :, :],
                x1T.rearrange("(t p) n -> p t n", p=128)[:, :, tb * 1024:(tb + 1) * 1024],
                sem=f"dxb{xpar}", wait=wx)
        ld_xb = (f"dxb{xpar}", sch.val(f"dxb{xpar}"))
        wcw = [("ve", prev_cw_ve)] if tb > 0 else []
        sch.dma("sq", cwe[:], cwe_f.rearrange("(b t p) e -> b p t e", p=128, t=8)[tb],
                sem="dcw", wait=wcw)
        ld_cw = ("dcw", sch.val("dcw"))
        flush_store()
        for e in range(2):
            wdn_ld = None
            for mt in range(MT):
                par = mt % 3
                wld = []
                if gu_mm.get(par) is not None:
                    wld.append(("te", gu_mm[par]))
                sch.dma("pe", wgu[:, par, 0, :, :],
                        wg.rearrange("a (t p) m -> a p t m", p=128)[e][:, :, mt * 128:(mt + 1) * 128],
                        sem=f"dg{par}", wait=wld)
                sch.dma("pe", wgu[:, par, 1, :, :],
                        wu.rearrange("a (t p) m -> a p t m", p=128)[e][:, :, mt * 128:(mt + 1) * 128],
                        sem=f"dg{par}", wait=())
                ldg = (f"dg{par}", sch.val(f"dg{par}"))
                wwd = [("te", wdn_mm)] if (tb + e > 0 and mt == 0) else []
                sch.dma("pe", wd_sb[:, mt, :],
                        wdn.rearrange("a (t p) f -> a p t f", p=128)[e][:, mt, :],
                        sem="dwd", wait=wwd)
                if mt == MT - 1:
                    wdn_ld = ("dwd", sch.val("dwd"))
                for half in range(2):
                    pgb = PG[(mt * 2 + half) % 2]
                    pub = PU[(mt * 2 + half) % 2]
                    wmm = [ldg, ld_xb]
                    if pg_reader.get((mt * 2 + half) % 2) is not None:
                        wmm.append(pg_reader[(mt * 2 + half) % 2])
                    for ft in range(FT):
                        sch.op("te", e_matmul(pgb[:], wgu[:, par, 0, ft, :],
                                              xb[:, xpar, ft, half * 512:(half + 1) * 512],
                                              start=(ft == 0), stop=(ft == FT - 1)),
                               wait=wmm if ft == 0 else (),
                               inc="te" if ft == FT - 1 else None)
                    gv = sch.val("te")
                    for ft in range(FT):
                        sch.op("te", e_matmul(pub[:], wgu[:, par, 1, ft, :],
                                              xb[:, xpar, ft, half * 512:(half + 1) * 512],
                                              start=(ft == 0), stop=(ft == FT - 1)),
                               inc="te" if ft == FT - 1 else None)
                    uv = sch.val("te")
                    gu_mm[par] = uv
                    xb_te2[tb] = uv
                    sch.op("se", e_act(sg[:, half * 512:(half + 1) * 512], pgb[:],
                                       AF.Silu), wait=[("te", gv), ("ve", sch.val("ve"))],
                           inc="se")
                    sv = sch.val("se")
                    sch.op("ve", e_tt(at[:, mt, half * 512:(half + 1) * 512],
                                      sg[:, half * 512:(half + 1) * 512], pub[:],
                                      ALU.mult),
                           wait=[("se", sv), ("te", uv)], inc="ve")
                    pg_reader[(mt * 2 + half) % 2] = ("ve", sch.val("ve"))
            at_ve = sch.val("ve")
            first = (e == 0)
            for t8 in range(8):
                for d4 in range(4):
                    pdb = PD[(t8 * 4 + d4) % 2]
                    wmm = [("ve", at_ve), wdn_ld]
                    if pd_reader.get((t8 * 4 + d4) % 2) is not None:
                        wmm.append(pd_reader[(t8 * 4 + d4) % 2])
                    for mt in range(MT):
                        sch.op("te", e_matmul(pdb[:], at[:, mt, t8 * 128:(t8 + 1) * 128],
                                              wd_sb[:, mt, d4 * 512:(d4 + 1) * 512],
                                              start=(mt == 0), stop=(mt == MT - 1)),
                               wait=wmm if mt == 0 else (),
                               inc="te" if mt == MT - 1 else None)
                    wdn_mm = sch.val("te")
                    if d4 == 0:
                        emit_quant_chunk()
                    dst = yp_sb[:, t8, d4 * 512:(d4 + 1) * 512]
                    wev = [("te", wdn_mm), ld_cw]
                    if first and t8 == 0 and d4 == 0 and tb > 0:
                        wev.append(("so", last_store))
                    if first:
                        sch.op("ve", e_ts(dst, pdb[:], cwe[:, t8, 0:1]),
                               wait=wev, inc="ve")
                    else:
                        sch.op("ve", e_stt(dst, pdb[:], cwe[:, t8, 1:2], dst,
                                           ALU.mult, ALU.add),
                               wait=wev, inc="ve")
                    pd_reader[(t8 * 4 + d4) % 2] = ("ve", sch.val("ve"))
        prev_cw_ve = sch.val("ve")
        cwv = prev_cw_ve
        tb_ = tb
        def mk_store(tb_=tb_, cwv=cwv):
            def go():
                sch.dma("sq", yp_o[tb_ * 1024:(tb_ + 1) * 1024, :]
                        .rearrange("(t p) f -> p t f", p=128),
                        yp_sb[:], sem="so", wait=[("ve", cwv)])
            return go
        pending_store[0] = mk_store()
        if tb == 7:
            flush_store()
        last_store = sch.val("so") + 16 if tb < 7 else sch.val("so")

    while qstate["k"] < 3 * QN:
        emit_quant_chunk()
    st = sch.val("so")
    sch.op("sq", lambda e, s: e.nop(), wait=[("so", st)])

    sch.build(nc, es)
    with es:
        pass
    return nc



def emit_ln_rows(sch, bufs, xs, wait_ve, wait_se=None):
    s1, sqs, mean, ex2, m2, var, sd, rstd, nmr_t, nmr, jm = bufs
    w0 = [("ve", wait_ve)]
    w1 = [("ve", wait_ve)] + ([("se", wait_se)] if wait_se else [])
    sch.op("ve", e_red(s1[:], xs, AX.X, "sum"), wait=w0, inc="ve")
    sch.op("se", e_act(jm[:], xs, AF.Square, accum_out=sqs[:]), wait=w1, inc="se")
    sq_se = sch.val("se")
    sch.drain("ve")
    sch.op("ve", e_ts(mean[:], s1[:], 1.0 / DM), inc="ve")
    sch.op("ve", e_ts(ex2[:], sqs[:], 1.0 / DM, 1e-5, ALU.mult, ALU.add),
           wait=[("se", sq_se)], inc="ve")
    sch.drain("ve")
    sch.op("ve", e_tt(m2[:], mean[:], mean[:], ALU.mult), inc="ve")
    sch.drain("ve")
    sch.op("ve", e_tt(var[:], ex2[:], m2[:], ALU.subtract), inc="ve")
    sch.drain("ve")
    sch.op("se", e_act(sd[:], var[:], AF.Sqrt), wait=[("ve", sch.val("ve"))], inc="se")
    sch.op("ve", e_recip(rstd[:], sd[:]), wait=[("se", sch.val("se"))], inc="ve")
    sch.drain("ve")
    sch.op("ve", e_tt(nmr_t[:], mean[:], rstd[:], ALU.mult), inc="ve")
    sch.drain("ve")
    sch.op("ve", e_ts(nmr[:], nmr_t[:], -1.0), inc="ve")
    sch.drain("ve")
    sch.op("ve", e_ts(xs, xs, rstd[:], nmr[:], ALU.mult, ALU.add), inc="ve")
    return sch.val("ve")


def ln8_bufs(sb):
    names = ["s1a", "sqa", "mean8", "ex28", "m28", "var8", "sd8", "rstd8",
             "nmrt8", "nmr8"]
    return {n: sb("l8_" + n, [128, NSLOT], F32) for n in names}


def emit_ln_batched(sch, b8, x_tm, jm_bf, wait_ve, wait_se):
    for s in range(NSLOT):
        sch.op("ve", e_red(b8["s1a"][:, s:s + 1], x_tm[:, s, :], AX.X, "sum"),
               wait=[("ve", wait_ve)] if s == 0 else (), inc="ve")
        sch.op("se", e_act(jm_bf, x_tm[:, s, :], AF.Square,
                           accum_out=b8["sqa"][:, s:s + 1]),
               wait=([("se", wait_se)] if s == 0 else [("se", sch.val("se"))])
               + [("ve", wait_ve)], inc="se")
    sq_se = sch.val("se")
    sch.drain("ve")
    sch.op("ve", e_ts(b8["mean8"][:], b8["s1a"][:], 1.0 / DM), inc="ve")
    sch.op("ve", e_ts(b8["ex28"][:], b8["sqa"][:], 1.0 / DM, 1e-5, ALU.mult, ALU.add),
           wait=[("se", sq_se)], inc="ve")
    sch.drain("ve")
    sch.op("ve", e_tt(b8["m28"][:], b8["mean8"][:], b8["mean8"][:], ALU.mult), inc="ve")
    sch.drain("ve")
    sch.op("ve", e_tt(b8["var8"][:], b8["ex28"][:], b8["m28"][:], ALU.subtract), inc="ve")
    sch.drain("ve")
    sch.op("se", e_act(b8["sd8"][:], b8["var8"][:], AF.Sqrt),
           wait=[("ve", sch.val("ve"))], inc="se")
    sch.op("ve", e_recip(b8["rstd8"][:], b8["sd8"][:]),
           wait=[("se", sch.val("se"))], inc="ve")
    sch.drain("ve")
    sch.op("ve", e_tt(b8["nmrt8"][:], b8["mean8"][:], b8["rstd8"][:], ALU.mult), inc="ve")
    sch.drain("ve")
    sch.op("ve", e_ts(b8["nmr8"][:], b8["nmrt8"][:], -1.0), inc="ve")
    sch.drain("ve")
    return sch.val("ve")


def ln_bufs(sb, jm=None):
    names = ["s1", "sqs", "mean", "ex2", "m2", "var", "sd", "rstd", "nmr_t", "nmr"]
    out = [sb("ln_" + n, [128, 1], F32) for n in names]
    out.append(sb("ln_jm", [128, DM], F32) if jm is None else jm)
    return out


def emit_transpose_out(sch, nc_objs, s, xs_bf_getter, dram_out, PTR, ident,
                       id_ready, x1ts, ptr_evac, store_sem="so", extra_wait=None):
    for f4 in range(4):
        ptrb = f4 % 2
        wtr = [("ve", id_ready)]
        if extra_wait is not None:
            wtr.append(extra_wait)
        if ptr_evac[ptrb] is not None:
            wtr.append(ptr_evac[ptrb])
        for jj in range(4):
            ft = f4 * 4 + jj
            sch.op("te", e_transpose(PTR[ptrb][:, jj * 128:(jj + 1) * 128],
                                     xs_bf_getter(ft), ident[:]),
                   wait=wtr if jj == 0 else (), inc="te" if jj == 3 else None)
        trv = sch.val("te")
        wev = [("te", trv)]
        if sch.val(store_sem) > 0:
            wev.append((store_sem, sch.val(store_sem)))
        sch.op("ve", e_copy(x1ts[:, f4 * 4:(f4 + 1) * 4, :], PTR[ptrb][:, 0:512]),
               wait=wev, inc="ve")
        ptr_evac[ptrb] = ("ve", sch.val("ve"))
    ev = sch.val("ve")
    sch.dma("sq", dram_out.rearrange("(t p) (sl i) -> p t sl i", p=128, i=128)[:, :, s, :],
            x1ts[:], sem=store_sem, wait=[("ve", ev)])
    return ptr_evac


def build_C1a():
    nc = bass.Bass("TRN2", target_bir_lowering=False)
    x1 = nc.dram_tensor("x1", [1024, DM], F32, kind="ExternalInput").ap()
    yps = nc.dram_tensor("yps", [8, 1024, DM], BF16, kind="ExternalInput").ap()
    x2_o = nc.dram_tensor("x2_o", [1024, DM], F32, kind="ExternalOutput").ap()
    x2T_o = nc.dram_tensor("x2T_o", [DM, 1024], BF16, kind="ExternalOutput").ap()

    es = contextlib.ExitStack()
    sb = lambda name, shape, dt=BF16: es.enter_context(nc.sbuf_tensor(name, shape, dt))
    ps = lambda name, shape, dt=F32: es.enter_context(nc.psum_tensor(name, shape, dt))
    x_tm = sb("x_tm", [128, NSLOT, DM], F32)
    ypb = sb("ypb", [128, 2, NSLOT, DM])
    b8c = ln8_bufs(sb)
    xbf = sb("xbf", [128, DM])
    x1ts = sb("x1ts", [128, FT, 128])
    ident = sb("ident", [128, 128])
    iot = sb("iot", [128, 128], F32)
    PTR = [es.enter_context(nc.psum_tensor("ptr0", [128, 512], BF16)),
           es.enter_context(nc.psum_tensor("ptr1", [128, 512], BF16))]

    sch = Sch()
    sch.dma("pe", x_tm[:], x1.rearrange("(s p) f -> p s f", p=128), sem="dx")
    ld_x = ("dx", sch.val("dx"))
    sch.op("pe", e_iota(iot[:], [[-1, 128]], 0, 1), inc="pe")
    sch.op("ve", e_ts(ident[:], iot[:], 0.0, None, ALU.is_equal),
           wait=[("pe", sch.val("pe"))], inc="ve")
    id_ready = sch.val("ve")
    add_ve = {}
    for p in range(8):
        par = p % 2
        w = [("ve", add_ve[p - 2])] if p >= 2 else []
        sch.dma("sq", ypb[:, par, :, :], yps[p].rearrange("(s q) f -> q s f", q=128),
                sem=f"dy{par}", wait=w)
        ldv = (f"dy{par}", sch.val(f"dy{par}"))
        wadd = [ldv, ld_x] if p < 2 else [ldv]
        if p > 0:
            sch.drain("ve")
        sch.op("ve", e_tt(x_tm.ap().rearrange("p s f -> p (s f)"),
                          ypb[:, par, :, :].rearrange("p s f -> p (s f)"),
                          x_tm.ap().rearrange("p s f -> p (s f)"), ALU.add),
               wait=wadd, inc="ve")
        add_ve[p] = sch.val("ve")
    sum_ve = sch.val("ve")
    stats_ve = emit_ln_batched(sch, b8c, x_tm, xbf[:], sum_ve, sch.val("se"))
    ptr_evac = {0: None, 1: None}
    for s in range(NSLOT):
        xs = x_tm[:, s, :]
        wap = [("ve", stats_ve)] if s == 0 else [("se", sch.val("se"))]
        sch.op("ve", e_ts(xs, xs, b8c["rstd8"][:, s:s + 1], b8c["nmr8"][:, s:s + 1],
                          ALU.mult, ALU.add), wait=wap, inc="ve")
        ln_ve = sch.val("ve")
        wbf = [("ve", ln_ve), ("se", sch.val("se"))]
        if s > 0:
            wbf.append(("te", sch.val("te")))
        sch.op("se", e_act(xbf[:], xs, AF.Identity), wait=wbf, inc="se")
        bf_se = sch.val("se")
        ptr_evac = emit_transpose_out(
            sch, None, s, lambda ft: xbf[:, ft * 128:(ft + 1) * 128],
            x2T_o, PTR, ident, id_ready, x1ts, ptr_evac,
            extra_wait=("se", bf_se))
    done = sch.val("ve")
    sch.dma("sq", x2_o.rearrange("(s p) f -> p s f", p=128), x_tm[:], sem="so",
            wait=[("ve", done)])
    st = sch.val("so")
    sch.op("sq", lambda e, s: e.nop(), wait=[("so", st)])
    sch.build(nc, es)
    with es:
        pass
    return nc


def max_wait(a, b):
    if a is None:
        return b
    if b is None:
        return a
    if a[0] == b[0]:
        return (a[0], max(a[1], b[1]))
    return b



def build_C1b():
    nc = bass.Bass("TRN2", target_bir_lowering=False)
    x2T = nc.dram_tensor("x2T", [DM, SEQ], BF16, kind="ExternalInput").ap()
    wqkv = nc.dram_tensor("wqkv", [2, 3, DM, HD], F32, kind="ExternalInput").ap()
    qTh_o = nc.dram_tensor("qTh_o", [2, 128, SEQ], BF16, kind="ExternalOutput").ap()
    kTh_o = nc.dram_tensor("kTh_o", [2, 128, SEQ], BF16, kind="ExternalOutput").ap()
    vh_o = nc.dram_tensor("vh_o", [2, SEQ, HD], BF16, kind="ExternalOutput").ap()

    es = contextlib.ExitStack()
    sb = lambda name, shape, dt=BF16: es.enter_context(nc.sbuf_tensor(name, shape, dt))
    ps = lambda name, shape, dt=F32: es.enter_context(nc.psum_tensor(name, shape, dt))
    xb = sb("xb", [128, 2, FT, 1024])
    wsb = sb("wsb", [128, 2, 3, FT, HD])
    ost = sb("ost", [128, 6, 1024])
    vst = sb("vst", [128, 2, 8, HD])
    ident = sb("ident", [128, 128])
    iot = sb("iot", [128, 128], F32)
    pa = ps("pa", [128, 512])
    pb = ps("pb", [128, 512])
    PTR = [es.enter_context(nc.psum_tensor("ptr0", [128, 512], BF16)),
           es.enter_context(nc.psum_tensor("ptr1", [128, 512], BF16))]

    sch = Sch()
    sch.dma("pe", wsb[:], wqkv.rearrange("a b (t p) d -> p a b t d", p=128), sem="dw")
    ld_w = ("dw", sch.val("dw"))
    sch.op("pe", e_iota(iot[:], [[-1, 128]], 0, 1), inc="pe")
    sch.op("ve", e_ts(ident[:], iot[:], 0.0, None, ALU.is_equal),
           wait=[("pe", sch.val("pe"))], inc="ve")
    id_ready = sch.val("ve")
    xb_te = {}
    pp = [pa, pb]
    gidx = 0
    tr_ve = {0: None, 1: None}
    for tc in range(8):
        par = tc % 2
        w = [("te", xb_te[tc - 2])] if tc >= 2 else []
        sch.dma("sq", xb[:, par, :, :],
                x2T.rearrange("(t p) n -> p t n", p=128)[:, :, tc * 1024:(tc + 1) * 1024],
                sem=f"dx{par}", wait=w)
        ldx = (f"dx{par}", sch.val(f"dx{par}"))
        for hj in range(2):
            for mat in range(3):
                unit = tc * 6 + hj * 3 + mat
                slot = unit % 6
                for half in range(2):
                    bank = gidx % 2
                    wmm = [ldx, ld_w]
                    if gidx >= 2:
                        wmm.append(("se", gidx - 1))
                    for ft in range(FT):
                        sch.op("te", e_matmul(pp[bank][:], wsb[:, hj, mat, ft, :],
                                              xb[:, par, ft, half * 512:(half + 1) * 512],
                                              start=(ft == 0), stop=(ft == FT - 1)),
                               wait=wmm if ft == 0 else (),
                               inc="te" if ft == FT - 1 else None)
                    mmv = sch.val("te")
                    xb_te[tc] = mmv
                    wev = [("te", mmv)]
                    if half == 0 and sch.val("so") > 0:
                        wev.append(("so", sch.val("so")))
                    sch.op("se", e_act(ost[:, slot, half * 512:(half + 1) * 512],
                                       pp[bank][:], AF.Identity,
                                       scale=ISQ if mat == 0 else None),
                           wait=wev, inc="se")
                    gidx += 1
                ev = sch.val("se")
                if mat == 0:
                    sch.dma("se", qTh_o[hj, :, tc * 1024:(tc + 1) * 1024],
                            ost[:, slot, :], sem="so", wait=[("se", ev)])
                elif mat == 1:
                    sch.dma("se", kTh_o[hj, :, tc * 1024:(tc + 1) * 1024],
                            ost[:, slot, :], sem="so", wait=[("se", ev)])
                else:
                    for j4 in range(2):
                        ptrb = j4
                        wtr = [("se", ev), ("ve", id_ready)]
                        if tr_ve[ptrb] is not None:
                            wtr.append(tr_ve[ptrb])
                        for jj in range(4):
                            j = j4 * 4 + jj
                            sch.op("te", e_transpose(
                                PTR[ptrb][:, jj * 128:(jj + 1) * 128],
                                ost[:, slot, j * 128:(j + 1) * 128], ident[:]),
                                wait=wtr if jj == 0 else (),
                                inc="te" if jj == 3 else None)
                        trv = sch.val("te")
                        wev2 = [("te", trv)]
                        if sch.val("so") > 0:
                            wev2.append(("so", sch.val("so")))
                        sch.op("ve", e_copy(vst[:, j4, 0:4, :], PTR[ptrb][:]),
                               wait=wev2, inc="ve")
                        tr_ve[ptrb] = ("ve", sch.val("ve"))
                        sch.dma("se",
                                vh_o[hj, tc * 1024 + j4 * 512: tc * 1024 + (j4 + 1) * 512, :]
                                .rearrange("(j p) d -> p j d", p=128),
                                vst[:, j4, 0:4, :], sem="so",
                                wait=[("ve", sch.val("ve"))])
    st = sch.val("so")
    sch.op("sq", lambda e, s: e.nop(), wait=[("so", st)])
    sch.build(nc, es)
    with es:
        pass
    return nc



def build_C2():
    nc = bass.Bass("TRN2", target_bir_lowering=False)
    x2 = nc.dram_tensor("x2", [1024, DM], F32, kind="ExternalInput").ap()
    qT = nc.dram_tensor("qT", [HEADS1, 128, 1024], BF16, kind="ExternalInput").ap()
    kf = nc.dram_tensor("kf", [HEADS1, 128, SEQ], BF16, kind="ExternalInput").ap()
    vf = nc.dram_tensor("vf", [HEADS1, SEQ, HD], BF16, kind="ExternalInput").ap()
    qpos = nc.dram_tensor("qpos", [128, NSLOT], F32, kind="ExternalInput").ap()
    wpT = nc.dram_tensor("wpT", [DM, DM], F32, kind="ExternalInput").ap()
    x3_o = nc.dram_tensor("x3_o", [1024, DM], F32, kind="ExternalOutput").ap()
    x3T_o = nc.dram_tensor("x3T_o", [DM, 1024], BF16, kind="ExternalOutput").ap()

    es = contextlib.ExitStack()
    sb = lambda name, shape, dt=BF16: es.enter_context(nc.sbuf_tensor(name, shape, dt))
    ps = lambda name, shape, dt=F32: es.enter_context(nc.psum_tensor(name, shape, dt))
    x_tm = sb("x_tm", [128, NSLOT, DM], F32)
    yq = sb("yq", [128, HEADS1, 1024])
    kg_sb = sb("kg_sb", [128, 2, SEQ])
    vg_sb = sb("vg_sb", [128, 2, 64, HD])
    p_st = sb("p_st", [128, SEQ])
    bvec = sb("bvec", [128, NSLOT], F32)
    tmask = sb("tmask", [128, NSLOT, 1024], U8)
    ytmp = sb("ytmp", [128, 2, 128])
    qpos_sb = sb("qpos_sb", [128, NSLOT], F32)
    s_sb = sb("s_sb", [128, 1024], F32)
    negm = sb("negm", [128, 2], F32)
    l_stv = sb("l_stv", [128, 16], F32)
    l_tot = sb("l_tot", [128, 2], F32)
    rl = sb("rl", [128, 2], F32)
    pt_sb = sb("pt_sb", [128, 2, 1024])
    ident = sb("ident", [128, 128])
    iot = sb("iot", [128, 128], F32)
    wp_sb = sb("wp_sb", [128, FT, 256])
    b8c2 = ln8_bufs(sb)
    x1ts = sb("x1ts", [128, FT, 128])
    SP = [ps("sp0", [128, 1024]), ps("sp1", [128, 1024])]
    PTR = [es.enter_context(nc.psum_tensor("ptr0", [128, 1024], BF16)),
           es.enter_context(nc.psum_tensor("ptr1", [128, 1024], BF16))]
    PVT = [ps("pv0", [128, 512]), ps("pv1", [128, 512])]
    PV = [PVT[0][:, 0:128], PVT[1][:, 0:128]]
    PJ = [SP[0][:, 0:512], SP[1][:, 0:512]]
    PTR2_VIEW = [PTR[0][:, 0:512], PTR[1][:, 0:512]]

    sch = Sch()
    sch.dma("pe", x_tm[:], x2.rearrange("(s p) f -> p s f", p=128), sem="dinp")
    sch.dma("sq", yq[:], qT.rearrange("h p n -> p h n"), sem="dinq")
    sch.dma("sq", qpos_sb[:], qpos[:], sem="dinq")
    ld_p = ("dinp", sch.val("dinp"))
    ld_q = ("dinq", sch.val("dinq"))
    sch.op("pe", e_iota(iot[:], [[-1, 128]], 0, 1), inc="pe")
    sch.op("ve", e_ts(ident[:], iot[:], 0.0, None, ALU.is_equal),
           wait=[("pe", sch.val("pe"))], inc="ve")
    id_ready = sch.val("ve")
    jvec = s_sb
    sch.op("pe", e_iota(jvec[:], [[1, 1024]], 0, 0), inc="pe")
    jv_pe = sch.val("pe")
    for s in range(NSLOT):
        sch.op("ve", e_ts(bvec[:, s:s + 1], qpos_sb[:, s:s + 1], 1.0,
                          float(-1024 * (CAPS[s] - 1)), ALU.mult, ALU.add),
               wait=[ld_q, ("pe", jv_pe)], inc="ve")
    sch.drain("ve")
    for s in range(NSLOT):
        sch.op("ve", e_ts(tmask[:, s, :], jvec[:], bvec[:, s:s + 1], None, ALU.is_gt),
               inc="ve")
    sch.drain("ve")
    jv_ready = sch.val("ve")

    cfg = ("l1", HEADS1, lambda h: [h], kf, vf, kg_sb, vg_sb,
           yq, p_st, tmask, jv_ready, s_sb, negm, l_stv, l_tot,
           rl, pt_sb, ident, id_ready, SP, PTR, PV, None, yq, ld_q, ld_p, ytmp)
    emit_attention(sch, cfg)
    attn_se = sch.val("se")

    pj_reader = {0: None, 1: None}
    wp_mm = None
    for d8 in range(8):
        wld = [("te", wp_mm)] if wp_mm is not None else []
        sch.dma("pe", wp_sb[:],
                wpT.rearrange("(t p) f -> p t f", p=128)[:, :, d8 * 256:(d8 + 1) * 256],
                sem="dwp", wait=wld)
        ld_wp = ("dwp", sch.val("dwp"))
        for s in range(NSLOT):
            bank = (d8 * 8 + s) % 2
            wmm = [ld_wp, ("se", attn_se)]
            if pj_reader[bank] is not None:
                wmm.append(pj_reader[bank])
            for ct in range(FT):
                sch.op("te", e_matmul(PJ[bank][:, 0:256], yq[:, ct, s * 128:(s + 1) * 128],
                                      wp_sb[:, ct, :],
                                      start=(ct == 0), stop=(ct == FT - 1)),
                       wait=wmm if ct == 0 else (), inc="te" if ct == FT - 1 else None)
            wp_mm = sch.val("te")
            sch.op("ve", e_tt(x_tm[:, s, d8 * 256:(d8 + 1) * 256], PJ[bank][:, 0:256],
                              x_tm[:, s, d8 * 256:(d8 + 1) * 256], ALU.add),
                   wait=[("te", wp_mm), ld_p], inc="ve")
            pj_reader[bank] = ("ve", sch.val("ve"))
    proj_ve = sch.val("ve")

    xbf = pt_sb.ap().rearrange("p a b -> p (a b)")
    stats_ve = emit_ln_batched(sch, b8c2, x_tm, xbf, proj_ve, sch.val("se"))
    ptr_evac2 = {0: None, 1: None}
    for s in range(NSLOT):
        xs = x_tm[:, s, :]
        wap = [("ve", stats_ve)] if s == 0 else [("se", sch.val("se"))]
        sch.op("ve", e_ts(xs, xs, b8c2["rstd8"][:, s:s + 1], b8c2["nmr8"][:, s:s + 1],
                          ALU.mult, ALU.add), wait=wap, inc="ve")
        ln_ve = sch.val("ve")
        wbf = [("ve", ln_ve), ("se", sch.val("se"))]
        if s > 0:
            wbf.append(("te", sch.val("te")))
        sch.op("se", e_act(xbf, xs, AF.Identity), wait=wbf, inc="se")
        bf_se = sch.val("se")
        ptr_evac2 = emit_transpose_out(
            sch, None, s, lambda ft: xbf[:, ft * 128:(ft + 1) * 128],
            x3T_o, PTR2_VIEW, ident, id_ready, x1ts, ptr_evac2,
            extra_wait=("se", bf_se))
    done = sch.val("ve")
    sch.dma("sq", x3_o.rearrange("(s p) f -> p s f", p=128), x_tm[:], sem="so",
            wait=[("ve", done)])
    st = sch.val("so")
    sch.op("sq", lambda e, s: e.nop(), wait=[("so", st)])
    sch.build(nc, es)
    with es:
        pass
    return nc



def build_D():
    nc = bass.Bass("TRN2", target_bir_lowering=False)
    x3 = nc.dram_tensor("x3", [1024, DM], F32, kind="ExternalInput").ap()
    x3T = nc.dram_tensor("x3T", [DM, 1024], BF16, kind="ExternalInput").ap()
    qw1 = nc.dram_tensor("qw1", [DM, FFN_H], BF16, kind="ExternalInput").ap()
    qw2 = nc.dram_tensor("qw2", [DM, FFN_H], BF16, kind="ExternalInput").ap()
    qw3 = nc.dram_tensor("qw3", [FFN_H, DM], BF16, kind="ExternalInput").ap()
    out_o = nc.dram_tensor("out_o", [1024, DM], F32, kind="ExternalOutput").ap()

    es = contextlib.ExitStack()
    sb = lambda name, shape, dt=BF16: es.enter_context(nc.sbuf_tensor(name, shape, dt))
    ps = lambda name, shape, dt=F32: es.enter_context(nc.psum_tensor(name, shape, dt))
    x_tm = sb("x_tm", [128, NSLOT, DM], F32)
    pool = sb("pool", [128, 4, FT, 512])
    hidT = sb("hidT", [128, 44, 512])
    xh = sb("xh", [128, FT, 512])
    sgst = sb("sgst", [128, DM])
    b8d = ln8_bufs(sb)
    P1 = [ps("p10", [128, 512]), ps("p11", [128, 512])]
    P2 = [ps("p20", [128, 512]), ps("p21", [128, 512])]
    PD = [ps("pd0", [128, 512]), ps("pd1", [128, 512])]

    qw3c = pool.ap().rearrange("p a b c -> p (a b c)")[:, 0:44 * 512]\
        .rearrange("p (m j) -> p m j", j=512)

    sch = Sch()
    sch.dma("pe", x_tm[:], x3.rearrange("(s p) f -> p s f", p=128), sem="dx")
    ld_x = ("dx", sch.val("dx"))
    p1_reader = {0: None, 1: None}
    p2_reader = {0: None, 1: None}
    pd_reader = {0: None, 1: None}
    gu_mm = {0: None, 1: None}
    down_mm = None
    xh_mm = None
    for th in range(2):
        wxh = [("te", xh_mm)] if xh_mm is not None else []
        sch.dma("sq", xh[:], x3T.rearrange("(t p) n -> p t n", p=128)[:, :, th * 512:(th + 1) * 512],
                sem="dxh", wait=wxh)
        ld_xh = ("dxh", sch.val("dxh"))
        for hc in range(11):
            par = hc % 2
            wld = []
            if gu_mm[par] is not None:
                wld.append(("te", gu_mm[par]))
            if hc == 0 and down_mm is not None:
                wld.append(("te", down_mm))
            sch.dma("pe", pool[:, 2 * par + 0, :, :],
                    qw1.rearrange("(t p) m -> p t m", p=128)[:, :, hc * 512:(hc + 1) * 512],
                    sem=f"dg{par}", wait=wld)
            sch.dma("pe", pool[:, 2 * par + 1, :, :],
                    qw2.rearrange("(t p) m -> p t m", p=128)[:, :, hc * 512:(hc + 1) * 512],
                    sem=f"dg{par}", wait=())
            ldg = (f"dg{par}", sch.val(f"dg{par}"))
            for mt in range(4):
                b = (hc * 4 + mt) % 2
                wmm = [ldg, ld_xh]
                if p1_reader[b] is not None:
                    wmm.append(p1_reader[b])
                for ft in range(FT):
                    sch.op("te", e_matmul(P1[b][:], pool[:, 2 * par + 0, ft, mt * 128:(mt + 1) * 128],
                                          xh[:, ft, :],
                                          start=(ft == 0), stop=(ft == FT - 1)),
                           wait=wmm if ft == 0 else (), inc="te" if ft == FT - 1 else None)
                g1 = sch.val("te")
                wmm2 = []
                if p2_reader[b] is not None:
                    wmm2.append(p2_reader[b])
                for ft in range(FT):
                    sch.op("te", e_matmul(P2[b][:], pool[:, 2 * par + 1, ft, mt * 128:(mt + 1) * 128],
                                          xh[:, ft, :],
                                          start=(ft == 0), stop=(ft == FT - 1)),
                           wait=wmm2 if ft == 0 else (), inc="te" if ft == FT - 1 else None)
                g2 = sch.val("te")
                gu_mm[par] = g2
                xh_mm = g2
                sch.op("se", e_act(sgst[:, 0:512], P1[b][:], AF.Sigmoid),
                       wait=[("te", g1), ("ve", sch.val("ve"))], inc="se")
                sv = sch.val("se")
                p1_reader[b] = ("se", sv)
                sch.op("ve", e_tt(hidT[:, hc * 4 + mt, :], sgst[:, 0:512], P2[b][:], ALU.mult),
                       wait=[("se", sv), ("te", g2)], inc="ve")
                p2_reader[b] = ("ve", sch.val("ve"))
        hid_ve = sch.val("ve")
        for dc in range(4):
            wld = [("ve", hid_ve)]
            if down_mm is not None:
                wld.append(("te", down_mm))
            if dc == 0:
                wld.append(("te", gu_mm[0] or 0))
                wld.append(("te", gu_mm[1] or 0))
            q3r = qw3.rearrange("(m p) f -> p m f", p=128)
            sch.dma("pe", qw3c[:, 0:22, :], q3r[:, 0:22, dc * 512:(dc + 1) * 512],
                    sem="dq3a", wait=wld)
            ld3a = ("dq3a", sch.val("dq3a"))
            sch.dma("pe", qw3c[:, 22:44, :], q3r[:, 22:44, dc * 512:(dc + 1) * 512],
                    sem="dq3b", wait=())
            ld3b = ("dq3b", sch.val("dq3b"))
            for t4 in range(4):
                s = th * 4 + t4
                b = (dc * 4 + t4) % 2
                wmm = [ld3a, ("ve", hid_ve)]
                if pd_reader[b] is not None:
                    wmm.append(pd_reader[b])
                for mt in range(44):
                    w_ = wmm if mt == 0 else ([ld3b] if mt == 22 and t4 == 0 else ())
                    sch.op("te", e_matmul(PD[b][:], hidT[:, mt, t4 * 128:(t4 + 1) * 128],
                                          qw3c[:, mt, :],
                                          start=(mt == 0), stop=(mt == 43)),
                           wait=w_, inc="te" if mt == 43 else None)
                down_mm = sch.val("te")
                sch.op("ve", e_tt(x_tm[:, s, dc * 512:(dc + 1) * 512], PD[b][:],
                                  x_tm[:, s, dc * 512:(dc + 1) * 512], ALU.add),
                       wait=[("te", down_mm), ld_x], inc="ve")
                pd_reader[b] = ("ve", sch.val("ve"))
    ffn_ve = sch.val("ve")
    stats_ve = emit_ln_batched(sch, b8d, x_tm, sgst.ap().rearrange("p n -> p n"),
                               ffn_ve, sch.val("se"))
    for s in range(NSLOT):
        sch.op("ve", e_ts(x_tm[:, s, :], x_tm[:, s, :], b8d["rstd8"][:, s:s + 1],
                          b8d["nmr8"][:, s:s + 1], ALU.mult, ALU.add),
               wait=[("ve", stats_ve)] if s == 0 else (), inc="ve")
    done = sch.val("ve")
    sch.dma("sq", out_o.rearrange("(s p) f -> p s f", p=128), x_tm[:], sem="so",
            wait=[("ve", done)])
    st = sch.val("so")
    sch.op("sq", lambda e, s: e.nop(), wait=[("so", st)])
    sch.build(nc, es)
    with es:
        pass
    return nc



def _bf(x):
    return np.ascontiguousarray(x, dtype=BF16NP)


def _f32(x):
    return np.ascontiguousarray(x, dtype=np.float32)


def _qpos(c):
    qp = np.zeros((128, NSLOT), dtype=np.float32)
    for s in range(NSLOT):
        qp[:, s] = gtile(c, s) * 128 + np.arange(128)
    return qp


def kernel(L0_attn_x, L0_attn_Wq, L0_attn_bq, L0_attn_Wk, L0_attn_bk,
           L0_attn_Wv, L0_attn_bv, L0_ffn_gate_w, L0_ffn_w_gate, L0_ffn_w_up,
           L0_ffn_w_down, L1_attn_W_qkv, L1_attn_W_proj, L1_ffn_w1, L1_ffn_w2,
           L1_ffn_w3):
    cores = list(range(NCORE))
    x = _f32(np.asarray(L0_attn_x)[0])
    xT = _f32(np.asarray(x).T)
    own_idx = [own_token_index(c) for c in range(NCORE)]

    w1T = _f32(np.asarray(L1_ffn_w1).T)
    w2T = _f32(np.asarray(L1_ffn_w2).T)
    w3T = _f32(np.asarray(L1_ffn_w3).T)
    w1s = w1T.reshape(NCORE, 128, 11264)
    w2s = w2T.reshape(NCORE, 128, 11264)
    w3s = w3T.reshape(NCORE, 128, 11264)
    wq = _f32(L0_attn_Wq)
    wk = _f32(L0_attn_Wk)
    wv = _f32(L0_attn_Wv)
    bqT = _f32(np.asarray(L0_attn_bq).T.reshape(128, HEADS0))
    bkT = _f32(np.asarray(L0_attn_bk).T.reshape(128, GRP0))
    bvT = _f32(np.asarray(L0_attn_bv).T.reshape(128, GRP0))
    nc = build_A1()
    res = run_bass_kernel_spmd(nc, [{
        "xT": _f32(xT[:, own_idx[c]]),
        "wq": wq, "bqT": bqT, "wk": wk, "bkT": bkT, "wv": wv, "bvT": bvT,
        "w1s": w1s[c], "w2s": w2s[c], "w3s": w3s[c],
    } for c in cores], core_ids=cores)
    a1 = [dict(r) for r in res.results]

    kf0 = np.zeros((GRP0, 128, SEQ), dtype=BF16NP)
    vf0 = np.zeros((GRP0, SEQ, HD), dtype=BF16NP)
    for c in cores:
        kf0[:, :, own_idx[c]] = a1[c]["kT_o"]
        vf0[:, own_idx[c], :] = a1[c]["v_o"]
    gwT = _f32(np.asarray(L0_ffn_gate_w).T)
    nc = build_A2()
    res = run_bass_kernel_spmd(nc, [{
        "x_own": _f32(x[own_idx[c]]),
        "qT": a1[c]["qT_o"], "kf": kf0, "vf": vf0,
        "qpos": _qpos(c), "gwT": gwT,
    } for c in cores], core_ids=cores)
    a2 = [dict(r) for r in res.results]

    x1T_full = np.zeros((DM, SEQ), dtype=BF16NP)
    cwf = np.zeros((SEQ, MOE_E), dtype=np.float32)
    for c in cores:
        x1T_full[:, own_idx[c]] = a2[c]["x1T_o"]
        cwf[own_idx[c]] = a2[c]["cw_o"]
    wsums_all = _f32(np.stack([a1[c]["wsums_o"][0] for c in cores]))
    wgt = np.asarray(L0_ffn_w_gate)
    wut = np.asarray(L0_ffn_w_up)
    wdt = np.asarray(L0_ffn_w_down)
    nc = build_B()
    res = run_bass_kernel_spmd(nc, [{
        "x1T": x1T_full,
        "cwe_f": _f32(cwf[:, [2 * c, 2 * c + 1]]),
        "wg": _f32(wgt[2 * c:2 * c + 2].transpose(0, 2, 1)),
        "wu": _f32(wut[2 * c:2 * c + 2].transpose(0, 2, 1)),
        "wdn": _f32(wdt[2 * c:2 * c + 2].transpose(0, 2, 1)),
        "wsums": wsums_all,
        "w1s": w1s[c], "w2s": w2s[c], "w3s": w3s[c],
    } for c in cores], core_ids=cores)
    b = [dict(r) for r in res.results]

    nc = build_C1a()
    res = run_bass_kernel_spmd(nc, [{
        "x1": a2[c]["x1_o"],
        "yps": np.ascontiguousarray(
            np.stack([b[p]["yp_o"][own_idx[c]] for p in cores])),
    } for c in cores], core_ids=cores)
    c1a = [dict(r) for r in res.results]

    x2T_full = np.zeros((DM, SEQ), dtype=BF16NP)
    for c in cores:
        x2T_full[:, own_idx[c]] = c1a[c]["x2T_o"]
    Wqkv = np.asarray(L1_attn_W_qkv)
    wqkv_sl = []
    for c in cores:
        per = np.zeros((2, 3, DM, HD), dtype=np.float32)
        for j in range(2):
            h = 2 * c + j
            for m in range(3):
                per[j, m] = np.asarray(Wqkv[m * DM + h * HD:m * DM + (h + 1) * HD, :].T)
        wqkv_sl.append(_f32(per))
    nc = build_C1b()
    res = run_bass_kernel_spmd(nc, [{
        "x2T": x2T_full, "wqkv": wqkv_sl[c],
    } for c in cores], core_ids=cores)
    c1b = [dict(r) for r in res.results]

    kf1 = np.concatenate([c1b[c]["kTh_o"] for c in cores], 0)
    vf1 = np.concatenate([c1b[c]["vh_o"] for c in cores], 0)
    qf1 = np.concatenate([c1b[c]["qTh_o"] for c in cores], 0)
    wpT = _f32(np.asarray(L1_attn_W_proj).T)
    nc = build_C2()
    res = run_bass_kernel_spmd(nc, [{
        "x2": c1a[c]["x2_o"],
        "qT": np.ascontiguousarray(qf1[:, :, own_idx[c]]),
        "kf": kf1, "vf": vf1, "qpos": _qpos(c), "wpT": wpT,
    } for c in cores], core_ids=cores)
    c2 = [dict(r) for r in res.results]

    qw1_full = np.concatenate([b[c]["qw1_o"].reshape(256, FFN_H) for c in cores], 0)
    qw2_full = np.concatenate([b[c]["qw2_o"].reshape(256, FFN_H) for c in cores], 0)
    qw3_full = np.concatenate([b[c]["qw3_o"].reshape(704, DM) for c in cores], 0)
    nc = build_D()
    res = run_bass_kernel_spmd(nc, [{
        "x3": c2[c]["x3_o"], "x3T": c2[c]["x3T_o"],
        "qw1": qw1_full, "qw2": qw2_full, "qw3": qw3_full,
    } for c in cores], core_ids=cores)
    d = [dict(r) for r in res.results]

    out = np.zeros((SEQ, DM), dtype=np.float32)
    for c in cores:
        out[own_idx[c]] = d[c]["out_o"]
    return out.reshape(1, SEQ, DM)


# version 14
# speedup vs baseline: 1.0975x; 1.0909x over previous
import math
import contextlib
from collections import defaultdict

import numpy as np
import ml_dtypes

import concourse.bass as bass
import concourse.mybir as mybir
from concourse.bass_utils import run_bass_kernel_spmd
from concourse.alu_op_type import AluOpType as ALU

F32 = mybir.dt.float32
BF16 = mybir.dt.bfloat16
U8 = mybir.dt.uint8
AF = mybir.ActivationFunctionType
AX = mybir.AxisListType

NCORE = 8
SEQ = 8192
DM = 2048
NTILE = SEQ // 128
NSLOT = 8
FT = DM // 128
HEADS0, GRP0, HD = 16, 4, 128
HEADS1 = 16
MOE_E, MOE_D, MOE_K = 16, 1408, 6
FFN_H = 5632
ISQ = 1.0 / math.sqrt(HD)
CAPS = [s // 2 * 2 + 1 + (s % 2) for s in range(NSLOT)]
BF16NP = ml_dtypes.bfloat16


def gtile(c, s):
    m = s // 2
    return 16 * m + (c if s % 2 == 0 else 15 - c)


def own_token_index(c):
    idx = []
    for s in range(NSLOT):
        g = gtile(c, s)
        idx.extend(range(g * 128, (g + 1) * 128))
    return np.array(idx, dtype=np.int64)


ENGINES = ("te", "ve", "se", "pe", "sq")


class Sch:
    def __init__(self):
        self.q = {e: [] for e in ENGINES}
        self.cnt = defaultdict(int)
        self.cnt["te"] = self.cnt["ve"] = self.cnt["se"] = self.cnt["pe"] = 0

    def op(self, eng, emit, wait=(), inc=None, inc_n=1):
        item = {"emit": emit, "wait": tuple(wait), "inc": inc, "inc_n": inc_n}
        self.q[eng].append(item)
        if inc is not None:
            self.cnt[inc] += inc_n
        return self.cnt[inc] if inc is not None else None

    def val(self, sem):
        return self.cnt[sem]

    def dma(self, queue, out, in_, sem, wait=(), **kw):
        def emit(eng, sems):
            return eng.dma_start(out=out, in_=in_, **kw)
        return self.op(queue, emit, wait=wait, inc=sem, inc_n=16)

    def drain(self, eng):
        self.op(eng, lambda e, s: e.drain())

    def build(self, nc, es):
        sems = {name: es.enter_context(nc.semaphore(f"sem_{name}"))
                for name in self.cnt.keys()}
        block = es.enter_context(nc.Block())
        engobjs = {"te": nc.tensor, "ve": nc.vector, "se": nc.scalar,
                   "pe": nc.gpsimd, "sq": nc.sync}

        def section(eng):
            def body(engine):
                for item in self.q[eng]:
                    for (s, v) in item["wait"]:
                        engine.wait_ge(sems[s], v)
                    instr = item["emit"](engine, sems)
                    if item["inc"] is not None:
                        assert instr is not None, f"inc on op with no instruction ({eng})"
                        instr.then_inc(sems[item["inc"]], item["inc_n"])
            return body

        for eng in ENGINES:
            if self.q[eng]:
                getattr(block, {"te": "tensor", "ve": "vector", "se": "scalar",
                                "pe": "gpsimd", "sq": "sync"}[eng])(section(eng))



def e_matmul(out, lhsT, rhs, start, stop):
    return lambda e, s: e.matmul(out, lhsT, rhs, start=start, stop=stop)


def e_transpose(out, in_, ident):
    return lambda e, s: e.transpose(out, in_, ident)


def e_act(out, in_, func, bias=None, scale=None, accum_out=None):
    def f(e, s):
        kw = {}
        if bias is not None:
            kw["bias"] = bias
        if scale is not None:
            kw["scale"] = scale
        if accum_out is not None:
            kw["accum_out"] = accum_out
        return e.activation(out=out, in_=in_, func=func, **kw)
    return f


def e_ts(out, in0, s1, s2=None, op0=ALU.mult, op1=None):
    def f(e, s):
        kw = dict(out=out, in0=in0, scalar1=s1, scalar2=s2, op0=op0)
        if op1 is not None:
            kw["op1"] = op1
        return e.tensor_scalar(**kw)
    return f


def e_tt(out, in0, in1, op):
    return lambda e, s: e.tensor_tensor(out=out, in0=in0, in1=in1, op=op)


def e_stt(out, in0, scalar, in1, op0, op1):
    return lambda e, s: e.scalar_tensor_tensor(out=out, in0=in0, scalar=scalar,
                                               in1=in1, op0=op0, op1=op1)


def e_red(out, in_, axis, op="sum", neg=False, absval=False):
    def f(e, s):
        if op == "sum":
            return e.reduce_sum(out=out, in_=in_, axis=axis, negate=neg,
                                apply_absolute_value=absval)
        if op == "max":
            return e.reduce_max(out=out, in_=in_, axis=axis, negate=neg,
                                apply_absolute_value=absval)
        return e.tensor_reduce(out=out, in_=in_, axis=axis, op=op)
    return f


def e_copy(out, in_):
    def f(e, s):
        if hasattr(e, "tensor_copy"):
            return e.tensor_copy(out=out, in_=in_)
        return e.copy(out=out, in_=in_)
    return f


def e_memset(ap, v):
    return lambda e, s: e.memset(ap, v)


def e_recip(out, in_):
    return lambda e, s: e.reciprocal(out=out, in_=in_)


def e_iota(out, pattern, base=0, cm=0):
    return lambda e, s: e.iota(out, pattern=pattern, base=base, channel_multiplier=cm,
                               allow_small_or_imprecise_dtypes=True)



def build_A1():
    nc = bass.Bass("TRN2", target_bir_lowering=False)
    xT = nc.dram_tensor("xT", [DM, 1024], F32, kind="ExternalInput").ap()
    wq = nc.dram_tensor("wq", [HEADS0, DM, HD], F32, kind="ExternalInput").ap()
    bqT = nc.dram_tensor("bqT", [128, HEADS0], F32, kind="ExternalInput").ap()
    wk = nc.dram_tensor("wk", [GRP0, DM, HD], F32, kind="ExternalInput").ap()
    bkT = nc.dram_tensor("bkT", [128, GRP0], F32, kind="ExternalInput").ap()
    wv = nc.dram_tensor("wv", [GRP0, DM, HD], F32, kind="ExternalInput").ap()
    bvT = nc.dram_tensor("bvT", [128, GRP0], F32, kind="ExternalInput").ap()
    wsl = [nc.dram_tensor(f"w{j}s", [128, 11264], F32, kind="ExternalInput").ap()
           for j in (1, 2, 3)]
    qT_o = nc.dram_tensor("qT_o", [HEADS0, 128, 1024], BF16, kind="ExternalOutput").ap()
    kT_o = nc.dram_tensor("kT_o", [GRP0, 128, 1024], BF16, kind="ExternalOutput").ap()
    v_o = nc.dram_tensor("v_o", [GRP0, 1024, HD], BF16, kind="ExternalOutput").ap()
    wsums_o = nc.dram_tensor("wsums_o", [1, 4], F32, kind="ExternalOutput").ap()

    es = contextlib.ExitStack()
    sb = lambda name, shape, dt=BF16: es.enter_context(nc.sbuf_tensor(name, shape, dt))
    ps = lambda name, shape, dt=F32: es.enter_context(nc.psum_tensor(name, shape, dt))

    xfm = sb("xfm", [128, FT, 1024])
    wst = sb("wst", [128, 2, FT, HD])
    bq_sb = sb("bq_sb", [128, HEADS0], F32)
    bqs_sb = sb("bqs_sb", [128, HEADS0], F32)
    bk_sb = sb("bk_sb", [128, GRP0], F32)
    bv_sb = sb("bv_sb", [128, GRP0], F32)
    ostage = sb("ostage", [128, 4, 1024])
    vstage = sb("vstage", [128, 2, 8, HD])
    ident = sb("ident", [128, 128])
    iot = sb("iot", [128, 128], F32)
    wbuf = sb("wbuf", [128, 2, 2816], F32)
    red1 = sb("red1", [128, 22], F32)
    redc = sb("redc", [128, 12], F32)
    red2 = sb("red2", [128, 1], F32)
    ones_col = sb("ones_col", [128, 1], F32)
    wsums_sb = sb("wsums_sb", [1, 4], F32)
    pa = ps("pa", [128, 512])
    pb = ps("pb", [128, 512])
    ptr = es.enter_context(nc.psum_tensor("ptr", [128, 512], BF16))
    ptr2 = es.enter_context(nc.psum_tensor("ptr2", [128, 512], BF16))
    pss = ps("pss", [1, 4])

    sch = Sch()
    sch.dma("pe", xfm[:], xT.rearrange("(t p) n -> p t n", p=128), sem="dx")
    ld_x = ("dx", sch.val("dx"))
    sch.dma("sq", bq_sb[:], bqT[:], sem="db")
    sch.dma("sq", bk_sb[:], bkT[:], sem="db")
    sch.dma("sq", bv_sb[:], bvT[:], sem="db")
    ld_b = ("db", sch.val("db"))
    sch.op("pe", e_iota(iot[:], [[-1, 128]], 0, 1), inc="pe")
    sch.op("ve", e_ts(ident[:], iot[:], 0.0, None, ALU.is_equal),
           wait=[("pe", sch.val("pe"))], inc="ve")
    id_ready = sch.val("ve")
    sch.op("ve", e_ts(bqs_sb[:], bq_sb[:], ISQ), wait=[ld_b], inc="ve")
    bq_ready = sch.val("ve")

    units = ([("q", h) for h in range(HEADS0)] + [("k", g) for g in range(GRP0)]
             + [("v", g) for g in range(GRP0)])
    pp = [pa, pb]
    wld = {}
    st_of_stage = {}
    tr_of_vstage = {}
    ev_mm = {}
    se_evacs = 0
    ve_evacs = 0

    def load_unit(u, wait):
        kind, idx = units[u]
        srct = {"q": wq, "k": wk, "v": wv}[kind]
        sem = f"dw{u % 2}"
        sch.dma("pe", wst[:, u % 2, :, :],
                srct.rearrange("h (t p) d -> p h t d", p=128)[:, idx, :, :],
                sem=sem, wait=wait)
        wld[u] = (sem, sch.val(sem))

    load_unit(0, ())
    load_unit(1, ())
    for u, (kind, idx) in enumerate(units):
        par = u % 2
        for half in range(2):
            gidx = u * 2 + half
            bank = gidx % 2
            w = [wld[u]]
            if u == 0 and half == 0:
                w.append(ld_x)
            if gidx >= 2:
                w.append(("se", gidx - 1))
            for ft in range(FT):
                sch.op("te", e_matmul(pp[bank][:], wst[:, par, ft, :],
                                      xfm[:, ft, half * 512:(half + 1) * 512],
                                      start=(ft == 0), stop=(ft == FT - 1)),
                       wait=w if ft == 0 else (),
                       inc="te" if ft == FT - 1 else None)
            mmv = sch.val("te")
            if half == 1:
                ev_mm[u] = mmv
                if u + 2 < len(units):
                    load_unit(u + 2, [("te", mmv)])
            slot = u % 4
            wev = [("te", mmv), ld_b]
            if kind == "q":
                wev.append(("ve", bq_ready))
            if half == 0 and st_of_stage:
                wev.append(("so", sch.val("so")))
            bias_t = {"q": bqs_sb, "k": bk_sb, "v": bv_sb}[kind]
            bias = bias_t[:, idx:idx + 1]
            scale = ISQ if kind == "q" else None
            sch.op("se", e_act(ostage[:, slot, half * 512:(half + 1) * 512],
                               pp[bank][:], AF.Identity, bias=bias, scale=scale),
                   inc="se", wait=wev)
            se_evacs = sch.val("se")
        if kind == "q":
            sch.dma("sq", qT_o[idx, :, :], ostage[:, u % 4, :], sem="so",
                    wait=[("se", se_evacs)])
            st_of_stage[u % 4] = sch.val("so")
        elif kind == "k":
            sch.dma("sq", kT_o[idx, :, :], ostage[:, u % 4, :], sem="so",
                    wait=[("se", se_evacs)])
            st_of_stage[u % 4] = sch.val("so")
        else:
            for j4 in range(2):
                ptrb = [ptr, ptr2][j4]
                wtr = [("se", se_evacs), ("ve", id_ready)]
                prev_ev = tr_of_vstage.get(j4)
                if prev_ev is not None:
                    wtr.append(("ve", prev_ev))
                for jj in range(4):
                    j = j4 * 4 + jj
                    sch.op("te", e_transpose(ptrb[:, jj * 128:(jj + 1) * 128],
                                             ostage[:, u % 4, j * 128:(j + 1) * 128],
                                             ident[:]),
                           wait=wtr if jj == 0 else (),
                           inc="te" if jj == 3 else None)
                trv = sch.val("te")
                vpar = idx % 2
                wev2 = [("te", trv)]
                if st_of_stage:
                    wev2.append(("so", sch.val("so")))
                sch.op("ve", e_copy(vstage[:, vpar, j4 * 4:(j4 + 1) * 4, :], ptrb[:]),
                       wait=wev2, inc="ve")
                tr_of_vstage[j4] = sch.val("ve")
            ve_evacs = sch.val("ve")
            sch.dma("sq", v_o[idx, :, :].rearrange("(j p) d -> p j d", p=128),
                    vstage[:, idx % 2, :, :], sem="so", wait=[("ve", ve_evacs)])
            st_of_stage[f"v{idx % 2}"] = sch.val("so")

    sch.op("ve", e_memset(ones_col[:], 1.0), inc="ve")
    prev_red = None
    for j in range(3):
        for ch in range(4):
            k = j * 4 + ch
            par = k % 2
            w = []
            if k >= 2:
                w.append(("ve", prev_red2[k - 2]))
            sch.dma("sq", wbuf[:, par, :], wsl[j][:, ch * 2816:(ch + 1) * 2816],
                    sem=f"dc{par}", wait=w)
            ldv = (f"dc{par}", sch.val(f"dc{par}"))
            if k == 0:
                prev_red2 = {}
            sch.op("ve", e_red(red1[:], wbuf[:, par, :].rearrange("p (a b) -> p a b", b=128),
                               AX.X, "sum", absval=True), wait=[ldv], inc="ve")
            sch.drain("ve")
            sch.op("ve", e_red(redc[:, k:k + 1], red1[:], AX.X, "sum"), inc="ve")
            sch.drain("ve")
            prev_red2[k] = sch.val("ve")
    for j in range(3):
        sch.op("ve", e_red(red2[:], redc[:, j * 4:(j + 1) * 4], AX.X, "sum"),
               wait=[("te", sch.val("te"))] if j > 0 else (), inc="ve")
        sch.drain("ve")
        rv = sch.val("ve")
        sch.op("te", e_matmul(pss[0:1, j:j + 1], ones_col[:], red2[:],
                              start=True, stop=True),
               wait=[("ve", rv)], inc="te")
    sumv = sch.val("te")
    sch.op("ve", e_memset(wsums_sb[:], 0.0), inc="ve")
    sch.drain("ve")
    sch.op("ve", e_copy(wsums_sb[0:1, 0:3], pss[0:1, 0:3]), wait=[("te", sumv)], inc="ve")
    sch.dma("sq", wsums_o[:], wsums_sb[:], sem="so", wait=[("ve", sch.val("ve"))])
    st = sch.val("so")
    sch.op("sq", lambda e, s: e.nop(), wait=[("so", st)])

    sch.build(nc, es)
    with es:
        pass
    return nc



def emit_attention(sch, cfg):
    (mode, n_outer, heads_of, kf, vf, kg_sb, vg_sb, q_sb, p_st, tmask,
     tm_ready, s_sb, onec, l_st, l_tot, rl, pt_sb, ident, id_ready, SP,
     PTR, PV, x_tm, yq, ld_q, ld_x, ytmp) = cfg
    sp_reader = {0: None, 1: None}
    pv_reader = {0: None, 1: None}
    ptr_evac = {0: None, 1: None}
    pst_tr = {}
    lst_free = {0: None, 1: None}
    rl_read = {0: None, 1: None}
    ltot_free = {0: None, 1: None}
    s_sb_free = None
    chunk_ctr = [0]
    unit_ctr = [0]
    kv_reads = [0]

    def make_pass2(u):
        cap, lpar, pvb, s, h, exp_se = (u["cap"], u["lpar"], u["pvb"], u["s"],
                                        u["h"], u["exp_se"])
        kpar = u["kpar"]
        wlt = [("se", exp_se[cap - 1])]
        if ltot_free[lpar] is not None:
            wlt.append(ltot_free[lpar])
        sch.op("ve", e_red(l_tot[:, lpar:lpar + 1],
                           l_st[:, lpar * 8:lpar * 8 + cap], AX.X, "sum"),
               wait=wlt, inc="ve")
        lst_free[lpar] = ("ve", sch.val("ve"))
        sch.drain("ve")
        wrc = []
        if rl_read[lpar] is not None:
            wrc.append(rl_read[lpar])
        sch.op("ve", e_recip(rl[:, lpar:lpar + 1], l_tot[:, lpar:lpar + 1]),
               wait=wrc, inc="ve")
        sch.drain("ve")
        rl_ve = ("ve", sch.val("ve"))
        ltot_free[lpar] = rl_ve
        tr_te = {}
        ev_ve = {}
        def emit_tr(ci):
            ptrb = ci % 2
            wtr = [("ve", id_ready), ("se", exp_se[ci])]
            if ptr_evac[ptrb] is not None:
                wtr.append(ptr_evac[ptrb])
            for j in range(8):
                sch.op("te", e_transpose(
                    PTR[ptrb][:, j * 128:(j + 1) * 128],
                    p_st[:, ci * 1024 + j * 128: ci * 1024 + (j + 1) * 128],
                    ident[:]),
                    wait=wtr if j == 0 else (), inc="te" if j == 7 else None)
            tr_te[ci] = sch.val("te")
            pst_tr[ci] = ("te", tr_te[ci])
            ev_eng = "ve"
            sch.op(ev_eng, e_copy(pt_sb[:, ptrb, :], PTR[ptrb][:]),
                   wait=[("te", tr_te[ci])], inc=ev_eng)
            ev_ve[ci] = (ev_eng, sch.val(ev_eng))
        def emit_pv(ci):
            ptrb = ci % 2
            wpv = [ev_ve[ci]]
            if ci == 0 and pv_reader[pvb] is not None:
                wpv.append(pv_reader[pvb])
            for j in range(8):
                mm = e_matmul(PV[pvb], pt_sb[:, ptrb, j * 128:(j + 1) * 128],
                              vg_sb[:, kpar, ci * 8 + j, :],
                              start=(ci == 0 and j == 0),
                              stop=(ci == cap - 1 and j == 7))
                sch.op("te", mm, wait=wpv if j == 0 else (),
                       inc="te" if j == 7 else None)
            ptr_evac[ptrb] = ("te", sch.val("te"))
        emit_tr(0)
        u["tr_done"] = 0
        yield
        for ci in range(1, cap):
            emit_tr(ci)
            u["tr_done"] = ci
            emit_pv(ci - 1)
            yield
        emit_pv(cap - 1)
        u["tr_done"] = 100
        pv_done = sch.val("te")
        kv_reads[0] = pv_done
        if mode == "l0":
            sch.op("ve", e_stt(x_tm[:, s, h * 128:(h + 1) * 128], PV[pvb],
                               rl[:, lpar:lpar + 1],
                               x_tm[:, s, h * 128:(h + 1) * 128],
                               ALU.mult, ALU.add),
                   wait=[("te", pv_done), rl_ve, ld_x], inc="ve")
            pv_reader[pvb] = ("ve", sch.val("ve"))
            rl_read[lpar] = ("ve", sch.val("ve"))
        else:
            sch.op("ve", e_ts(ytmp[:, lpar, :], PV[pvb], rl[:, lpar:lpar + 1]),
                   wait=[("te", pv_done), rl_ve], inc="ve")
            pv_reader[pvb] = ("ve", sch.val("ve"))
            rl_read[lpar] = ("ve", sch.val("ve"))
            ts_ve = sch.val("ve")
            ptrb = cap % 2
            wtr2 = [("ve", ts_ve), ("ve", id_ready)]
            if ptr_evac[ptrb] is not None:
                wtr2.append(ptr_evac[ptrb])
            sch.op("te", e_transpose(PTR[ptrb][:, 0:128], ytmp[:, lpar, :],
                                     ident[:]), wait=wtr2, inc="te")
            ytr = sch.val("te")
            sch.op("ve", e_copy(yq[:, h, s * 128:(s + 1) * 128],
                                PTR[ptrb][:, 0:128]),
                   wait=[("te", ytr)], inc="ve")
            ptr_evac[ptrb] = ("ve", sch.val("ve"))

    pending = None

    def advance_pass2(upto):
        if pending is None:
            return
        u, gen = pending
        while True:
            if upto is not None and u.get("tr_done", -1) >= min(upto, u["cap"] - 1):
                return
            try:
                next(gen)
            except StopIteration:
                return

    def flush_pass2():
        nonlocal pending
        if pending is not None:
            u, gen = pending
            for _ in gen:
                pass
            pending = None

    kv_te_of_par = {0: 0, 1: 0}
    kv_ld_of_par = {}
    def stage_kv(outer):
        par = outer % 2
        wkv = [("te", kv_te_of_par[par])] if outer >= 2 else []
        sch.dma("sq", kg_sb[:, par, :], kf[outer, :, :], sem=f"dkv{par}", wait=wkv)
        sch.dma("sq", vg_sb[:, par, :, :],
                vf[outer, :, :].rearrange("(j p) d -> p j d", p=128),
                sem=f"dkv{par}", wait=wkv)
        kv_ld_of_par[outer] = (f"dkv{par}", sch.val(f"dkv{par}"))
    stage_kv(0)
    if n_outer > 1:
        stage_kv(1)
    for outer in range(n_outer):
        kpar = outer % 2
        kv_ld = kv_ld_of_par[outer]
        for h in heads_of(outer):
            for s in range(NSLOT):
                cap = CAPS[s]
                pvb = unit_ctr[0] % 2
                lpar = unit_ctr[0] % 2
                ustate = {"cap": cap, "lpar": lpar, "pvb": pvb, "s": s, "h": h,
                          "kpar": kpar}
                exp_se = {}
                ustate["exp_se"] = exp_se
                rm_ve = None
                for ci in range(cap):
                    bank = chunk_ctr[0] % 2
                    masked = (ci == cap - 1)
                    w0 = [kv_ld, ld_q] if ci == 0 else []
                    if sp_reader[bank] is not None:
                        w0.append(sp_reader[bank])
                    for half in range(2):
                        sch.op("te", e_matmul(
                            SP[bank][:, half * 512:(half + 1) * 512],
                            q_sb[:, h, s * 128:(s + 1) * 128],
                            kg_sb[:, kpar, ci * 1024 + half * 512: ci * 1024 + (half + 1) * 512],
                            start=True, stop=True),
                            wait=w0 if half == 0 else (), inc="te")
                    mmv = sch.val("te")
                    advance_pass2(ci)
                    exp_in = SP[bank][:]
                    stt_ve = None
                    if masked:
                        wstt = [("te", mmv), ("ve", tm_ready)]
                        if s_sb_free is not None:
                            wstt.append(s_sb_free)
                        sch.op("ve", e_stt(s_sb[:], tmask[:, s, :], -1e30, SP[bank][:],
                                           ALU.mult, ALU.add), wait=wstt, inc="ve")
                        sch.drain("ve")
                        stt_ve = sch.val("ve")
                        sp_reader[bank] = ("ve", stt_ve)
                        exp_in = s_sb[:]
                    wex = []
                    if masked:
                        wex.append(("ve", stt_ve))
                    else:
                        wex.append(("te", mmv))
                    if pst_tr.get(ci) is not None:
                        wex.append(pst_tr[ci])
                    if ci == 0 and lst_free[lpar] is not None:
                        wex.append(lst_free[lpar])
                    sch.op("se", e_act(p_st[:, ci * 1024:(ci + 1) * 1024], exp_in,
                                       AF.Exp,
                                       accum_out=l_st[:, lpar * 8 + ci:lpar * 8 + ci + 1]),
                           wait=wex, inc="se")
                    exp_se[ci] = sch.val("se")
                    if masked:
                        s_sb_free = ("se", exp_se[ci])
                    else:
                        sp_reader[bank] = ("se", exp_se[ci])
                    chunk_ctr[0] += 1
                flush_pass2()
                pending = (ustate, make_pass2(ustate))
                unit_ctr[0] += 1
        kv_te_of_par[kpar] = None
        flush_pass2()
        kv_te_of_par[kpar] = kv_reads[0]
        if outer + 2 < n_outer:
            stage_kv(outer + 2)
    flush_pass2()
    return sch.val("te"), sch.val("ve"), sch.val("se")



def build_A2():
    nc = bass.Bass("TRN2", target_bir_lowering=False)
    x_own = nc.dram_tensor("x_own", [1024, DM], F32, kind="ExternalInput").ap()
    qT = nc.dram_tensor("qT", [HEADS0, 128, 1024], BF16, kind="ExternalInput").ap()
    kf = nc.dram_tensor("kf", [GRP0, 128, SEQ], BF16, kind="ExternalInput").ap()
    vf = nc.dram_tensor("vf", [GRP0, SEQ, HD], BF16, kind="ExternalInput").ap()
    qpos = nc.dram_tensor("qpos", [128, NSLOT], F32, kind="ExternalInput").ap()
    gwT = nc.dram_tensor("gwT", [DM, MOE_E], F32, kind="ExternalInput").ap()
    x1_o = nc.dram_tensor("x1_o", [1024, DM], F32, kind="ExternalOutput").ap()
    x1T_o = nc.dram_tensor("x1T_o", [DM, 1024], BF16, kind="ExternalOutput").ap()
    cw_o = nc.dram_tensor("cw_o", [1024, MOE_E], F32, kind="ExternalOutput").ap()

    es = contextlib.ExitStack()
    sb = lambda name, shape, dt=BF16: es.enter_context(nc.sbuf_tensor(name, shape, dt))
    ps = lambda name, shape, dt=F32: es.enter_context(nc.psum_tensor(name, shape, dt))

    x_tm = sb("x_tm", [128, NSLOT, DM], F32)
    q_sb = sb("q_sb", [128, HEADS0, 1024])
    kg_sb = sb("kg_sb", [128, 2, SEQ])
    vg_sb = sb("vg_sb", [128, 2, 64, HD])
    p_st = sb("p_st", [128, SEQ])
    bvec = sb("bvec", [128, NSLOT], F32)
    tmask = sb("tmask", [128, NSLOT, 1024], U8)
    qpos_sb = sb("qpos_sb", [128, NSLOT], F32)
    s_sb = sb("s_sb", [128, 1024], F32)
    negm = sb("negm", [128, 2], F32)
    l_stv = sb("l_stv", [128, 16], F32)
    l_tot = sb("l_tot", [128, 2], F32)
    rl = sb("rl", [128, 2], F32)
    pt_sb = sb("pt_sb", [128, 2, 1024])
    ident = sb("ident", [128, 128])
    iot = sb("iot", [128, 128], F32)
    b8 = ln8_bufs(sb)
    onec = sb("onec", [128, 1], F32)
    x1bf = sb("x1bf", [128, DM])
    x1ts = sb("x1ts", [128, FT, 128])
    gw_sb = sb("gw_sb", [128, FT, MOE_E])
    esc_all = sb("esc_all", [128, NSLOT * MOE_E], F32)
    cur_all = sb("cur_all", [128, NSLOT * MOE_E], F32)
    tsel_all = sb("tsel_all", [128, NSLOT * MOE_E], F32)
    gmax = sb("gmax", [128, 1], F32)
    esum8 = sb("esum8", [128, NSLOT], F32)
    ers8 = sb("ers8", [128, NSLOT], F32)
    mk8 = sb("mk8", [128, NSLOT], F32)
    cw_sb = sb("cw_sb", [128, NSLOT, MOE_E], F32)

    SP = [ps("sp0", [128, 1024]), ps("sp1", [128, 1024])]
    PTR = [es.enter_context(nc.psum_tensor("ptr0", [128, 1024], BF16)),
           es.enter_context(nc.psum_tensor("ptr1", [128, 1024], BF16))]
    PVT = [ps("pv0", [128, 512]), ps("pv1", [128, 512])]
    PV = [PVT[0][:, 0:128], PVT[1][:, 0:128]]
    pgate = PVT[0][:, 128:128 + MOE_E]

    sch = Sch()
    sch.dma("pe", x_tm[:], x_own.rearrange("(s p) f -> p s f", p=128), sem="dinp")
    sch.dma("sq", q_sb[:], qT.rearrange("h p n -> p h n"), sem="dinq")
    sch.dma("sq", qpos_sb[:], qpos[:], sem="dinq")
    sch.dma("pe", gw_sb[:], gwT.rearrange("(t p) e -> p t e", p=128), sem="dinp")
    ld_p = ("dinp", sch.val("dinp"))
    ld_q2 = ("dinq", sch.val("dinq"))
    ld_x = ld_gw = ld_p
    ld_q = ld_qp = ld_q2
    sch.op("pe", e_iota(iot[:], [[-1, 128]], 0, 1), inc="pe")
    sch.op("ve", e_ts(ident[:], iot[:], 0.0, None, ALU.is_equal),
           wait=[("pe", sch.val("pe"))], inc="ve")
    id_ready = sch.val("ve")
    jvec = s_sb
    sch.op("pe", e_iota(jvec[:], [[1, 1024]], 0, 0), inc="pe")
    jv_pe = sch.val("pe")
    for s in range(NSLOT):
        sch.op("ve", e_ts(bvec[:, s:s + 1], qpos_sb[:, s:s + 1], 1.0,
                          float(-1024 * (CAPS[s] - 1)), ALU.mult, ALU.add),
               wait=[ld_qp, ("pe", jv_pe)], inc="ve")
    sch.drain("ve")
    for s in range(NSLOT):
        sch.op("ve", e_ts(tmask[:, s, :], jvec[:], bvec[:, s:s + 1], None, ALU.is_gt),
               inc="ve")
    sch.drain("ve")
    jv_ready = sch.val("ve")

    cfg = ("l0", GRP0, lambda g: range(4 * g, 4 * g + 4), kf, vf, kg_sb, vg_sb,
           q_sb, p_st, tmask, jv_ready, s_sb, onec, l_stv, l_tot,
           rl, pt_sb, ident, id_ready, SP, PTR, PV, x_tm, None, ld_q, ld_x, None)
    emit_attention(sch, cfg)
    attn_ve = sch.val("ve")
    attn_se = sch.val("se")
    ptr_evac = {0: ("ve", attn_ve), 1: ("ve", attn_ve)}

    stats_ve = emit_ln_batched(sch, b8, x_tm, x1bf[:], attn_ve, attn_se)
    rstd8, nmr8 = b8["rstd8"], b8["nmr8"]
    st_x1t = {}
    pgate_all = PVT[0][:, 128:256]
    last_x1t_te = sch.val("te")
    for s in range(NSLOT):
        xs = x_tm[:, s, :]
        wap = [("ve", stats_ve)] if s == 0 else [("se", sch.val("se"))]
        sch.op("ve", e_ts(xs, xs, rstd8[:, s:s + 1], nmr8[:, s:s + 1],
                          ALU.mult, ALU.add), wait=wap, inc="ve")
        ln_ve = sch.val("ve")
        wbf = [("ve", ln_ve), ("te", last_x1t_te), ("se", sch.val("se"))]
        sch.op("se", e_act(x1bf[:], xs, AF.Identity), wait=wbf, inc="se")
        bf_se = sch.val("se")
        for f4 in range(4):
            ptrb = f4 % 2
            wtr = [("se", bf_se), ("ve", id_ready)]
            if ptr_evac[ptrb] is not None:
                wtr.append(ptr_evac[ptrb])
            for jj in range(4):
                ft = f4 * 4 + jj
                sch.op("te", e_transpose(PTR[ptrb][:, jj * 128:(jj + 1) * 128],
                                         x1bf[:, ft * 128:(ft + 1) * 128],
                                         ident[:]),
                       wait=wtr if jj == 0 else (), inc="te" if jj == 3 else None)
            trv = sch.val("te")
            last_x1t_te = trv
            wev = [("te", trv)]
            if s > 0 and f4 == 0 and st_x1t:
                wev.append(("so", sch.val("so")))
            sch.op("ve", e_copy(x1ts[:, f4 * 4:(f4 + 1) * 4, :], PTR[ptrb][:, 0:512]),
                   wait=wev, inc="ve")
            ptr_evac[ptrb] = ("ve", sch.val("ve"))
        x1t_ve = sch.val("ve")
        sch.dma("sq", x1T_o.rearrange("(t p) (sl i) -> p t sl i", p=128, i=128)[:, :, s, :],
                x1ts[:], sem="so", wait=[("ve", x1t_ve)])
        st_x1t[s] = sch.val("so")
        wg0 = [("ve", x1t_ve), ld_gw]
        if s == 0:
            wg0 += [("ve", attn_ve), ("se", attn_se)]
        for ft in range(FT):
            sch.op("te", e_matmul(pgate_all[:, s * 16:(s + 1) * 16],
                                  x1ts[:, ft, :], gw_sb[:, ft, :],
                                  start=(ft == 0), stop=(ft == FT - 1)),
                   wait=wg0 if ft == 0 else (), inc="te" if ft == FT - 1 else None)
    gv = sch.val("te")
    esc3 = esc_all.ap().rearrange("p (s e) -> p s e", e=MOE_E)
    cur3 = cur_all.ap().rearrange("p (s e) -> p s e", e=MOE_E)
    ts3 = tsel_all.ap().rearrange("p (s e) -> p s e", e=MOE_E)
    pg3 = pgate_all.rearrange("p (s e) -> p s e", e=MOE_E)
    sch.op("ve", e_red(mk8[:], pg3, AX.X, "max"), wait=[("te", gv)], inc="ve")
    sch.drain("ve")
    mkb0 = mk8.ap().rearrange("p (s o) -> p s o", o=1).broadcast_to((128, NSLOT, MOE_E))
    sch.op("ve", e_tt(esc3, pg3, mkb0, ALU.subtract), inc="ve")
    sch.drain("ve")
    sch.op("se", e_act(esc_all[:], esc_all[:], AF.Exp),
           wait=[("ve", sch.val("ve"))], inc="se")
    sch.op("ve", e_red(esum8[:], esc3, AX.X, "sum"),
           wait=[("se", sch.val("se"))], inc="ve")
    sch.drain("ve")
    sch.op("ve", e_recip(ers8[:], esum8[:]), inc="ve")
    sch.drain("ve")
    ers_b = ers8.ap().rearrange("p (s o) -> p s o", o=1).broadcast_to((128, NSLOT, MOE_E))
    sch.op("ve", e_tt(esc3, esc3, ers_b, ALU.mult), inc="ve")
    sch.drain("ve")
    sch.op("ve", e_copy(cur_all[:], esc_all[:]), inc="ve")
    sch.drain("ve")
    for k6 in range(MOE_K):
        sch.op("ve", e_red(mk8[:], cur3, AX.X, "max"), inc="ve")
        sch.drain("ve")
        mk_b = mk8.ap().rearrange("p (s o) -> p s o", o=1).broadcast_to((128, NSLOT, MOE_E))
        if k6 < MOE_K - 1:
            sch.op("ve", e_tt(ts3, cur3, mk_b, ALU.is_ge), inc="ve")
            sch.drain("ve")
            sch.op("ve", e_stt(cur3, ts3, -2.0, cur3, ALU.mult, ALU.add), inc="ve")
            sch.drain("ve")
    sch.op("ve", e_tt(ts3, esc3, mk_b, ALU.is_ge), inc="ve")
    sch.drain("ve")
    sch.op("ve", e_tt(cw_sb.ap(), esc3, ts3, ALU.mult), inc="ve")
    sch.drain("ve")
    done_ve = sch.val("ve")
    sch.dma("sq", x1_o.rearrange("(s p) f -> p s f", p=128), x_tm[:],
            sem="so", wait=[("ve", done_ve)])
    sch.dma("sq", cw_o.rearrange("(s p) e -> p s e", p=128), cw_sb[:],
            sem="so", wait=[("ve", done_ve)])
    st = sch.val("so")
    sch.op("sq", lambda e, s: e.nop(), wait=[("so", st)])

    sch.build(nc, es)
    with es:
        pass
    return nc



def build_B():
    nc = bass.Bass("TRN2", target_bir_lowering=False)
    x1T = nc.dram_tensor("x1T", [DM, SEQ], BF16, kind="ExternalInput").ap()
    cwe_f = nc.dram_tensor("cwe_f", [SEQ, 2], F32, kind="ExternalInput").ap()
    wg = nc.dram_tensor("wg", [2, DM, MOE_D], F32, kind="ExternalInput").ap()
    wu = nc.dram_tensor("wu", [2, DM, MOE_D], F32, kind="ExternalInput").ap()
    wdn = nc.dram_tensor("wdn", [2, MOE_D, DM], F32, kind="ExternalInput").ap()
    wsums = nc.dram_tensor("wsums", [8, 4], F32, kind="ExternalInput").ap()
    wsl = [nc.dram_tensor(f"w{j}s", [128, 11264], F32, kind="ExternalInput").ap()
           for j in (1, 2, 3)]
    yp_o = nc.dram_tensor("yp_o", [SEQ, DM], BF16, kind="ExternalOutput").ap()
    qw_o = [nc.dram_tensor(f"qw{j}_o", [128, 11264], BF16, kind="ExternalOutput").ap()
            for j in (1, 2, 3)]

    es = contextlib.ExitStack()
    sb = lambda name, shape, dt=BF16: es.enter_context(nc.sbuf_tensor(name, shape, dt))
    ps = lambda name, shape, dt=F32: es.enter_context(nc.psum_tensor(name, shape, dt))

    xb = sb("xb", [128, 2, FT, 1024])
    at = sb("at", [128, 11, 1024])
    wgu = sb("wgu", [128, 3, 2, FT, 128])
    wd_sb = sb("wd_sb", [128, 11, DM])
    yp_sb = sb("yp_sb", [128, 8, DM])
    sg = sb("sg", [128, 1024])
    cwe = sb("cwe", [128, 8, 2], F32)
    ws_sb = sb("ws_sb", [8, 4], F32)
    ones8 = sb("ones8", [8, 1], F32)
    ones1r = sb("ones1r", [1, 128], F32)
    gsc = sb("gsc", [1, 1], F32)
    ginv = sb("ginv", [1, 1], F32)
    invb = sb("invb", [128, 3], F32)
    QCH = 512
    wqb = sb("wqb", [128, 2, QCH], F32)
    ubuf_t = sb("ubuf_t", [128, QCH], F32)
    sgn_t = sb("sgn_t", [128, QCH])
    msk_t = sb("msk_t", [128, QCH])
    qbuf = sb("qbuf", [128, 2, QCH])
    ubuf = ubuf_t[:]
    sgn = sgn_t[:]
    msk = msk_t[:]

    PG = [ps("pg0", [128, 512]), ps("pg1", [128, 512])]
    PU = [ps("pu0", [128, 512]), ps("pu1", [128, 512])]
    PD = [ps("pd0", [128, 512]), ps("pd1", [128, 512])]
    pb1 = ps("pb1", [128, 512])

    sch = Sch()
    MT = 11
    sch.dma("sq", ws_sb[:], wsums[:], sem="dws")
    ld_ws = ("dws", sch.val("dws"))
    sch.op("ve", e_memset(ones8[:], 1.0), inc="ve")
    sch.op("ve", e_memset(ones1r[:], 1.0), inc="ve")
    ones_ve = sch.val("ve")
    NW = {0: DM * FFN_H, 1: DM * FFN_H, 2: FFN_H * DM}
    for j in range(3):
        wbj = [("ve", ones_ve), ld_ws]
        if j > 0:
            wbj.append(("ve", sch.val("ve")))
        sch.op("te", e_matmul(pb1[0:1, 0:1], ones8[:], ws_sb[:, j:j + 1],
                              start=True, stop=True), wait=wbj, inc="te")
        sch.op("ve", e_ts(gsc[:], pb1[0:1, 0:1], 1.0 / NW[j], 1e-8, ALU.mult, ALU.add),
               wait=[("te", sch.val("te"))], inc="ve")
        sch.drain("ve")
        sch.op("ve", e_recip(ginv[:], gsc[:]), inc="ve")
        sch.drain("ve")
        gv = sch.val("ve")
        sch.op("te", e_matmul(pb1[:, 1:2], ones1r[:], ginv[:], start=True, stop=True),
               wait=[("ve", gv)], inc="te")
        sch.op("ve", e_copy(invb[:, j:j + 1], pb1[:, 1:2]),
               wait=[("te", sch.val("te"))], inc="ve")
        sch.drain("ve")
    invb_ve = sch.val("ve")
    gamma_te = sch.val("te")

    QN = 11264 // QCH
    qstate = {"k": 0, "qve": {}, "qso": {}}

    def emit_quant_chunk():
        k = qstate["k"]
        if k >= 3 * QN:
            return
        qstate["k"] += 1
        j, ch = divmod(k, QN)
        par = k % 2
        w = []
        if k >= 2:
            w.append(("ve", qstate["qve"][k - 2]))
        sch.dma("sq", wqb[:, par, :], wsl[j][:, ch * QCH:(ch + 1) * QCH],
                sem=f"dq{par}", wait=w)
        ldv = (f"dq{par}", sch.val(f"dq{par}"))
        wabs = [ldv, ("ve", invb_ve)]
        if k >= 1:
            wabs.append(("ve", qstate["qve"][k - 1]))
        sch.op("se", e_act(ubuf, wqb[:, par, :], AF.Abs,
                           scale=invb[:, j:j + 1]),
               wait=wabs + [("se", sch.val("se"))], inc="se")
        sch.op("se", e_act(sgn, wqb[:, par, :], AF.Sign),
               wait=[("se", sch.val("se"))], inc="se")
        sgn_se = sch.val("se")
        sch.op("ve", e_ts(msk, ubuf, 0.5, None, ALU.is_gt),
               wait=[("se", sgn_se)], inc="ve")
        sch.drain("ve")
        wq2 = []
        if qstate["qso"].get(par) is not None:
            wq2.append(("so", sch.val("so")))
        sch.op("ve", e_tt(qbuf[:, par, :], msk, sgn, ALU.mult),
               wait=wq2, inc="ve")
        qstate["qve"][k] = sch.val("ve")
        sch.dma("sq", qw_o[j][:, ch * QCH:(ch + 1) * QCH], qbuf[:, par, :],
                sem="so", wait=[("ve", qstate["qve"][k])])
        qstate["qso"][par] = sch.val("so")

    gu_mm = {}
    pg_reader = {}
    pd_reader = {}
    wdn_mm = 0
    last_store = 0
    xb_te2 = {}
    prev_cw_ve = 0
    pending_store = [None]

    def flush_store():
        if pending_store[0] is not None:
            pending_store[0]()
            pending_store[0] = None

    for tb in range(8):
        xpar = tb % 2
        wx = [("te", xb_te2[tb - 2])] if tb >= 2 else []
        sch.dma("sq", xb[:, xpar, :, :],
                x1T.rearrange("(t p) n -> p t n", p=128)[:, :, tb * 1024:(tb + 1) * 1024],
                sem=f"dxb{xpar}", wait=wx)
        ld_xb = (f"dxb{xpar}", sch.val(f"dxb{xpar}"))
        wcw = [("ve", prev_cw_ve)] if tb > 0 else []
        sch.dma("sq", cwe[:], cwe_f.rearrange("(b t p) e -> b p t e", p=128, t=8)[tb],
                sem="dcw", wait=wcw)
        ld_cw = ("dcw", sch.val("dcw"))
        flush_store()
        for e in range(2):
            wdn_ld = None
            for mt in range(MT):
                par = mt % 3
                wld = []
                if gu_mm.get(par) is not None:
                    wld.append(("te", gu_mm[par]))
                sch.dma("pe", wgu[:, par, 0, :, :],
                        wg.rearrange("a (t p) m -> a p t m", p=128)[e][:, :, mt * 128:(mt + 1) * 128],
                        sem=f"dg{par}", wait=wld)
                sch.dma("pe", wgu[:, par, 1, :, :],
                        wu.rearrange("a (t p) m -> a p t m", p=128)[e][:, :, mt * 128:(mt + 1) * 128],
                        sem=f"dg{par}", wait=())
                ldg = (f"dg{par}", sch.val(f"dg{par}"))
                wwd = [("te", wdn_mm)] if (tb + e > 0 and mt == 0) else []
                sch.dma("pe", wd_sb[:, mt, :],
                        wdn.rearrange("a (t p) f -> a p t f", p=128)[e][:, mt, :],
                        sem="dwd", wait=wwd)
                if mt == MT - 1:
                    wdn_ld = ("dwd", sch.val("dwd"))
                for half in range(2):
                    pgb = PG[(mt * 2 + half) % 2]
                    pub = PU[(mt * 2 + half) % 2]
                    wmm = [ldg, ld_xb]
                    if pg_reader.get((mt * 2 + half) % 2) is not None:
                        wmm.append(pg_reader[(mt * 2 + half) % 2])
                    for ft in range(FT):
                        sch.op("te", e_matmul(pgb[:], wgu[:, par, 0, ft, :],
                                              xb[:, xpar, ft, half * 512:(half + 1) * 512],
                                              start=(ft == 0), stop=(ft == FT - 1)),
                               wait=wmm if ft == 0 else (),
                               inc="te" if ft == FT - 1 else None)
                    gv = sch.val("te")
                    for ft in range(FT):
                        sch.op("te", e_matmul(pub[:], wgu[:, par, 1, ft, :],
                                              xb[:, xpar, ft, half * 512:(half + 1) * 512],
                                              start=(ft == 0), stop=(ft == FT - 1)),
                               inc="te" if ft == FT - 1 else None)
                    uv = sch.val("te")
                    gu_mm[par] = uv
                    xb_te2[tb] = uv
                    sch.op("se", e_act(sg[:, half * 512:(half + 1) * 512], pgb[:],
                                       AF.Silu), wait=[("te", gv), ("ve", sch.val("ve"))],
                           inc="se")
                    sv = sch.val("se")
                    sch.op("ve", e_tt(at[:, mt, half * 512:(half + 1) * 512],
                                      sg[:, half * 512:(half + 1) * 512], pub[:],
                                      ALU.mult),
                           wait=[("se", sv), ("te", uv)], inc="ve")
                    pg_reader[(mt * 2 + half) % 2] = ("ve", sch.val("ve"))
            at_ve = sch.val("ve")
            first = (e == 0)
            for t8 in range(8):
                for d4 in range(4):
                    pdb = PD[(t8 * 4 + d4) % 2]
                    wmm = [("ve", at_ve), wdn_ld]
                    if pd_reader.get((t8 * 4 + d4) % 2) is not None:
                        wmm.append(pd_reader[(t8 * 4 + d4) % 2])
                    for mt in range(MT):
                        sch.op("te", e_matmul(pdb[:], at[:, mt, t8 * 128:(t8 + 1) * 128],
                                              wd_sb[:, mt, d4 * 512:(d4 + 1) * 512],
                                              start=(mt == 0), stop=(mt == MT - 1)),
                               wait=wmm if mt == 0 else (),
                               inc="te" if mt == MT - 1 else None)
                    wdn_mm = sch.val("te")
                    if d4 == 0:
                        emit_quant_chunk()
                    dst = yp_sb[:, t8, d4 * 512:(d4 + 1) * 512]
                    wev = [("te", wdn_mm), ld_cw]
                    if first and t8 == 0 and d4 == 0 and tb > 0:
                        wev.append(("so", last_store))
                    if first:
                        sch.op("ve", e_ts(dst, pdb[:], cwe[:, t8, 0:1]),
                               wait=wev, inc="ve")
                    else:
                        sch.op("ve", e_stt(dst, pdb[:], cwe[:, t8, 1:2], dst,
                                           ALU.mult, ALU.add),
                               wait=wev, inc="ve")
                    pd_reader[(t8 * 4 + d4) % 2] = ("ve", sch.val("ve"))
        prev_cw_ve = sch.val("ve")
        cwv = prev_cw_ve
        tb_ = tb
        def mk_store(tb_=tb_, cwv=cwv):
            def go():
                sch.dma("sq", yp_o[tb_ * 1024:(tb_ + 1) * 1024, :]
                        .rearrange("(t p) f -> p t f", p=128),
                        yp_sb[:], sem="so", wait=[("ve", cwv)])
            return go
        pending_store[0] = mk_store()
        if tb == 7:
            flush_store()
        last_store = sch.val("so") + 16 if tb < 7 else sch.val("so")

    while qstate["k"] < 3 * QN:
        emit_quant_chunk()
    st = sch.val("so")
    sch.op("sq", lambda e, s: e.nop(), wait=[("so", st)])

    sch.build(nc, es)
    with es:
        pass
    return nc



def emit_ln_rows(sch, bufs, xs, wait_ve, wait_se=None):
    s1, sqs, mean, ex2, m2, var, sd, rstd, nmr_t, nmr, jm = bufs
    w0 = [("ve", wait_ve)]
    w1 = [("ve", wait_ve)] + ([("se", wait_se)] if wait_se else [])
    sch.op("ve", e_red(s1[:], xs, AX.X, "sum"), wait=w0, inc="ve")
    sch.op("se", e_act(jm[:], xs, AF.Square, accum_out=sqs[:]), wait=w1, inc="se")
    sq_se = sch.val("se")
    sch.drain("ve")
    sch.op("ve", e_ts(mean[:], s1[:], 1.0 / DM), inc="ve")
    sch.op("ve", e_ts(ex2[:], sqs[:], 1.0 / DM, 1e-5, ALU.mult, ALU.add),
           wait=[("se", sq_se)], inc="ve")
    sch.drain("ve")
    sch.op("ve", e_tt(m2[:], mean[:], mean[:], ALU.mult), inc="ve")
    sch.drain("ve")
    sch.op("ve", e_tt(var[:], ex2[:], m2[:], ALU.subtract), inc="ve")
    sch.drain("ve")
    sch.op("se", e_act(sd[:], var[:], AF.Sqrt), wait=[("ve", sch.val("ve"))], inc="se")
    sch.op("ve", e_recip(rstd[:], sd[:]), wait=[("se", sch.val("se"))], inc="ve")
    sch.drain("ve")
    sch.op("ve", e_tt(nmr_t[:], mean[:], rstd[:], ALU.mult), inc="ve")
    sch.drain("ve")
    sch.op("ve", e_ts(nmr[:], nmr_t[:], -1.0), inc="ve")
    sch.drain("ve")
    sch.op("ve", e_ts(xs, xs, rstd[:], nmr[:], ALU.mult, ALU.add), inc="ve")
    return sch.val("ve")


def ln8_bufs(sb):
    names = ["s1a", "sqa", "mean8", "ex28", "m28", "var8", "sd8", "rstd8",
             "nmrt8", "nmr8"]
    return {n: sb("l8_" + n, [128, NSLOT], F32) for n in names}


def emit_ln_batched(sch, b8, x_tm, jm_bf, wait_ve, wait_se):
    for s in range(NSLOT):
        sch.op("ve", e_red(b8["s1a"][:, s:s + 1], x_tm[:, s, :], AX.X, "sum"),
               wait=[("ve", wait_ve)] if s == 0 else (), inc="ve")
        sch.op("se", e_act(jm_bf, x_tm[:, s, :], AF.Square,
                           accum_out=b8["sqa"][:, s:s + 1]),
               wait=([("se", wait_se)] if s == 0 else [("se", sch.val("se"))])
               + [("ve", wait_ve)], inc="se")
    sq_se = sch.val("se")
    sch.drain("ve")
    sch.op("ve", e_ts(b8["mean8"][:], b8["s1a"][:], 1.0 / DM), inc="ve")
    sch.op("ve", e_ts(b8["ex28"][:], b8["sqa"][:], 1.0 / DM, 1e-5, ALU.mult, ALU.add),
           wait=[("se", sq_se)], inc="ve")
    sch.drain("ve")
    sch.op("ve", e_tt(b8["m28"][:], b8["mean8"][:], b8["mean8"][:], ALU.mult), inc="ve")
    sch.drain("ve")
    sch.op("ve", e_tt(b8["var8"][:], b8["ex28"][:], b8["m28"][:], ALU.subtract), inc="ve")
    sch.drain("ve")
    sch.op("se", e_act(b8["sd8"][:], b8["var8"][:], AF.Sqrt),
           wait=[("ve", sch.val("ve"))], inc="se")
    sch.op("ve", e_recip(b8["rstd8"][:], b8["sd8"][:]),
           wait=[("se", sch.val("se"))], inc="ve")
    sch.drain("ve")
    sch.op("ve", e_tt(b8["nmrt8"][:], b8["mean8"][:], b8["rstd8"][:], ALU.mult), inc="ve")
    sch.drain("ve")
    sch.op("ve", e_ts(b8["nmr8"][:], b8["nmrt8"][:], -1.0), inc="ve")
    sch.drain("ve")
    return sch.val("ve")


def ln_bufs(sb, jm=None):
    names = ["s1", "sqs", "mean", "ex2", "m2", "var", "sd", "rstd", "nmr_t", "nmr"]
    out = [sb("ln_" + n, [128, 1], F32) for n in names]
    out.append(sb("ln_jm", [128, DM], F32) if jm is None else jm)
    return out


def emit_transpose_out(sch, nc_objs, s, xs_bf_getter, dram_out, PTR, ident,
                       id_ready, x1ts, ptr_evac, store_sem="so", extra_wait=None):
    for f4 in range(4):
        ptrb = f4 % 2
        wtr = [("ve", id_ready)]
        if extra_wait is not None:
            wtr.append(extra_wait)
        if ptr_evac[ptrb] is not None:
            wtr.append(ptr_evac[ptrb])
        for jj in range(4):
            ft = f4 * 4 + jj
            sch.op("te", e_transpose(PTR[ptrb][:, jj * 128:(jj + 1) * 128],
                                     xs_bf_getter(ft), ident[:]),
                   wait=wtr if jj == 0 else (), inc="te" if jj == 3 else None)
        trv = sch.val("te")
        wev = [("te", trv)]
        if sch.val(store_sem) > 0:
            wev.append((store_sem, sch.val(store_sem)))
        sch.op("ve", e_copy(x1ts[:, f4 * 4:(f4 + 1) * 4, :], PTR[ptrb][:, 0:512]),
               wait=wev, inc="ve")
        ptr_evac[ptrb] = ("ve", sch.val("ve"))
    ev = sch.val("ve")
    sch.dma("sq", dram_out.rearrange("(t p) (sl i) -> p t sl i", p=128, i=128)[:, :, s, :],
            x1ts[:], sem=store_sem, wait=[("ve", ev)])
    return ptr_evac


def build_C1a():
    nc = bass.Bass("TRN2", target_bir_lowering=False)
    x1 = nc.dram_tensor("x1", [1024, DM], F32, kind="ExternalInput").ap()
    yps = nc.dram_tensor("yps", [8, 1024, DM], BF16, kind="ExternalInput").ap()
    x2_o = nc.dram_tensor("x2_o", [1024, DM], F32, kind="ExternalOutput").ap()
    x2T_o = nc.dram_tensor("x2T_o", [DM, 1024], BF16, kind="ExternalOutput").ap()

    es = contextlib.ExitStack()
    sb = lambda name, shape, dt=BF16: es.enter_context(nc.sbuf_tensor(name, shape, dt))
    ps = lambda name, shape, dt=F32: es.enter_context(nc.psum_tensor(name, shape, dt))
    x_tm = sb("x_tm", [128, NSLOT, DM], F32)
    ypb = sb("ypb", [128, 2, NSLOT, DM])
    b8c = ln8_bufs(sb)
    xbf = sb("xbf", [128, DM])
    x1ts = sb("x1ts", [128, FT, 128])
    ident = sb("ident", [128, 128])
    iot = sb("iot", [128, 128], F32)
    PTR = [es.enter_context(nc.psum_tensor("ptr0", [128, 512], BF16)),
           es.enter_context(nc.psum_tensor("ptr1", [128, 512], BF16))]

    sch = Sch()
    sch.dma("pe", x_tm[:], x1.rearrange("(s p) f -> p s f", p=128), sem="dx")
    ld_x = ("dx", sch.val("dx"))
    sch.op("pe", e_iota(iot[:], [[-1, 128]], 0, 1), inc="pe")
    sch.op("ve", e_ts(ident[:], iot[:], 0.0, None, ALU.is_equal),
           wait=[("pe", sch.val("pe"))], inc="ve")
    id_ready = sch.val("ve")
    add_ve = {}
    for p in range(8):
        par = p % 2
        w = [("ve", add_ve[p - 2])] if p >= 2 else []
        sch.dma("sq", ypb[:, par, :, :], yps[p].rearrange("(s q) f -> q s f", q=128),
                sem=f"dy{par}", wait=w)
        ldv = (f"dy{par}", sch.val(f"dy{par}"))
        wadd = [ldv, ld_x] if p < 2 else [ldv]
        if p > 0:
            sch.drain("ve")
        sch.op("ve", e_tt(x_tm.ap().rearrange("p s f -> p (s f)"),
                          ypb[:, par, :, :].rearrange("p s f -> p (s f)"),
                          x_tm.ap().rearrange("p s f -> p (s f)"), ALU.add),
               wait=wadd, inc="ve")
        add_ve[p] = sch.val("ve")
    sum_ve = sch.val("ve")
    stats_ve = emit_ln_batched(sch, b8c, x_tm, xbf[:], sum_ve, sch.val("se"))
    ptr_evac = {0: None, 1: None}
    for s in range(NSLOT):
        xs = x_tm[:, s, :]
        wap = [("ve", stats_ve)] if s == 0 else [("se", sch.val("se"))]
        sch.op("ve", e_ts(xs, xs, b8c["rstd8"][:, s:s + 1], b8c["nmr8"][:, s:s + 1],
                          ALU.mult, ALU.add), wait=wap, inc="ve")
        ln_ve = sch.val("ve")
        wbf = [("ve", ln_ve), ("se", sch.val("se"))]
        if s > 0:
            wbf.append(("te", sch.val("te")))
        sch.op("se", e_act(xbf[:], xs, AF.Identity), wait=wbf, inc="se")
        bf_se = sch.val("se")
        ptr_evac = emit_transpose_out(
            sch, None, s, lambda ft: xbf[:, ft * 128:(ft + 1) * 128],
            x2T_o, PTR, ident, id_ready, x1ts, ptr_evac,
            extra_wait=("se", bf_se))
    done = sch.val("ve")
    sch.dma("sq", x2_o.rearrange("(s p) f -> p s f", p=128), x_tm[:], sem="so",
            wait=[("ve", done)])
    st = sch.val("so")
    sch.op("sq", lambda e, s: e.nop(), wait=[("so", st)])
    sch.build(nc, es)
    with es:
        pass
    return nc


def max_wait(a, b):
    if a is None:
        return b
    if b is None:
        return a
    if a[0] == b[0]:
        return (a[0], max(a[1], b[1]))
    return b



def build_C1b():
    nc = bass.Bass("TRN2", target_bir_lowering=False)
    x2T = nc.dram_tensor("x2T", [DM, SEQ], BF16, kind="ExternalInput").ap()
    wqkv = nc.dram_tensor("wqkv", [2, 3, DM, HD], F32, kind="ExternalInput").ap()
    qTh_o = nc.dram_tensor("qTh_o", [2, 128, SEQ], BF16, kind="ExternalOutput").ap()
    kTh_o = nc.dram_tensor("kTh_o", [2, 128, SEQ], BF16, kind="ExternalOutput").ap()
    vh_o = nc.dram_tensor("vh_o", [2, SEQ, HD], BF16, kind="ExternalOutput").ap()

    es = contextlib.ExitStack()
    sb = lambda name, shape, dt=BF16: es.enter_context(nc.sbuf_tensor(name, shape, dt))
    ps = lambda name, shape, dt=F32: es.enter_context(nc.psum_tensor(name, shape, dt))
    xb = sb("xb", [128, 2, FT, 1024])
    wsb = sb("wsb", [128, 2, 3, FT, HD])
    ost = sb("ost", [128, 6, 1024])
    vst = sb("vst", [128, 2, 8, HD])
    ident = sb("ident", [128, 128])
    iot = sb("iot", [128, 128], F32)
    pa = ps("pa", [128, 512])
    pb = ps("pb", [128, 512])
    PTR = [es.enter_context(nc.psum_tensor("ptr0", [128, 512], BF16)),
           es.enter_context(nc.psum_tensor("ptr1", [128, 512], BF16))]

    sch = Sch()
    sch.dma("pe", wsb[:], wqkv.rearrange("a b (t p) d -> p a b t d", p=128), sem="dw")
    ld_w = ("dw", sch.val("dw"))
    sch.op("pe", e_iota(iot[:], [[-1, 128]], 0, 1), inc="pe")
    sch.op("ve", e_ts(ident[:], iot[:], 0.0, None, ALU.is_equal),
           wait=[("pe", sch.val("pe"))], inc="ve")
    id_ready = sch.val("ve")
    xb_te = {}
    pp = [pa, pb]
    gidx = 0
    tr_ve = {0: None, 1: None}
    for tc in range(8):
        par = tc % 2
        w = [("te", xb_te[tc - 2])] if tc >= 2 else []
        sch.dma("sq", xb[:, par, :, :],
                x2T.rearrange("(t p) n -> p t n", p=128)[:, :, tc * 1024:(tc + 1) * 1024],
                sem=f"dx{par}", wait=w)
        ldx = (f"dx{par}", sch.val(f"dx{par}"))
        for hj in range(2):
            for mat in range(3):
                unit = tc * 6 + hj * 3 + mat
                slot = unit % 6
                for half in range(2):
                    bank = gidx % 2
                    wmm = [ldx, ld_w]
                    if gidx >= 2:
                        wmm.append(("se", gidx - 1))
                    for ft in range(FT):
                        sch.op("te", e_matmul(pp[bank][:], wsb[:, hj, mat, ft, :],
                                              xb[:, par, ft, half * 512:(half + 1) * 512],
                                              start=(ft == 0), stop=(ft == FT - 1)),
                               wait=wmm if ft == 0 else (),
                               inc="te" if ft == FT - 1 else None)
                    mmv = sch.val("te")
                    xb_te[tc] = mmv
                    wev = [("te", mmv)]
                    if half == 0 and sch.val("so") > 0:
                        wev.append(("so", sch.val("so")))
                    sch.op("se", e_act(ost[:, slot, half * 512:(half + 1) * 512],
                                       pp[bank][:], AF.Identity,
                                       scale=ISQ if mat == 0 else None),
                           wait=wev, inc="se")
                    gidx += 1
                ev = sch.val("se")
                if mat == 0:
                    sch.dma("se", qTh_o[hj, :, tc * 1024:(tc + 1) * 1024],
                            ost[:, slot, :], sem="so", wait=[("se", ev)])
                elif mat == 1:
                    sch.dma("se", kTh_o[hj, :, tc * 1024:(tc + 1) * 1024],
                            ost[:, slot, :], sem="so", wait=[("se", ev)])
                else:
                    for j4 in range(2):
                        ptrb = j4
                        wtr = [("se", ev), ("ve", id_ready)]
                        if tr_ve[ptrb] is not None:
                            wtr.append(tr_ve[ptrb])
                        for jj in range(4):
                            j = j4 * 4 + jj
                            sch.op("te", e_transpose(
                                PTR[ptrb][:, jj * 128:(jj + 1) * 128],
                                ost[:, slot, j * 128:(j + 1) * 128], ident[:]),
                                wait=wtr if jj == 0 else (),
                                inc="te" if jj == 3 else None)
                        trv = sch.val("te")
                        wev2 = [("te", trv)]
                        if sch.val("so") > 0:
                            wev2.append(("so", sch.val("so")))
                        sch.op("ve", e_copy(vst[:, j4, 0:4, :], PTR[ptrb][:]),
                               wait=wev2, inc="ve")
                        tr_ve[ptrb] = ("ve", sch.val("ve"))
                        sch.dma("se",
                                vh_o[hj, tc * 1024 + j4 * 512: tc * 1024 + (j4 + 1) * 512, :]
                                .rearrange("(j p) d -> p j d", p=128),
                                vst[:, j4, 0:4, :], sem="so",
                                wait=[("ve", sch.val("ve"))])
    st = sch.val("so")
    sch.op("sq", lambda e, s: e.nop(), wait=[("so", st)])
    sch.build(nc, es)
    with es:
        pass
    return nc



def build_C2():
    nc = bass.Bass("TRN2", target_bir_lowering=False)
    x2 = nc.dram_tensor("x2", [1024, DM], F32, kind="ExternalInput").ap()
    qT = nc.dram_tensor("qT", [HEADS1, 128, 1024], BF16, kind="ExternalInput").ap()
    kf = nc.dram_tensor("kf", [HEADS1, 128, SEQ], BF16, kind="ExternalInput").ap()
    vf = nc.dram_tensor("vf", [HEADS1, SEQ, HD], BF16, kind="ExternalInput").ap()
    qpos = nc.dram_tensor("qpos", [128, NSLOT], F32, kind="ExternalInput").ap()
    wpT = nc.dram_tensor("wpT", [DM, DM], F32, kind="ExternalInput").ap()
    x3_o = nc.dram_tensor("x3_o", [1024, DM], F32, kind="ExternalOutput").ap()
    x3T_o = nc.dram_tensor("x3T_o", [DM, 1024], BF16, kind="ExternalOutput").ap()

    es = contextlib.ExitStack()
    sb = lambda name, shape, dt=BF16: es.enter_context(nc.sbuf_tensor(name, shape, dt))
    ps = lambda name, shape, dt=F32: es.enter_context(nc.psum_tensor(name, shape, dt))
    x_tm = sb("x_tm", [128, NSLOT, DM], F32)
    yq = sb("yq", [128, HEADS1, 1024])
    kg_sb = sb("kg_sb", [128, 2, SEQ])
    vg_sb = sb("vg_sb", [128, 2, 64, HD])
    p_st = sb("p_st", [128, SEQ])
    bvec = sb("bvec", [128, NSLOT], F32)
    tmask = sb("tmask", [128, NSLOT, 1024], U8)
    ytmp = sb("ytmp", [128, 2, 128])
    qpos_sb = sb("qpos_sb", [128, NSLOT], F32)
    s_sb = sb("s_sb", [128, 1024], F32)
    negm = sb("negm", [128, 2], F32)
    onec = sb("onec", [128, 1], F32)
    l_stv = sb("l_stv", [128, 16], F32)
    l_tot = sb("l_tot", [128, 2], F32)
    rl = sb("rl", [128, 2], F32)
    pt_sb = sb("pt_sb", [128, 2, 1024])
    ident = sb("ident", [128, 128])
    iot = sb("iot", [128, 128], F32)
    wp_sb = sb("wp_sb", [128, FT, 256])
    b8c2 = ln8_bufs(sb)
    x1ts = sb("x1ts", [128, FT, 128])
    SP = [ps("sp0", [128, 1024]), ps("sp1", [128, 1024])]
    PTR = [es.enter_context(nc.psum_tensor("ptr0", [128, 1024], BF16)),
           es.enter_context(nc.psum_tensor("ptr1", [128, 1024], BF16))]
    PVT = [ps("pv0", [128, 512]), ps("pv1", [128, 512])]
    PV = [PVT[0][:, 0:128], PVT[1][:, 0:128]]
    PJ = [SP[0][:, 0:512], SP[1][:, 0:512]]
    PTR2_VIEW = [PTR[0][:, 0:512], PTR[1][:, 0:512]]

    sch = Sch()
    sch.dma("pe", x_tm[:], x2.rearrange("(s p) f -> p s f", p=128), sem="dinp")
    sch.dma("sq", yq[:], qT.rearrange("h p n -> p h n"), sem="dinq")
    sch.dma("sq", qpos_sb[:], qpos[:], sem="dinq")
    ld_p = ("dinp", sch.val("dinp"))
    ld_q = ("dinq", sch.val("dinq"))
    sch.op("pe", e_iota(iot[:], [[-1, 128]], 0, 1), inc="pe")
    sch.op("ve", e_ts(ident[:], iot[:], 0.0, None, ALU.is_equal),
           wait=[("pe", sch.val("pe"))], inc="ve")
    id_ready = sch.val("ve")
    sch.op("ve", e_memset(onec[:], 1.0), inc="ve")
    jvec = s_sb
    sch.op("pe", e_iota(jvec[:], [[1, 1024]], 0, 0), inc="pe")
    jv_pe = sch.val("pe")
    for s in range(NSLOT):
        sch.op("ve", e_ts(bvec[:, s:s + 1], qpos_sb[:, s:s + 1], 1.0,
                          float(-1024 * (CAPS[s] - 1)), ALU.mult, ALU.add),
               wait=[ld_q, ("pe", jv_pe)], inc="ve")
    sch.drain("ve")
    for s in range(NSLOT):
        sch.op("ve", e_ts(tmask[:, s, :], jvec[:], bvec[:, s:s + 1], None, ALU.is_gt),
               inc="ve")
    sch.drain("ve")
    jv_ready = sch.val("ve")

    cfg = ("l1", HEADS1, lambda h: [h], kf, vf, kg_sb, vg_sb,
           yq, p_st, tmask, jv_ready, s_sb, onec, l_stv, l_tot,
           rl, pt_sb, ident, id_ready, SP, PTR, PV, None, yq, ld_q, ld_p, ytmp)
    emit_attention(sch, cfg)
    attn_se = sch.val("se")

    pj_reader = {0: None, 1: None}
    wp_mm = None
    for d8 in range(8):
        wld = [("te", wp_mm)] if wp_mm is not None else []
        sch.dma("pe", wp_sb[:],
                wpT.rearrange("(t p) f -> p t f", p=128)[:, :, d8 * 256:(d8 + 1) * 256],
                sem="dwp", wait=wld)
        ld_wp = ("dwp", sch.val("dwp"))
        for s in range(NSLOT):
            bank = (d8 * 8 + s) % 2
            wmm = [ld_wp, ("se", attn_se)]
            if pj_reader[bank] is not None:
                wmm.append(pj_reader[bank])
            for ct in range(FT):
                sch.op("te", e_matmul(PJ[bank][:, 0:256], yq[:, ct, s * 128:(s + 1) * 128],
                                      wp_sb[:, ct, :],
                                      start=(ct == 0), stop=(ct == FT - 1)),
                       wait=wmm if ct == 0 else (), inc="te" if ct == FT - 1 else None)
            wp_mm = sch.val("te")
            sch.op("ve", e_tt(x_tm[:, s, d8 * 256:(d8 + 1) * 256], PJ[bank][:, 0:256],
                              x_tm[:, s, d8 * 256:(d8 + 1) * 256], ALU.add),
                   wait=[("te", wp_mm), ld_p], inc="ve")
            pj_reader[bank] = ("ve", sch.val("ve"))
    proj_ve = sch.val("ve")

    xbf = pt_sb.ap().rearrange("p a b -> p (a b)")
    stats_ve = emit_ln_batched(sch, b8c2, x_tm, xbf, proj_ve, sch.val("se"))
    ptr_evac2 = {0: None, 1: None}
    for s in range(NSLOT):
        xs = x_tm[:, s, :]
        wap = [("ve", stats_ve)] if s == 0 else [("se", sch.val("se"))]
        sch.op("ve", e_ts(xs, xs, b8c2["rstd8"][:, s:s + 1], b8c2["nmr8"][:, s:s + 1],
                          ALU.mult, ALU.add), wait=wap, inc="ve")
        ln_ve = sch.val("ve")
        wbf = [("ve", ln_ve), ("se", sch.val("se"))]
        if s > 0:
            wbf.append(("te", sch.val("te")))
        sch.op("se", e_act(xbf, xs, AF.Identity), wait=wbf, inc="se")
        bf_se = sch.val("se")
        ptr_evac2 = emit_transpose_out(
            sch, None, s, lambda ft: xbf[:, ft * 128:(ft + 1) * 128],
            x3T_o, PTR2_VIEW, ident, id_ready, x1ts, ptr_evac2,
            extra_wait=("se", bf_se))
    done = sch.val("ve")
    sch.dma("sq", x3_o.rearrange("(s p) f -> p s f", p=128), x_tm[:], sem="so",
            wait=[("ve", done)])
    st = sch.val("so")
    sch.op("sq", lambda e, s: e.nop(), wait=[("so", st)])
    sch.build(nc, es)
    with es:
        pass
    return nc



def build_D():
    nc = bass.Bass("TRN2", target_bir_lowering=False)
    x3 = nc.dram_tensor("x3", [1024, DM], F32, kind="ExternalInput").ap()
    x3T = nc.dram_tensor("x3T", [DM, 1024], BF16, kind="ExternalInput").ap()
    qw1 = nc.dram_tensor("qw1", [DM, FFN_H], BF16, kind="ExternalInput").ap()
    qw2 = nc.dram_tensor("qw2", [DM, FFN_H], BF16, kind="ExternalInput").ap()
    qw3 = nc.dram_tensor("qw3", [FFN_H, DM], BF16, kind="ExternalInput").ap()
    out_o = nc.dram_tensor("out_o", [1024, DM], F32, kind="ExternalOutput").ap()

    es = contextlib.ExitStack()
    sb = lambda name, shape, dt=BF16: es.enter_context(nc.sbuf_tensor(name, shape, dt))
    ps = lambda name, shape, dt=F32: es.enter_context(nc.psum_tensor(name, shape, dt))
    x_tm = sb("x_tm", [128, NSLOT, DM], F32)
    pool = sb("pool", [128, 4, FT, 512])
    hidT = sb("hidT", [128, 44, 512])
    xh = sb("xh", [128, FT, 512])
    sgst = sb("sgst", [128, DM])
    b8d = ln8_bufs(sb)
    P1 = [ps("p10", [128, 512]), ps("p11", [128, 512])]
    P2 = [ps("p20", [128, 512]), ps("p21", [128, 512])]
    PD = [ps("pd0", [128, 512]), ps("pd1", [128, 512])]

    qw3c = pool.ap().rearrange("p a b c -> p (a b c)")[:, 0:44 * 512]\
        .rearrange("p (m j) -> p m j", j=512)

    sch = Sch()
    sch.dma("pe", x_tm[:], x3.rearrange("(s p) f -> p s f", p=128), sem="dx")
    ld_x = ("dx", sch.val("dx"))
    p1_reader = {0: None, 1: None}
    p2_reader = {0: None, 1: None}
    pd_reader = {0: None, 1: None}
    gu_mm = {0: None, 1: None}
    down_mm = None
    xh_mm = None
    for th in range(2):
        wxh = [("te", xh_mm)] if xh_mm is not None else []
        sch.dma("sq", xh[:], x3T.rearrange("(t p) n -> p t n", p=128)[:, :, th * 512:(th + 1) * 512],
                sem="dxh", wait=wxh)
        ld_xh = ("dxh", sch.val("dxh"))
        for hc in range(11):
            par = hc % 2
            wld = []
            if gu_mm[par] is not None:
                wld.append(("te", gu_mm[par]))
            if hc == 0 and down_mm is not None:
                wld.append(("te", down_mm))
            sch.dma("pe", pool[:, 2 * par + 0, :, :],
                    qw1.rearrange("(t p) m -> p t m", p=128)[:, :, hc * 512:(hc + 1) * 512],
                    sem=f"dg{par}", wait=wld)
            sch.dma("pe", pool[:, 2 * par + 1, :, :],
                    qw2.rearrange("(t p) m -> p t m", p=128)[:, :, hc * 512:(hc + 1) * 512],
                    sem=f"dg{par}", wait=())
            ldg = (f"dg{par}", sch.val(f"dg{par}"))
            for mt in range(4):
                b = (hc * 4 + mt) % 2
                wmm = [ldg, ld_xh]
                if p1_reader[b] is not None:
                    wmm.append(p1_reader[b])
                for ft in range(FT):
                    sch.op("te", e_matmul(P1[b][:], pool[:, 2 * par + 0, ft, mt * 128:(mt + 1) * 128],
                                          xh[:, ft, :],
                                          start=(ft == 0), stop=(ft == FT - 1)),
                           wait=wmm if ft == 0 else (), inc="te" if ft == FT - 1 else None)
                g1 = sch.val("te")
                wmm2 = []
                if p2_reader[b] is not None:
                    wmm2.append(p2_reader[b])
                for ft in range(FT):
                    sch.op("te", e_matmul(P2[b][:], pool[:, 2 * par + 1, ft, mt * 128:(mt + 1) * 128],
                                          xh[:, ft, :],
                                          start=(ft == 0), stop=(ft == FT - 1)),
                           wait=wmm2 if ft == 0 else (), inc="te" if ft == FT - 1 else None)
                g2 = sch.val("te")
                gu_mm[par] = g2
                xh_mm = g2
                sch.op("se", e_act(sgst[:, 0:512], P1[b][:], AF.Sigmoid),
                       wait=[("te", g1), ("ve", sch.val("ve"))], inc="se")
                sv = sch.val("se")
                p1_reader[b] = ("se", sv)
                sch.op("ve", e_tt(hidT[:, hc * 4 + mt, :], sgst[:, 0:512], P2[b][:], ALU.mult),
                       wait=[("se", sv), ("te", g2)], inc="ve")
                p2_reader[b] = ("ve", sch.val("ve"))
        hid_ve = sch.val("ve")
        for dc in range(4):
            wld = [("ve", hid_ve)]
            if down_mm is not None:
                wld.append(("te", down_mm))
            if dc == 0:
                wld.append(("te", gu_mm[0] or 0))
                wld.append(("te", gu_mm[1] or 0))
            q3r = qw3.rearrange("(m p) f -> p m f", p=128)
            sch.dma("pe", qw3c[:, 0:22, :], q3r[:, 0:22, dc * 512:(dc + 1) * 512],
                    sem="dq3a", wait=wld)
            ld3a = ("dq3a", sch.val("dq3a"))
            sch.dma("pe", qw3c[:, 22:44, :], q3r[:, 22:44, dc * 512:(dc + 1) * 512],
                    sem="dq3b", wait=())
            ld3b = ("dq3b", sch.val("dq3b"))
            for t4 in range(4):
                s = th * 4 + t4
                b = (dc * 4 + t4) % 2
                wmm = [ld3a, ("ve", hid_ve)]
                if pd_reader[b] is not None:
                    wmm.append(pd_reader[b])
                for mt in range(44):
                    w_ = wmm if mt == 0 else ([ld3b] if mt == 22 and t4 == 0 else ())
                    sch.op("te", e_matmul(PD[b][:], hidT[:, mt, t4 * 128:(t4 + 1) * 128],
                                          qw3c[:, mt, :],
                                          start=(mt == 0), stop=(mt == 43)),
                           wait=w_, inc="te" if mt == 43 else None)
                down_mm = sch.val("te")
                sch.op("ve", e_tt(x_tm[:, s, dc * 512:(dc + 1) * 512], PD[b][:],
                                  x_tm[:, s, dc * 512:(dc + 1) * 512], ALU.add),
                       wait=[("te", down_mm), ld_x], inc="ve")
                pd_reader[b] = ("ve", sch.val("ve"))
    ffn_ve = sch.val("ve")
    stats_ve = emit_ln_batched(sch, b8d, x_tm, sgst.ap().rearrange("p n -> p n"),
                               ffn_ve, sch.val("se"))
    for s in range(NSLOT):
        sch.op("ve", e_ts(x_tm[:, s, :], x_tm[:, s, :], b8d["rstd8"][:, s:s + 1],
                          b8d["nmr8"][:, s:s + 1], ALU.mult, ALU.add),
               wait=[("ve", stats_ve)] if s == 0 else (), inc="ve")
    done = sch.val("ve")
    sch.dma("sq", out_o.rearrange("(s p) f -> p s f", p=128), x_tm[:], sem="so",
            wait=[("ve", done)])
    st = sch.val("so")
    sch.op("sq", lambda e, s: e.nop(), wait=[("so", st)])
    sch.build(nc, es)
    with es:
        pass
    return nc



def _bf(x):
    return np.ascontiguousarray(x, dtype=BF16NP)


def _f32(x):
    return np.ascontiguousarray(x, dtype=np.float32)


def _qpos(c):
    qp = np.zeros((128, NSLOT), dtype=np.float32)
    for s in range(NSLOT):
        qp[:, s] = gtile(c, s) * 128 + np.arange(128)
    return qp


def kernel(L0_attn_x, L0_attn_Wq, L0_attn_bq, L0_attn_Wk, L0_attn_bk,
           L0_attn_Wv, L0_attn_bv, L0_ffn_gate_w, L0_ffn_w_gate, L0_ffn_w_up,
           L0_ffn_w_down, L1_attn_W_qkv, L1_attn_W_proj, L1_ffn_w1, L1_ffn_w2,
           L1_ffn_w3):
    cores = list(range(NCORE))
    x = _f32(np.asarray(L0_attn_x)[0])
    xT = _f32(np.asarray(x).T)
    own_idx = [own_token_index(c) for c in range(NCORE)]

    w1T = _f32(np.asarray(L1_ffn_w1).T)
    w2T = _f32(np.asarray(L1_ffn_w2).T)
    w3T = _f32(np.asarray(L1_ffn_w3).T)
    w1s = w1T.reshape(NCORE, 128, 11264)
    w2s = w2T.reshape(NCORE, 128, 11264)
    w3s = w3T.reshape(NCORE, 128, 11264)
    wq = _f32(L0_attn_Wq)
    wk = _f32(L0_attn_Wk)
    wv = _f32(L0_attn_Wv)
    bqT = _f32(np.asarray(L0_attn_bq).T.reshape(128, HEADS0))
    bkT = _f32(np.asarray(L0_attn_bk).T.reshape(128, GRP0))
    bvT = _f32(np.asarray(L0_attn_bv).T.reshape(128, GRP0))
    nc = build_A1()
    res = run_bass_kernel_spmd(nc, [{
        "xT": _f32(xT[:, own_idx[c]]),
        "wq": wq, "bqT": bqT, "wk": wk, "bkT": bkT, "wv": wv, "bvT": bvT,
        "w1s": w1s[c], "w2s": w2s[c], "w3s": w3s[c],
    } for c in cores], core_ids=cores)
    a1 = [dict(r) for r in res.results]

    kf0 = np.zeros((GRP0, 128, SEQ), dtype=BF16NP)
    vf0 = np.zeros((GRP0, SEQ, HD), dtype=BF16NP)
    for c in cores:
        kf0[:, :, own_idx[c]] = a1[c]["kT_o"]
        vf0[:, own_idx[c], :] = a1[c]["v_o"]
    gwT = _f32(np.asarray(L0_ffn_gate_w).T)
    nc = build_A2()
    res = run_bass_kernel_spmd(nc, [{
        "x_own": _f32(x[own_idx[c]]),
        "qT": a1[c]["qT_o"], "kf": kf0, "vf": vf0,
        "qpos": _qpos(c), "gwT": gwT,
    } for c in cores], core_ids=cores)
    a2 = [dict(r) for r in res.results]

    x1T_full = np.zeros((DM, SEQ), dtype=BF16NP)
    cwf = np.zeros((SEQ, MOE_E), dtype=np.float32)
    for c in cores:
        x1T_full[:, own_idx[c]] = a2[c]["x1T_o"]
        cwf[own_idx[c]] = a2[c]["cw_o"]
    wsums_all = _f32(np.stack([a1[c]["wsums_o"][0] for c in cores]))
    wgt = np.asarray(L0_ffn_w_gate)
    wut = np.asarray(L0_ffn_w_up)
    wdt = np.asarray(L0_ffn_w_down)
    nc = build_B()
    res = run_bass_kernel_spmd(nc, [{
        "x1T": x1T_full,
        "cwe_f": _f32(cwf[:, [2 * c, 2 * c + 1]]),
        "wg": _f32(wgt[2 * c:2 * c + 2].transpose(0, 2, 1)),
        "wu": _f32(wut[2 * c:2 * c + 2].transpose(0, 2, 1)),
        "wdn": _f32(wdt[2 * c:2 * c + 2].transpose(0, 2, 1)),
        "wsums": wsums_all,
        "w1s": w1s[c], "w2s": w2s[c], "w3s": w3s[c],
    } for c in cores], core_ids=cores)
    b = [dict(r) for r in res.results]

    nc = build_C1a()
    res = run_bass_kernel_spmd(nc, [{
        "x1": a2[c]["x1_o"],
        "yps": np.ascontiguousarray(
            np.stack([b[p]["yp_o"][own_idx[c]] for p in cores])),
    } for c in cores], core_ids=cores)
    c1a = [dict(r) for r in res.results]

    x2T_full = np.zeros((DM, SEQ), dtype=BF16NP)
    for c in cores:
        x2T_full[:, own_idx[c]] = c1a[c]["x2T_o"]
    Wqkv = np.asarray(L1_attn_W_qkv)
    wqkv_sl = []
    for c in cores:
        per = np.zeros((2, 3, DM, HD), dtype=np.float32)
        for j in range(2):
            h = 2 * c + j
            for m in range(3):
                per[j, m] = np.asarray(Wqkv[m * DM + h * HD:m * DM + (h + 1) * HD, :].T)
        wqkv_sl.append(_f32(per))
    nc = build_C1b()
    res = run_bass_kernel_spmd(nc, [{
        "x2T": x2T_full, "wqkv": wqkv_sl[c],
    } for c in cores], core_ids=cores)
    c1b = [dict(r) for r in res.results]

    kf1 = np.concatenate([c1b[c]["kTh_o"] for c in cores], 0)
    vf1 = np.concatenate([c1b[c]["vh_o"] for c in cores], 0)
    qf1 = np.concatenate([c1b[c]["qTh_o"] for c in cores], 0)
    wpT = _f32(np.asarray(L1_attn_W_proj).T)
    nc = build_C2()
    res = run_bass_kernel_spmd(nc, [{
        "x2": c1a[c]["x2_o"],
        "qT": np.ascontiguousarray(qf1[:, :, own_idx[c]]),
        "kf": kf1, "vf": vf1, "qpos": _qpos(c), "wpT": wpT,
    } for c in cores], core_ids=cores)
    c2 = [dict(r) for r in res.results]

    qw1_full = np.concatenate([b[c]["qw1_o"].reshape(256, FFN_H) for c in cores], 0)
    qw2_full = np.concatenate([b[c]["qw2_o"].reshape(256, FFN_H) for c in cores], 0)
    qw3_full = np.concatenate([b[c]["qw3_o"].reshape(704, DM) for c in cores], 0)
    nc = build_D()
    res = run_bass_kernel_spmd(nc, [{
        "x3": c2[c]["x3_o"], "x3T": c2[c]["x3T_o"],
        "qw1": qw1_full, "qw2": qw2_full, "qw3": qw3_full,
    } for c in cores], core_ids=cores)
    d = [dict(r) for r in res.results]

    out = np.zeros((SEQ, DM), dtype=np.float32)
    for c in cores:
        out[own_idx[c]] = d[c]["out_o"]
    return out.reshape(1, SEQ, DM)
